# version 02
# baseline (speedup 1.0000x reference)
import numpy as np
import ml_dtypes
from collections import defaultdict
from contextlib import ExitStack

import concourse.bass as bass
import concourse.mybir as mybir
from concourse.bass_utils import run_bass_kernel_spmd

F32 = mybir.dt.float32
BF16 = mybir.dt.bfloat16
AF = mybir.ActivationFunctionType
ALU = mybir.AluOpType
AX = mybir.AxisListType

E = 2048
T = 8192
H = 16
D = 128
HD = 5632
Bn = 2
EC = E // 128
FBn = 16
HBn = 44
QS = 512
NQS = T // QS
SLOTS = 4
EXT = (16, 32, 48, 64)
PASSES = ((0, 3), (1, 2))
ISQD = float(1.0 / np.sqrt(np.float32(128.0)))
LN_EPS = 1e-5
QZ_EPS = 1e-8
QTILES = [[j, 7 - j, 8 + j, 15 - j] for j in range(4)]

ENGS = ("pe", "act", "dve", "pool", "sp")
PHASE_ORDER = ("qz", "kv", "q", "att", "w")


class Prog:
    def __init__(self):
        self.th = {k: [] for k in ENGS}
        self.cnt = defaultdict(int)
        self._waited = defaultdict(int)
        self.dma_regs = set()

    def w(self, eng, sem, val):
        if val is None or val <= 0:
            return
        assert val <= self.cnt[sem.name], \
            f"wait ahead of emission: {sem.name} {val} > {self.cnt[sem.name]}"
        key = (eng, sem.name)
        if self._waited[key] >= val:
            return
        self._waited[key] = val
        self.th[eng].append(lambda e, s=sem, v=val: e.wait_ge(s, v))

    def wt(self, eng, tup):
        if tup is not None:
            self.w(eng, tup[0], tup[1])

    def o(self, eng, fn, inc=None, n=1):
        if inc is not None:
            self.cnt[inc.name] += n
            self.th[eng].append(lambda e, f=fn, s=inc, m=n: f(e).then_inc(s, m))
            return self.cnt[inc.name]
        self.th[eng].append(lambda e, f=fn: f(e))
        return None

    def dma(self, eng, sem, out, in_):
        self.dma_regs.add((eng, sem))
        return self.o(eng, lambda e, o=out, i=in_: e.dma_start(out=o, in_=i), inc=sem, n=16)


class LRing:

    def __init__(self, P, nc, stk, name, n, shape, dt):
        self.P = P
        self.n = n
        self.tiles = [stk.enter_context(nc.sbuf_tensor(f"{name}{i}", shape, dt)) for i in range(n)]
        self.sems = [stk.enter_context(nc.semaphore(f"{name}S{i}")) for i in range(n)]
        self.mark = [None] * n

    def load(self, eng, i, in_, view=None):
        k = i % self.n
        if self.mark[k] is not None:
            self.P.wt(eng, self.mark[k])
        t = self.tiles[k]
        out = view(t) if view else t[:]
        c = self.P.dma(eng, self.sems[k], out, in_)
        return t, (self.sems[k], c)

    def rel(self, i, sem, cnt):
        if cnt:
            self.mark[i % self.n] = (sem, cnt)


class SRing:

    def __init__(self, P, nc, stk, name, n, shape, dt):
        self.P = P
        self.n = n
        self.tiles = [stk.enter_context(nc.sbuf_tensor(f"{name}{i}", shape, dt)) for i in range(n)]
        self.sems = [stk.enter_context(nc.semaphore(f"{name}S{i}")) for i in range(n)]
        self.mark = [None] * n

    def acq(self, eng, i):
        k = i % self.n
        if self.mark[k] is not None:
            self.P.wt(eng, self.mark[k])
        return self.tiles[k]

    def store(self, eng, i, out, view=None):
        k = i % self.n
        t = self.tiles[k]
        in_ = view(t) if view else t[:]
        c = self.P.dma(eng, self.sems[k], out, in_)
        self.mark[k] = (self.sems[k], c)
        return (self.sems[k], c)


def build_program(taps=(), upto="w"):
    nc = bass.Bass("TRN2", target_bir_lowering=False)
    taps = set(taps)
    nphase = PHASE_ORDER.index(upto)

    def have(ph):
        return PHASE_ORDER.index(ph) <= nphase

    def dram(name, shape, dt):
        kind = "ExternalOutput" if name in taps else "Internal"
        return nc.dram_tensor(name, shape, dt, kind=kind).ap()

    xt = nc.dram_tensor("xt", [E, T], F32, kind="ExternalInput").ap()
    xq = nc.dram_tensor("xq", [E, SLOTS * QS], F32, kind="ExternalInput").ap()
    wqT = nc.dram_tensor("wqT", [E, E], F32, kind="ExternalInput").ap()
    wkT = nc.dram_tensor("wkT", [E, E], F32, kind="ExternalInput").ap()
    wvT = nc.dram_tensor("wvT", [E, E], F32, kind="ExternalInput").ap()
    woT = nc.dram_tensor("woT", [E, E], F32, kind="ExternalInput").ap()
    w1T = nc.dram_tensor("w1T", [E, HD], F32, kind="ExternalInput").ap()
    w2T = nc.dram_tensor("w2T", [E, HD], F32, kind="ExternalInput").ap()
    w3T = nc.dram_tensor("w3T", [HD, E], F32, kind="ExternalInput").ap()
    cosKF = nc.dram_tensor("cosKF", [128, T], F32, kind="ExternalInput").ap()
    sinKF = nc.dram_tensor("sinKF", [128, T], F32, kind="ExternalInput").ap()
    cosQFd = nc.dram_tensor("cosQF", [128, SLOTS * QS], F32, kind="ExternalInput").ap()
    sinQFd = nc.dram_tensor("sinQF", [128, SLOTS * QS], F32, kind="ExternalInput").ap()
    maskm = nc.dram_tensor("maskm", [SLOTS, 128, 2432], BF16, kind="ExternalInput").ap()

    wB = {nm: dram(nm + "B", [FBn, E, 128], BF16) for nm in ("wq", "wk", "wo")}
    wvP = dram("wvP", [E, E], BF16)
    q1B = dram("q1B", [HBn, E, 128], BF16)
    q2B = dram("q2B", [HBn, E, 128], BF16)
    q3B = dram("q3B", [FBn, HD, 128], BF16)
    KbT = dram("KbT", [E, T], BF16)
    Vb = dram("Vb", [T, E], BF16)
    outTd = dram("outTd", [E, SLOTS * QS], BF16)
    if "qtap" in taps:
        qtap = nc.dram_tensor("qtap", [E, SLOTS * QS], BF16, kind="ExternalOutput").ap()
    if "thrtap" in taps:
        thrtap = nc.dram_tensor("thrtap", [3, 128, 2], F32, kind="ExternalOutput").ap()
    yt = nc.dram_tensor("yt", [E, SLOTS * QS], F32, kind="ExternalOutput").ap()

    P = Prog()

    with ExitStack() as top:
        En = top.enter_context

        s_pe = En(nc.semaphore("s_pe"))
        s_act = En(nc.semaphore("s_act"))
        s_dve = En(nc.semaphore("s_dve"))
        s_pool = En(nc.semaphore("s_pool"))
        s_bar = En(nc.semaphore("s_bar"))
        s_once = En(nc.semaphore("s_once"))
        s_out = En(nc.semaphore("s_out"))

        def once(eng, out, in_):
            sem = s_once if eng == "sp" else s_out
            return sem, P.dma(eng, sem, out, in_)

        def barrier():
            for (eng, sem) in sorted(P.dma_regs, key=lambda x: (x[0], x[1].name)):
                P.w(eng, sem, P.cnt[sem.name])
            for eng in ENGS:
                P.cnt[s_bar.name] += 1
                P.th[eng].append(lambda e: e.sem_inc(s_bar, 1))
            tgt = P.cnt[s_bar.name]
            for eng in ENGS:
                P.w(eng, s_bar, tgt)

        ones_bf = En(nc.sbuf_tensor("ones_bf", [128, 1], BF16))
        ones1f = En(nc.sbuf_tensor("ones1f", [1, 128], F32))
        onecol = En(nc.sbuf_tensor("onecol", [128, 1], F32))
        epsP = En(nc.sbuf_tensor("epsP", [1, 1], F32))
        statrow = En(nc.sbuf_tensor("statrow", [1, 6 * QS], F32))
        thr = [En(nc.sbuf_tensor(f"thr{i}", [128, 2], F32)) for i in range(3)]
        pb = [En(nc.psum_tensor(f"pb{i}", [128, 512], F32)) for i in range(8)]

        P.o("dve", lambda e: e.memset(ones_bf[:], 1.0))
        P.o("dve", lambda e: e.memset(ones1f[:], 1.0))
        P.o("dve", lambda e: e.memset(onecol[:], 1.0))
        c_init = P.o("dve", lambda e: e.memset(epsP[:], LN_EPS), inc=s_dve)

        def ln_fixup_and_bcast(W):
            mu = statrow[0:1, 0 * QS:0 * QS + W]
            m2 = statrow[0:1, 1 * QS:1 * QS + W]
            msq = statrow[0:1, 2 * QS:2 * QS + W]
            var = statrow[0:1, 3 * QS:3 * QS + W]
            rstd = statrow[0:1, 4 * QS:4 * QS + W]
            nmr = statrow[0:1, 5 * QS:5 * QS + W]
            P.o("dve", lambda e: e.tensor_scalar_mul(mu, pb[4][0:1, :W], 1.0 / E), inc=s_dve)
            e1 = P.o("dve", lambda e: e.tensor_scalar_mul(m2, pb[5][0:1, :W], 1.0 / E), inc=s_dve)
            P.w("dve", s_dve, e1)
            e2 = P.o("dve", lambda e: e.tensor_mul(msq, mu, mu), inc=s_dve)
            P.w("dve", s_dve, e2)
            dc = P.o("dve", lambda e: e.tensor_sub(var, m2, msq), inc=s_dve)
            P.w("act", s_dve, dc)
            P.w("act", s_dve, c_init)
            ac = P.o("act", lambda e: e.activation(var, var, AF.Sqrt, bias=epsP[:], scale=1.0),
                     inc=s_act)
            P.w("dve", s_act, ac)
            e3 = P.o("dve", lambda e: e.reciprocal(rstd, var), inc=s_dve)
            P.w("dve", s_dve, e3)
            dc = P.o("dve", lambda e: e.scalar_tensor_tensor(nmr, mu, -1.0, rstd, ALU.mult, ALU.mult),
                     inc=s_dve)
            P.w("pe", s_dve, dc)
            P.o("pe", lambda e: e.matmul(pb[6][:, :W], ones1f[0:1, :], rstd, start=True, stop=True))
            pc = P.o("pe", lambda e: e.matmul(pb[7][:, :W], ones1f[0:1, :], nmr, start=True, stop=True),
                     inc=s_pe)
            return pc

        cast_done = {}
        qz_stack = ExitStack()
        ldring = LRing(P, nc, qz_stack, "qzld", 2, [128, 2048], F32)
        stg = SRing(P, nc, qz_stack, "qzst", 2, [128, 2048], BF16)
        acc = [qz_stack.enter_context(nc.sbuf_tensor(f"qzacc{i}", [128, 48], F32)) for i in range(3)]
        accs = [qz_stack.enter_context(nc.sbuf_tensor(f"qzaccs{i}", [128, 1], F32)) for i in range(3)]
        crow = qz_stack.enter_context(nc.sbuf_tensor("qzcrow", [1, 1], F32))

        it = 0
        for nm, wsrc in (("wq", wqT), ("wk", wkT), ("wv", wvT), ("wo", woT)):
            for ecc in range(EC):
                buf, lw_ = ldring.load("sp", it, wsrc[ecc * 128:(ecc + 1) * 128, :])
                P.wt("dve", lw_)
                sb = stg.acq("dve", it)
                dc = P.o("dve", lambda e, o=sb, i=buf: e.tensor_copy(o[:], i[:]), inc=s_dve)
                ldring.rel(it, s_dve, dc)
                P.w("act", s_dve, dc)
                if nm == "wv":
                    stg.store("act", it, wvP[ecc * 128:(ecc + 1) * 128, :])
                else:
                    for fb in range(FBn):
                        stg.store("act", it, wB[nm][fb, ecc * 128:(ecc + 1) * 128, :],
                                  view=lambda t, f=fb: t[:, f * 128:(f + 1) * 128])
                it += 1
            cast_done[nm] = [(s, P.cnt[s.name]) for s in stg.sems]

        WCHUNKS = []
        for wi, (src_, dst_, rows, cols) in enumerate(((w1T, q1B, E, HD), (w2T, q2B, E, HD),
                                                       (w3T, q3B, HD, E))):
            for r0 in range(0, rows, 128):
                for c0 in range(0, cols, 2048):
                    WCHUNKS.append((wi, src_, dst_, r0, c0, min(2048, cols - c0)))
        for i in range(3):
            P.o("dve", lambda e, a=acc[i]: e.memset(a[:], 0.0))
        pieces_seen = [0, 0, 0]
        gamma_cnt = [None, None, None]
        for (wi, src_, dst_, r0, c0, wdt) in WCHUNKS:
            buf, lw_ = ldring.load("sp", it, src_[r0:r0 + 128, c0:c0 + wdt],
                                   view=lambda t, w=wdt: t[:, :w])
            P.wt("dve", lw_)
            kcol = pieces_seen[wi]
            pieces_seen[wi] += 1
            dc = P.o("dve", lambda e, b=buf, w=wdt, a=acc[wi], k=kcol: e.reduce_sum(
                a[:, k:k + 1], b[:, :w], axis=AX.X, apply_absolute_value=True), inc=s_dve)
            ldring.rel(it, s_dve, dc)
            gamma_cnt[wi] = dc
            it += 1
        thr_done = [None, None, None]
        for wi in range(3):
            P.w("dve", s_dve, gamma_cnt[wi])
            gs_ = P.o("dve", lambda e, a=acc[wi], s_=accs[wi], n_=pieces_seen[wi]: e.reduce_sum(
                s_[:], a[:, :n_], axis=AX.X), inc=s_dve)
            P.w("pe", s_dve, gs_)
            pc = P.o("pe", lambda e, a=accs[wi]: e.matmul(pb[0][0:1, 0:1], a[:], onecol[:],
                                                          start=True, stop=True), inc=s_pe)
            P.w("dve", s_pe, pc)
            numel = float(E * HD)
            dc = P.o("dve", lambda e, n_=numel: e.tensor_scalar(
                crow[0:1, 0:1], pb[0][0:1, 0:1], 0.5 / n_, 0.5 * QZ_EPS, ALU.mult, ALU.add),
                inc=s_dve)
            P.w("pe", s_dve, dc)
            pc = P.o("pe", lambda e: e.matmul(pb[1][:, 0:1], ones1f[0:1, :], crow[0:1, 0:1],
                                              start=True, stop=True), inc=s_pe)
            P.w("dve", s_pe, pc)
            ec0 = P.o("dve", lambda e, t=thr[wi]: e.tensor_copy(t[:, 0:1], pb[1][:, 0:1]), inc=s_dve)
            P.w("dve", s_dve, ec0)
            thr_done[wi] = P.o("dve", lambda e, t=thr[wi]: e.tensor_scalar_mul(t[:, 1:2], t[:, 0:1], -1.0),
                               inc=s_dve)
        qz_stack.close()
        barrier()
        if "thrtap" in taps:
            for wi in range(3):
                once("act", thrtap[wi], thr[wi][:])
            barrier()

        qz_state = {"i": 0}

        def emit_quant(nmax, quld, qsA, qsB, qst):
            for _ in range(nmax):
                i = qz_state["i"]
                if i >= len(WCHUNKS):
                    return
                (wi, src_, dst_, r0, c0, wdt) = WCHUNKS[i]
                buf, lw_ = quld.load("sp", i, src_[r0:r0 + 128, c0:c0 + wdt],
                                     view=lambda t, w=wdt: t[:, :w])
                P.wt("pool", lw_)
                P.w("pool", s_dve, thr_done[wi])
                P.o("pool", lambda e, b=buf, w=wdt, t=thr[wi]: e.tensor_scalar(
                    qsA[:, :w], b[:, :w], t[:, 0:1], 2.0, ALU.is_ge, ALU.mult), inc=s_pool)
                a2 = P.o("pool", lambda e, b=buf, w=wdt, t=thr[wi]: e.tensor_scalar(
                    qsB[:, :w], b[:, :w], t[:, 1:2], 2.0, ALU.is_le, ALU.mult), inc=s_pool)
                sb = qst.acq("pool", i)
                P.w("pool", s_pool, a2)
                pc = P.o("pool", lambda e, o=sb, w=wdt: e.tensor_sub(o[:, :w], qsA[:, :w], qsB[:, :w]),
                         inc=s_pool)
                quld.rel(i, s_pool, pc)
                P.w("pool", s_pool, pc)
                for bi in range(wdt // 128):
                    gb = (c0 // 128) + bi
                    qst.store("pool", i, dst_[gb, r0:r0 + 128, :],
                              view=lambda t, b_=bi: t[:, b_ * 128:(b_ + 1) * 128])
                qz_state["i"] = i + 1

        qst_rings = []
        if upto == "qz":
            qq_stack = ExitStack()
            quld = LRing(P, nc, qq_stack, "quld", 2, [128, 2048], F32)
            qsA = qq_stack.enter_context(nc.sbuf_tensor("qsA", [128, 2048], F32))
            qsB = qq_stack.enter_context(nc.sbuf_tensor("qsB", [128, 2048], F32))
            qst = SRing(P, nc, qq_stack, "qust", 2, [128, 2048], BF16)
            emit_quant(len(WCHUNKS), quld, qsA, qsB, qst)
            qq_stack.close()

        qt_stack = ExitStack()
        if have("q"):
            QT = [qt_stack.enter_context(nc.sbuf_tensor(f"QT{h}", [128, SLOTS * QS], BF16)) for h in range(H)]

        if have("kv"):
            kv_stack = ExitStack()
            xbf = [kv_stack.enter_context(nc.sbuf_tensor(f"xbf{i}", [128, QS], BF16)) for i in range(EC)]
            kv_ld = LRing(P, nc, kv_stack, "kvld", 2, [128, QS], F32)
            kv_sqt = [kv_stack.enter_context(nc.sbuf_tensor(f"kvsq{i}", [128, QS], BF16)) for i in range(2)]
            kv_scr = kv_stack.enter_context(nc.sbuf_tensor("kvscr", [128, QS], F32))
            wvsp = [kv_stack.enter_context(nc.sbuf_tensor(f"kvwv{i}", [128, E], BF16)) for i in range(EC)]
            wk_r = LRing(P, nc, kv_stack, "kvwk", 2, [128, E], BF16)
            kst = SRing(P, nc, kv_stack, "kvks", 2, [128, QS], BF16)
            vst = SRing(P, nc, kv_stack, "kvvs", 2, [128, E], BF16)
            ctab = LRing(P, nc, kv_stack, "kvct", 2, [128, QS], F32)
            stab = LRing(P, nc, kv_stack, "kvst", 2, [128, QS], F32)
            kv_t1 = kv_stack.enter_context(nc.sbuf_tensor("kvt1", [128, QS], F32))
            kv_t2 = kv_stack.enter_context(nc.sbuf_tensor("kvt2", [128, QS], F32))
            kv_sqm = [None, None]

            def ln_block(tiles, sq_tiles, sqm, load_src, ldr, scr, it0):
                it_ = it0
                stat_stop = None
                for ecc in range(EC):
                    buf, lw_ = ldr.load("sp", it_, load_src(ecc))
                    P.wt("dve", lw_)
                    dc = P.o("dve", lambda e, o=tiles[ecc], i=buf: e.tensor_copy(o[:], i[:]), inc=s_dve)
                    ldr.rel(it_, s_dve, dc)
                    k2 = it_ % 2
                    if sqm[k2] is not None:
                        P.wt("act", sqm[k2])
                    sqb = sq_tiles[k2]
                    P.w("act", s_dve, dc)
                    ac = P.o("act", lambda e, o=sqb, i=tiles[ecc]: e.activation(o[:], i[:], AF.Square),
                             inc=s_act)
                    P.w("pe", s_act, ac)
                    P.o("pe", lambda e, i=tiles[ecc], st=(ecc == 0), sp_=(ecc == EC - 1): e.matmul(
                        pb[4][0:1, :], ones_bf[:], i[:], start=st, stop=sp_))
                    pc = P.o("pe", lambda e, i=sqb, st=(ecc == 0), sp_=(ecc == EC - 1): e.matmul(
                        pb[5][0:1, :], ones_bf[:], i[:], start=st, stop=sp_), inc=s_pe)
                    sqm[k2] = (s_pe, pc)
                    stat_stop = pc
                    it_ += 1
                P.w("dve", s_pe, stat_stop)
                bc = ln_fixup_and_bcast(QS)
                P.w("dve", s_pe, bc)
                ndone = None
                for ecc in range(EC):
                    em = P.o("dve", lambda e, x=tiles[ecc]: e.tensor_mul(scr[:], x[:], pb[6][:, :]),
                             inc=s_dve)
                    P.w("dve", s_dve, em)
                    ndone = P.o("dve", lambda e, x=tiles[ecc]: e.tensor_add(x[:], scr[:], pb[7][:, :]),
                                inc=s_dve)
                return ndone, it_

            for tup in cast_done["wv"]:
                P.wt("sp", tup)
            wv_lc = None
            for ecc in range(EC):
                wv_lc = once("sp", wvsp[ecc][:], wvP[ecc * 128:(ecc + 1) * 128, :])

            it = 0
            for qs_i in range(NQS):
                gt0 = qs_i * QS
                ndone, it = ln_block(xbf, kv_sqt, kv_sqm,
                                     lambda ecc, g=gt0: xt[ecc * 128:(ecc + 1) * 128, g:g + QS],
                                     kv_ld, kv_scr, it)
                P.w("pe", s_dve, ndone)
                P.wt("pe", wv_lc)
                for tt in range(4):
                    vmm = None
                    for ecc in range(EC):
                        for f4 in range(4):
                            vmm = P.o("pe", lambda e, b=pb[2 + f4], w=wvsp[ecc], c=ecc, t_=tt, f_=f4,
                                      st=(ecc == 0), sp_=(ecc == EC - 1): e.matmul(
                                b[:, :], xbf[c][:, t_ * 128:(t_ + 1) * 128], w[:, f_ * 512:(f_ + 1) * 512],
                                start=st, stop=sp_),
                                inc=(s_pe if (ecc == EC - 1 and f4 == 3) else None))
                    vs = vst.acq("dve", qs_i * 4 + tt)
                    P.w("dve", s_pe, vmm)
                    vdone = None
                    for f4 in range(4):
                        vdone = P.o("dve", lambda e, o=vs, b=pb[2 + f4], f_=f4: e.tensor_copy(
                            o[:, f_ * 512:(f_ + 1) * 512], b[:, :]), inc=s_dve)
                    P.w("act", s_dve, vdone)
                    vst.store("act", qs_i * 4 + tt, Vb[gt0 + tt * 128:gt0 + (tt + 1) * 128, :])
                    P.w("pe", s_dve, vdone)
                cb, cl = ctab.load("sp", qs_i, cosKF[:, gt0:gt0 + QS])
                sb_, sl = stab.load("sp", qs_i, sinKF[:, gt0:gt0 + QS])
                for tup in cast_done["wk"]:
                    P.wt("sp", tup)
                kdone = None
                for fb in range(FBn):
                    wkb, lk = wk_r.load("sp", qs_i * FBn + fb,
                                        wB["wk"][fb].rearrange("(c p) f -> p c f", p=128),
                                        view=lambda t: t[:].rearrange("p (c f) -> p c f", f=128))
                    bk = pb[0 + (fb % 2)]
                    P.wt("pe", lk)
                    kmm = None
                    for ecc in range(EC):
                        kmm = P.o("pe", lambda e, b=bk, w=wkb, c=ecc, st=(ecc == 0), sp_=(ecc == EC - 1): e.matmul(
                            b[:, :], w[:, c * 128:(c + 1) * 128], xbf[c][:, :], start=st, stop=sp_),
                            inc=(s_pe if ecc == EC - 1 else None))
                    wk_r.rel(qs_i * FBn + fb, s_pe, kmm)
                    ks = kst.acq("dve", qs_i * FBn + fb)
                    P.w("dve", s_pe, kmm)
                    P.wt("dve", cl)
                    P.wt("dve", sl)
                    P.o("dve", lambda e, b=bk, c=cb: e.tensor_mul(kv_t1[:], b[:, :], c[:]), inc=s_dve)
                    P.o("dve", lambda e, b=bk, s=sb_: e.tensor_mul(kv_t2[0:64, :], b[64:128, :], s[64:128, :]),
                        inc=s_dve)
                    em = P.o("dve", lambda e, b=bk, s=sb_: e.tensor_mul(kv_t2[64:128, :], b[0:64, :], s[0:64, :]),
                             inc=s_dve)
                    P.w("dve", s_dve, em)
                    kdone = P.o("dve", lambda e, o=ks: e.tensor_add(o[:], kv_t1[:], kv_t2[:]), inc=s_dve)
                    P.w("act", s_dve, kdone)
                    kst.store("act", qs_i * FBn + fb, KbT[fb * 128:(fb + 1) * 128, gt0:gt0 + QS])
                    P.w("pe", s_dve, kdone)
                ctab.rel(qs_i, s_dve, kdone)
                stab.rel(qs_i, s_dve, kdone)
            kv_stack.close()
            barrier()

        if have("q"):
            q_stack = ExitStack()
            xqb = [q_stack.enter_context(nc.sbuf_tensor(f"xqb{i}", [128, QS], BF16)) for i in range(EC)]
            q_ld = LRing(P, nc, q_stack, "qld", 2, [128, QS], F32)
            q_sqt = [q_stack.enter_context(nc.sbuf_tensor(f"qsq{i}", [128, QS], BF16)) for i in range(2)]
            q_scr = q_stack.enter_context(nc.sbuf_tensor("qscr", [128, QS], F32))
            wq_r = LRing(P, nc, q_stack, "qwq", 2, [128, E], BF16)
            qtabC = q_stack.enter_context(nc.sbuf_tensor("qtabC", [128, SLOTS * QS], F32))
            qtabS = q_stack.enter_context(nc.sbuf_tensor("qtabS", [128, SLOTS * QS], F32))
            q_t1 = q_stack.enter_context(nc.sbuf_tensor("qt1", [128, QS], F32))
            q_t2 = q_stack.enter_context(nc.sbuf_tensor("qt2", [128, QS], F32))
            q_sqm = [None, None]

            def ln_block_q(tiles, sq_tiles, sqm, load_src, ldr, scr, it0):
                it_ = it0
                stat_stop = None
                for ecc in range(EC):
                    buf, lw_ = ldr.load("sp", it_, load_src(ecc))
                    P.wt("dve", lw_)
                    dc = P.o("dve", lambda e, o=tiles[ecc], i=buf: e.tensor_copy(o[:], i[:]), inc=s_dve)
                    ldr.rel(it_, s_dve, dc)
                    k2 = it_ % 2
                    if sqm[k2] is not None:
                        P.wt("act", sqm[k2])
                    sqb = sq_tiles[k2]
                    P.w("act", s_dve, dc)
                    ac = P.o("act", lambda e, o=sqb, i=tiles[ecc]: e.activation(o[:], i[:], AF.Square),
                             inc=s_act)
                    P.w("pe", s_act, ac)
                    P.o("pe", lambda e, i=tiles[ecc], st=(ecc == 0), sp_=(ecc == EC - 1): e.matmul(
                        pb[4][0:1, :], ones_bf[:], i[:], start=st, stop=sp_))
                    pc = P.o("pe", lambda e, i=sqb, st=(ecc == 0), sp_=(ecc == EC - 1): e.matmul(
                        pb[5][0:1, :], ones_bf[:], i[:], start=st, stop=sp_), inc=s_pe)
                    sqm[k2] = (s_pe, pc)
                    stat_stop = pc
                    it_ += 1
                P.w("dve", s_pe, stat_stop)
                bc = ln_fixup_and_bcast(QS)
                P.w("dve", s_pe, bc)
                ndone = None
                for ecc in range(EC):
                    em = P.o("dve", lambda e, x=tiles[ecc]: e.tensor_mul(scr[:], x[:], pb[6][:, :]),
                             inc=s_dve)
                    P.w("dve", s_dve, em)
                    ndone = P.o("dve", lambda e, x=tiles[ecc]: e.tensor_add(x[:], scr[:], pb[7][:, :]),
                                inc=s_dve)
                return ndone, it_

            tc0 = once("sp", qtabC[:], cosQFd)
            tc1 = once("sp", qtabS[:], sinQFd)

            it = 0
            for s in range(SLOTS):
                q0 = s * QS
                ndone, it = ln_block_q(xqb, q_sqt, q_sqm,
                                       lambda ecc, q=q0: xq[ecc * 128:(ecc + 1) * 128, q:q + QS],
                                       q_ld, q_scr, it)
                P.w("pe", s_dve, ndone)
                for tup in cast_done["wq"]:
                    P.wt("sp", tup)
                P.wt("dve", tc0)
                P.wt("dve", tc1)
                for fb in range(FBn):
                    wqb_t, lq = wq_r.load("sp", s * FBn + fb,
                                          wB["wq"][fb].rearrange("(c p) f -> p c f", p=128),
                                          view=lambda t: t[:].rearrange("p (c f) -> p c f", f=128))
                    bq = pb[0 + (fb % 2)]
                    P.wt("pe", lq)
                    qmm = None
                    for ecc in range(EC):
                        qmm = P.o("pe", lambda e, b=bq, w=wqb_t, c=ecc, st=(ecc == 0), sp_=(ecc == EC - 1): e.matmul(
                            b[:, :], w[:, c * 128:(c + 1) * 128], xqb[c][:, :], start=st, stop=sp_),
                            inc=(s_pe if ecc == EC - 1 else None))
                    wq_r.rel(s * FBn + fb, s_pe, qmm)
                    P.w("dve", s_pe, qmm)
                    P.o("dve", lambda e, b=bq, q_=q0: e.tensor_mul(q_t1[:], b[:, :], qtabC[:, q_:q_ + QS]),
                        inc=s_dve)
                    P.o("dve", lambda e, b=bq, q_=q0: e.tensor_mul(q_t2[0:64, :], b[64:128, :], qtabS[64:128, q_:q_ + QS]),
                        inc=s_dve)
                    em = P.o("dve", lambda e, b=bq, q_=q0: e.tensor_mul(q_t2[64:128, :], b[0:64, :], qtabS[0:64, q_:q_ + QS]),
                             inc=s_dve)
                    P.w("dve", s_dve, em)
                    qdone = P.o("dve", lambda e, h=fb, q_=q0: e.tensor_add(QT[h][:, q_:q_ + QS], q_t1[:], q_t2[:]),
                                inc=s_dve)
                    P.w("pe", s_dve, qdone)
            q_stack.close()
            barrier()
            if "qtap" in taps:
                for h in range(H):
                    once("act", qtap[h * 128:(h + 1) * 128, :], QT[h][:])
                barrier()

        if have("att"):
            att_stack = ExitStack()
            kt_r = LRing(P, nc, att_stack, "akt", 2, [128, 1024], BF16)
            vt_r = LRing(P, nc, att_stack, "avt", 2, [128, 8 * 128], BF16)
            ptile = [att_stack.enter_context(nc.sbuf_tensor(f"ap{i}", [128, QS], BF16)) for i in range(6)]
            pmark = [None] * 6
            masks = [att_stack.enter_context(nc.sbuf_tensor(f"amask{i}", [128, 2432], BF16)) for i in range(SLOTS)]
            rrow = att_stack.enter_context(nc.sbuf_tensor("arr", [1, 2 * QS], F32))
            rbt = att_stack.enter_context(nc.sbuf_tensor("arb", [128, QS], F32))
            ost = SRing(P, nc, att_stack, "aost", 2, [128, QS], BF16)
            quld = LRing(P, nc, att_stack, "quld", 2, [128, 2048], F32)
            qsA = att_stack.enter_context(nc.sbuf_tensor("qsA", [128, 2048], F32))
            qsB = att_stack.enter_context(nc.sbuf_tensor("qsB", [128, 2048], F32))
            qst = SRing(P, nc, att_stack, "qust", 2, [128, 2048], BF16)
            qst_rings = [qst]

            mload = None
            for s in range(SLOTS):
                mload = once("sp", masks[s][:], maskm[s])

            sbw = [None, None]
            pvfree = [None, None]
            rfree = [None, None]
            rrmark = [None, None]
            pb6mark = [None]
            oi = [0]
            ci = 0

            def finish(h, slot, ridx, pvstop):
                rbank = pb[2 + ridx]
                vbank = pb[4 + ridx]
                P.w("dve", s_pe, pvstop)
                if rrmark[ridx] is not None:
                    P.w("dve", s_pe, rrmark[ridx])
                rr = rrow[0:1, ridx * QS:(ridx + 1) * QS]
                dc = P.o("dve", lambda e, b=rbank, r=rr: e.reciprocal(r, b[0:1, :]), inc=s_dve)
                rfree[ridx] = dc
                P.w("pe", s_dve, dc)
                if pb6mark[0] is not None:
                    P.w("pe", s_dve, pb6mark[0])
                bc = P.o("pe", lambda e, r=rr: e.matmul(pb[6][:, :], ones1f[0:1, :], r,
                                                        start=True, stop=True), inc=s_pe)
                rrmark[ridx] = bc
                P.w("dve", s_pe, bc)
                em = P.o("dve", lambda e: e.tensor_copy(rbt[:], pb[6][:, :]), inc=s_dve)
                P.w("dve", s_dve, em)
                osb = ost.acq("dve", oi[0])
                dc2 = P.o("dve", lambda e, o=osb, b=vbank: e.tensor_mul(o[:], b[:, :], rbt[:]), inc=s_dve)
                pb6mark[0] = dc2
                pvfree[ridx] = dc2
                P.w("act", s_dve, dc2)
                ost.store("act", oi[0], outTd[h * 128:(h + 1) * 128, slot * QS:(slot + 1) * QS])
                oi[0] += 1

            P.wt("dve", mload)
            for h in range(H):
                for (sA, sB) in PASSES:
                    emax = max(EXT[sA], EXT[sB])
                    nkblk = (emax + 7) // 8
                    pend = [None]
                    last_pv = {}
                    for kb in range(nkblk):
                        kt, lk = kt_r.load("sp", kb, KbT[h * 128:(h + 1) * 128, kb * 1024:(kb + 1) * 1024])
                        vt, lv = vt_r.load("sp", kb,
                                           Vb[kb * 1024:(kb + 1) * 1024, h * 128:(h + 1) * 128]
                                           .rearrange("(kc p) d -> p kc d", p=128),
                                           view=lambda t: t[:].rearrange("p (kc d) -> p kc d", d=128))
                        last_smm = None
                        for i8 in range(8):
                            kc = kb * 8 + i8
                            for slot, ridx in ((sA, 0), (sB, 1)):
                                if kc >= EXT[slot]:
                                    continue
                                sbank_i = ci % 2
                                sbank = pb[sbank_i]
                                pslot_i = ci % 6
                                pslot = ptile[pslot_i]
                                if pmark[pslot_i] is not None:
                                    P.wt("act", pmark[pslot_i])
                                if sbw[sbank_i] is not None:
                                    P.w("pe", s_act, sbw[sbank_i])
                                P.wt("pe", lk)
                                smm = P.o("pe", lambda e, b=sbank, k=kt, i=i8, s_=slot, h_=h: e.matmul(
                                    b[:, :], k[:, i * 128:(i + 1) * 128], QT[h_][:, s_ * QS:(s_ + 1) * QS],
                                    start=True, stop=True), inc=s_pe)
                                last_smm = smm
                                if pend[0] is not None:
                                    pend[0]()
                                    pend[0] = None
                                P.w("act", s_pe, smm)
                                ec_ = P.o("act", lambda e, o=pslot, b=sbank: e.activation(
                                    o[:], b[:, :], AF.Exp, bias=0.0, scale=ISQD), inc=s_act)
                                sbw[sbank_i] = ec_
                                gate = (s_act, ec_)
                                if 16 * slot <= kc < 16 * slot + 16:
                                    j0 = 128 * (16 * slot + 15 - kc)
                                    P.w("dve", s_act, ec_)
                                    mc = P.o("dve", lambda e, p_=pslot, m=masks[slot], j=j0: e.tensor_mul(
                                        p_[:], p_[:], m[:, j:j + QS]), inc=s_dve)
                                    gate = (s_dve, mc)

                                def mk_pend(slot=slot, ridx=ridx, kc=kc, pslot=pslot, pslot_i=pslot_i,
                                            gate=gate, vt=vt, i8=i8, lv=lv, h=h, kb=kb):
                                    def emit():
                                        P.w("pe", gate[0], gate[1])
                                        P.wt("pe", lv)
                                        rbank = pb[2 + ridx]
                                        vbank = pb[4 + ridx]
                                        if kc == 0:
                                            if rfree[ridx] is not None:
                                                P.w("pe", s_dve, rfree[ridx])
                                            if pvfree[ridx] is not None:
                                                P.w("pe", s_dve, pvfree[ridx])
                                        P.o("pe", lambda e, b=rbank, p_=pslot, st=(kc == 0),
                                            sp_=(kc == EXT[slot] - 1): e.matmul(
                                            b[0:1, :], ones_bf[:], p_[:], start=st, stop=sp_))
                                        pc = P.o("pe", lambda e, b=vbank, v=vt, i=i8, p_=pslot,
                                                 st=(kc == 0), sp_=(kc == EXT[slot] - 1): e.matmul(
                                            b[:, :], v[:, i * 128:(i + 1) * 128], p_[:], start=st, stop=sp_),
                                            inc=s_pe)
                                        pmark[pslot_i] = (s_pe, pc)
                                        last_pv[kb] = pc
                                        if kc == EXT[slot] - 1:
                                            finish(h, slot, ridx, pc)
                                    return emit

                                pend[0] = mk_pend()
                                ci += 1
                        kt_r.rel(kb, s_pe, last_smm)
                        if kb - 1 in last_pv:
                            vt_r.rel(kb - 1, s_pe, last_pv[kb - 1])
                    if pend[0] is not None:
                        pend[0]()
                        pend[0] = None
                    for kb in range(nkblk):
                        if kb in last_pv:
                            vt_r.rel(kb, s_pe, last_pv[kb])
                    emit_quant(3, quld, qsA, qsB, qst)
            emit_quant(len(WCHUNKS), quld, qsA, qsB, qst)
            qst_done = [(s, P.cnt[s.name]) for s in qst.sems]
            att_stack.close()
            qt_stack.close()
            barrier()
        else:
            qt_stack.close()
            qst_done = []

        if have("w"):
            w_stack = ExitStack()
            s_ot = w_stack.enter_context(nc.semaphore("s_ot"))
            otile = [w_stack.enter_context(nc.sbuf_tensor(f"wot{i}", [128, QS], BF16)) for i in range(EC)]
            x1T = [w_stack.enter_context(nc.sbuf_tensor(f"wx1{i}", [128, QS], F32)) for i in range(FBn)]
            ln2 = [w_stack.enter_context(nc.sbuf_tensor(f"wln{i}", [128, QS], BF16)) for i in range(FBn)]
            hid = [w_stack.enter_context(nc.sbuf_tensor(f"whid{i}", [128, QS], BF16)) for i in range(HBn)]
            wcol = LRing(P, nc, w_stack, "wwc", 2, [128, E], BF16)
            q12c = LRing(P, nc, w_stack, "wq12", 4, [128, E], BF16)
            q3c = LRing(P, nc, w_stack, "wq3", 2, [128, HBn * 128], BF16)
            xres = LRing(P, nc, w_stack, "wxr", 2, [128, QS], F32)
            sigt = w_stack.enter_context(nc.sbuf_tensor("wsig", [128, QS], BF16))
            w_sqt = [w_stack.enter_context(nc.sbuf_tensor(f"wsq{i}", [128, QS], BF16)) for i in range(2)]
            w_sqm = [None, None]
            yst = SRing(P, nc, w_stack, "wyst", 2, [128, QS], F32)
            w_scr = w_stack.enter_context(nc.sbuf_tensor("wscr", [128, QS], F32))

            for tup in qst_done:
                P.wt("sp", tup)
            wo_last = [None]
            for s in range(SLOTS):
                q0 = s * QS
                if wo_last[0] is not None:
                    P.w("sp", s_pe, wo_last[0])
                for ecc in range(EC):
                    P.dma("sp", s_ot, otile[ecc][:], outTd[ecc * 128:(ecc + 1) * 128, q0:q0 + QS])
                olc = (s_ot, P.cnt[s_ot.name])
                for tup in cast_done["wo"]:
                    P.wt("sp", tup)
                for fb in range(FBn):
                    wc, lw = wcol.load("sp", s * FBn + fb,
                                       wB["wo"][fb].rearrange("(c p) f -> p c f", p=128),
                                       view=lambda t: t[:].rearrange("p (c f) -> p c f", f=128))
                    xr, lx = xres.load("sp", s * FBn + fb, xq[fb * 128:(fb + 1) * 128, q0:q0 + QS])
                    bw = pb[fb % 2]
                    P.wt("pe", lw)
                    P.wt("pe", olc)
                    wmm = None
                    for ecc in range(EC):
                        wmm = P.o("pe", lambda e, b=bw, w=wc, c=ecc, st=(ecc == 0), sp_=(ecc == EC - 1): e.matmul(
                            b[:, :], w[:, c * 128:(c + 1) * 128], otile[c][:], start=st, stop=sp_),
                            inc=(s_pe if ecc == EC - 1 else None))
                    wo_last[0] = wmm
                    wcol.rel(s * FBn + fb, s_pe, wmm)
                    P.w("dve", s_pe, wmm)
                    P.wt("dve", lx)
                    dc = P.o("dve", lambda e, o=x1T[fb], b=bw, x=xr: e.tensor_add(o[:], b[:, :], x[:]),
                             inc=s_dve)
                    xres.rel(s * FBn + fb, s_dve, dc)
                    P.w("pe", s_dve, dc)
                stat_stop = None
                P.w("act", s_dve, P.cnt[s_dve.name])
                for fb in range(FBn):
                    k2 = fb % 2
                    if w_sqm[k2] is not None:
                        P.wt("act", w_sqm[k2])
                    sqb = w_sqt[k2]
                    ac = P.o("act", lambda e, o=sqb, i=x1T[fb]: e.activation(o[:], i[:], AF.Square),
                             inc=s_act)
                    P.w("pe", s_act, ac)
                    P.o("pe", lambda e, i=x1T[fb], st=(fb == 0), sp_=(fb == FBn - 1): e.matmul(
                        pb[4][0:1, :], onecol[:], i[:], start=st, stop=sp_))
                    pc = P.o("pe", lambda e, i=sqb, st=(fb == 0), sp_=(fb == FBn - 1): e.matmul(
                        pb[5][0:1, :], ones_bf[:], i[:], start=st, stop=sp_), inc=s_pe)
                    w_sqm[k2] = (s_pe, pc)
                    stat_stop = pc
                P.w("dve", s_pe, stat_stop)
                bc = ln_fixup_and_bcast(QS)
                P.w("dve", s_pe, bc)
                ndone = None
                for fb in range(FBn):
                    em = P.o("dve", lambda e, x=x1T[fb]: e.tensor_mul(w_scr[:], x[:], pb[6][:, :]),
                             inc=s_dve)
                    P.w("dve", s_dve, em)
                    ndone = P.o("dve", lambda e, o=ln2[fb]: e.tensor_add(o[:], w_scr[:], pb[7][:, :]),
                                inc=s_dve)
                P.w("pe", s_dve, ndone)
                for hb in range(HBn):
                    c1, l1 = q12c.load("sp", 2 * (s * HBn + hb),
                                       q1B[hb].rearrange("(c p) f -> p c f", p=128),
                                       view=lambda t: t[:].rearrange("p (c f) -> p c f", f=128))
                    c2, l2 = q12c.load("sp", 2 * (s * HBn + hb) + 1,
                                       q2B[hb].rearrange("(c p) f -> p c f", p=128),
                                       view=lambda t: t[:].rearrange("p (c f) -> p c f", f=128))
                    b1 = pb[hb % 2]
                    b2 = pb[2 + hb % 2]
                    P.wt("pe", l1)
                    g1 = None
                    for ecc in range(EC):
                        g1 = P.o("pe", lambda e, b=b1, w=c1, c=ecc, st=(ecc == 0), sp_=(ecc == EC - 1): e.matmul(
                            b[:, :], w[:, c * 128:(c + 1) * 128], ln2[c][:], start=st, stop=sp_),
                            inc=(s_pe if ecc == EC - 1 else None))
                    P.wt("pe", l2)
                    g2 = None
                    for ecc in range(EC):
                        g2 = P.o("pe", lambda e, b=b2, w=c2, c=ecc, st=(ecc == 0), sp_=(ecc == EC - 1): e.matmul(
                            b[:, :], w[:, c * 128:(c + 1) * 128], ln2[c][:], start=st, stop=sp_),
                            inc=(s_pe if ecc == EC - 1 else None))
                    q12c.rel(2 * (s * HBn + hb), s_pe, g1)
                    q12c.rel(2 * (s * HBn + hb) + 1, s_pe, g2)
                    P.w("act", s_pe, g1)
                    ac = P.o("act", lambda e, b=b1: e.activation(sigt[:], b[:, :], AF.Sigmoid,
                                                                 bias=0.0, scale=0.5), inc=s_act)
                    P.w("dve", s_pe, g2)
                    P.w("dve", s_act, ac)
                    dc = P.o("dve", lambda e, o=hid[hb], b=b2: e.scalar_tensor_tensor(
                        o[:], b[:, :], 0.5, sigt[:], ALU.mult, ALU.mult), inc=s_dve)
                    P.w("pe", s_dve, dc)
                    P.w("act", s_dve, dc)
                for fb in range(FBn):
                    c3, l3 = q3c.load("sp", s * FBn + fb,
                                      q3B[fb].rearrange("(c p) f -> p c f", p=128),
                                      view=lambda t: t[:].rearrange("p (c f) -> p c f", f=128))
                    b3 = pb[4 + fb % 2]
                    P.wt("pe", l3)
                    g3 = None
                    for hb in range(HBn):
                        g3 = P.o("pe", lambda e, b=b3, w=c3, c=hb, st=(hb == 0), sp_=(hb == HBn - 1): e.matmul(
                            b[:, :], w[:, c * 128:(c + 1) * 128], hid[c][:], start=st, stop=sp_),
                            inc=(s_pe if hb == HBn - 1 else None))
                    q3c.rel(s * FBn + fb, s_pe, g3)
                    ys = yst.acq("dve", s * FBn + fb)
                    P.w("dve", s_pe, g3)
                    dc = P.o("dve", lambda e, o=ys, b=b3, x=x1T[fb]: e.scalar_tensor_tensor(
                        o[:], b[:, :], 0.5, x[:], ALU.mult, ALU.add), inc=s_dve)
                    P.w("pe", s_dve, dc)
                    P.w("act", s_dve, dc)
                    yst.store("act", s * FBn + fb, yt[fb * 128:(fb + 1) * 128, q0:q0 + QS])
            w_stack.close()

        for (eng, sem) in sorted(P.dma_regs, key=lambda x: (x[0], x[1].name)):
            P.w("sp", sem, P.cnt[sem.name])

        with nc.Block() as block:
            @block.tensor
            def _(t):
                for f in P.th["pe"]:
                    f(t)

            @block.scalar
            def _(a):
                for f in P.th["act"]:
                    f(a)

            @block.vector
            def _(v):
                for f in P.th["dve"]:
                    f(v)

            @block.gpsimd
            def _(g):
                for f in P.th["pool"]:
                    f(g)

            @block.sync
            def _(sy):
                for f in P.th["sp"]:
                    f(sy)

    return nc



def _rope_tables():
    inv = (1.0 / (np.float32(1000000.0) ** (np.arange(0, D, 2, dtype=np.float32) / np.float32(D)))).astype(np.float32)
    t = np.arange(T, dtype=np.float32)
    f = t[:, None] * inv[None, :]
    return np.cos(f).astype(np.float32), np.sin(f).astype(np.float32)


_NC_CACHE = {}


def make_in_maps(x, wq, wk, wv, wo, w1, w2, w3):
    cosf, sinf = _rope_tables()
    cosKF = np.ascontiguousarray(np.concatenate([cosf.T, cosf.T], axis=0))
    sinKF = np.ascontiguousarray(np.concatenate([sinf.T, -sinf.T], axis=0))
    xT = [np.ascontiguousarray(x[b].T) for b in range(Bn)]
    wqT = np.ascontiguousarray(wq.T); wkT = np.ascontiguousarray(wk.T)
    wvT = np.ascontiguousarray(wv.T); woT = np.ascontiguousarray(wo.T)
    w1T = np.ascontiguousarray(w1.T); w2T = np.ascontiguousarray(w2.T)
    w3T = np.ascontiguousarray(w3.T)
    jcols = np.arange(2432)
    kps = np.arange(128)
    in_maps = []
    for c in range(8):
        b, j = c // 4, c % 4
        gs = QTILES[j]
        cols = np.concatenate([np.arange(g * QS, (g + 1) * QS) for g in gs])
        xqT = np.ascontiguousarray(xT[b][:, cols])
        cosQ = np.ascontiguousarray(cosKF[:, cols])
        sinQ = np.ascontiguousarray(np.concatenate([sinf.T[:, cols], -sinf.T[:, cols]], axis=0))
        mm = np.zeros((SLOTS, 128, 2432), dtype=np.float32)
        for s in range(SLOTS):
            C = 128 * (16 * s + 15) - QS * gs[s]
            mm[s] = (jcols[None, :] - kps[:, None] >= C).astype(np.float32)
        in_maps.append({
            "xt": xT[b], "xq": xqT,
            "wqT": wqT, "wkT": wkT, "wvT": wvT, "woT": woT,
            "w1T": w1T, "w2T": w2T, "w3T": w3T,
            "cosKF": cosKF, "sinKF": sinKF,
            "cosQF": cosQ, "sinQF": sinQ,
            "maskm": mm.astype(ml_dtypes.bfloat16),
        })
    return in_maps


def kernel(L0_attn_x, L0_attn_wq, L0_attn_wk, L0_attn_wv, L0_attn_wo,
           L0_ffn_w1, L0_ffn_w2, L0_ffn_w3):
    x = np.asarray(L0_attn_x)
    in_maps = make_in_maps(x, np.asarray(L0_attn_wq), np.asarray(L0_attn_wk),
                           np.asarray(L0_attn_wv), np.asarray(L0_attn_wo),
                           np.asarray(L0_ffn_w1), np.asarray(L0_ffn_w2), np.asarray(L0_ffn_w3))
    if "prog" not in _NC_CACHE:
        _NC_CACHE["prog"] = build_program()
    nc = _NC_CACHE["prog"]
    res = run_bass_kernel_spmd(nc, in_maps, core_ids=list(range(8)))
    out = np.empty((Bn, T, E), dtype=np.float32)
    for c in range(8):
        b, j = c // 4, c % 4
        ytc = res.results[c]["yt"]
        for s in range(SLOTS):
            g = QTILES[j][s]
            out[b, g * QS:(g + 1) * QS, :] = ytc[:, s * QS:(s + 1) * QS].T
    return out


# version 03
# speedup vs baseline: 1.0508x; 1.0508x over previous
import numpy as np
import ml_dtypes
from collections import defaultdict
from contextlib import ExitStack

import concourse.bass as bass
import concourse.mybir as mybir
from concourse.bass_utils import run_bass_kernel_spmd

F32 = mybir.dt.float32
BF16 = mybir.dt.bfloat16
AF = mybir.ActivationFunctionType
ALU = mybir.AluOpType
AX = mybir.AxisListType

E = 2048
T = 8192
H = 16
D = 128
HD = 5632
Bn = 2
EC = E // 128
FBn = 16
HBn = 44
QS = 512
NQS = T // QS
SLOTS = 4
EXT = (16, 32, 48, 64)
PASSES = ((0, 3), (1, 2))
ISQD = float(1.0 / np.sqrt(np.float32(128.0)))
LN_EPS = 1e-5
QZ_EPS = 1e-8
QTILES = [[j, 7 - j, 8 + j, 15 - j] for j in range(4)]

ENGS = ("pe", "act", "dve", "pool", "sp")
PHASE_ORDER = ("qz", "kv", "q", "att", "w")


class Prog:
    def __init__(self):
        self.th = {k: [] for k in ENGS}
        self.cnt = defaultdict(int)
        self._waited = defaultdict(int)
        self.dma_regs = set()

    def w(self, eng, sem, val):
        if val is None or val <= 0:
            return
        assert val <= self.cnt[sem.name], \
            f"wait ahead of emission: {sem.name} {val} > {self.cnt[sem.name]}"
        key = (eng, sem.name)
        if self._waited[key] >= val:
            return
        self._waited[key] = val
        self.th[eng].append(lambda e, s=sem, v=val: e.wait_ge(s, v))

    def wt(self, eng, tup):
        if tup is not None:
            self.w(eng, tup[0], tup[1])

    def o(self, eng, fn, inc=None, n=1):
        if inc is not None:
            self.cnt[inc.name] += n
            self.th[eng].append(lambda e, f=fn, s=inc, m=n: f(e).then_inc(s, m))
            return self.cnt[inc.name]
        self.th[eng].append(lambda e, f=fn: f(e))
        return None

    def dma(self, eng, sem, out, in_):
        self.dma_regs.add((eng, sem))
        return self.o(eng, lambda e, o=out, i=in_: e.dma_start(out=o, in_=i), inc=sem, n=16)


class LRing:

    def __init__(self, P, nc, stk, name, n, shape, dt):
        self.P = P
        self.n = n
        self.tiles = [stk.enter_context(nc.sbuf_tensor(f"{name}{i}", shape, dt)) for i in range(n)]
        self.sems = [stk.enter_context(nc.semaphore(f"{name}S{i}")) for i in range(n)]
        self.mark = [None] * n

    def load(self, eng, i, in_, view=None):
        k = i % self.n
        if self.mark[k] is not None:
            self.P.wt(eng, self.mark[k])
        t = self.tiles[k]
        out = view(t) if view else t[:]
        c = self.P.dma(eng, self.sems[k], out, in_)
        return t, (self.sems[k], c)

    def rel(self, i, sem, cnt):
        if cnt:
            self.mark[i % self.n] = (sem, cnt)


class SRing:

    def __init__(self, P, nc, stk, name, n, shape, dt):
        self.P = P
        self.n = n
        self.tiles = [stk.enter_context(nc.sbuf_tensor(f"{name}{i}", shape, dt)) for i in range(n)]
        self.sems = [stk.enter_context(nc.semaphore(f"{name}S{i}")) for i in range(n)]
        self.mark = [None] * n

    def acq(self, eng, i):
        k = i % self.n
        if self.mark[k] is not None:
            self.P.wt(eng, self.mark[k])
        return self.tiles[k]

    def store(self, eng, i, out, view=None):
        k = i % self.n
        t = self.tiles[k]
        in_ = view(t) if view else t[:]
        c = self.P.dma(eng, self.sems[k], out, in_)
        self.mark[k] = (self.sems[k], c)
        return (self.sems[k], c)


def build_program(taps=(), upto="w"):
    nc = bass.Bass("TRN2", target_bir_lowering=False)
    taps = set(taps)
    nphase = PHASE_ORDER.index(upto)

    def have(ph):
        return PHASE_ORDER.index(ph) <= nphase

    def dram(name, shape, dt):
        kind = "ExternalOutput" if name in taps else "Internal"
        return nc.dram_tensor(name, shape, dt, kind=kind).ap()

    xt = nc.dram_tensor("xt", [E, T], F32, kind="ExternalInput").ap()
    xq = nc.dram_tensor("xq", [E, SLOTS * QS], F32, kind="ExternalInput").ap()
    wqT = nc.dram_tensor("wqT", [E, E], F32, kind="ExternalInput").ap()
    wkT = nc.dram_tensor("wkT", [E, E], F32, kind="ExternalInput").ap()
    wvT = nc.dram_tensor("wvT", [E, E], F32, kind="ExternalInput").ap()
    woT = nc.dram_tensor("woT", [E, E], F32, kind="ExternalInput").ap()
    w1T = nc.dram_tensor("w1T", [E, HD], F32, kind="ExternalInput").ap()
    w2T = nc.dram_tensor("w2T", [E, HD], F32, kind="ExternalInput").ap()
    w3T = nc.dram_tensor("w3T", [HD, E], F32, kind="ExternalInput").ap()
    cosKF = nc.dram_tensor("cosKF", [128, T], F32, kind="ExternalInput").ap()
    sinKF = nc.dram_tensor("sinKF", [128, T], F32, kind="ExternalInput").ap()
    cosQFd = nc.dram_tensor("cosQF", [128, SLOTS * QS], F32, kind="ExternalInput").ap()
    sinQFd = nc.dram_tensor("sinQF", [128, SLOTS * QS], F32, kind="ExternalInput").ap()
    maskm = nc.dram_tensor("maskm", [SLOTS, 128, 2432], BF16, kind="ExternalInput").ap()

    wB = {nm: dram(nm + "B", [FBn, 128, EC, 128], BF16) for nm in ("wq", "wk", "wo")}
    wvP = dram("wvP", [E, E], BF16)
    q1B = dram("q1B", [HBn, 128, EC, 128], BF16)
    q2B = dram("q2B", [HBn, 128, EC, 128], BF16)
    q3B = dram("q3B", [FBn, 128, HBn, 128], BF16)
    KbT = dram("KbT", [E, T], BF16)
    Vb = dram("Vb", [T, E], BF16)
    outTd = dram("outTd", [E, SLOTS * QS], BF16)
    if "qtap" in taps:
        qtap = nc.dram_tensor("qtap", [E, SLOTS * QS], BF16, kind="ExternalOutput").ap()
    if "thrtap" in taps:
        thrtap = nc.dram_tensor("thrtap", [3, 128, 2], F32, kind="ExternalOutput").ap()
    yt = nc.dram_tensor("yt", [E, SLOTS * QS], F32, kind="ExternalOutput").ap()

    P = Prog()

    with ExitStack() as top:
        En = top.enter_context

        s_pe = En(nc.semaphore("s_pe"))
        s_act = En(nc.semaphore("s_act"))
        s_dve = En(nc.semaphore("s_dve"))
        s_pool = En(nc.semaphore("s_pool"))
        s_bar = En(nc.semaphore("s_bar"))
        s_once = En(nc.semaphore("s_once"))
        s_out = En(nc.semaphore("s_out"))

        def once(eng, out, in_):
            sem = s_once if eng == "sp" else s_out
            return sem, P.dma(eng, sem, out, in_)

        def barrier():
            for (eng, sem) in sorted(P.dma_regs, key=lambda x: (x[0], x[1].name)):
                P.w(eng, sem, P.cnt[sem.name])
            for eng in ENGS:
                P.cnt[s_bar.name] += 1
                P.th[eng].append(lambda e: e.sem_inc(s_bar, 1))
            tgt = P.cnt[s_bar.name]
            for eng in ENGS:
                P.w(eng, s_bar, tgt)

        ones_bf = En(nc.sbuf_tensor("ones_bf", [128, 1], BF16))
        ones1f = En(nc.sbuf_tensor("ones1f", [1, 128], F32))
        onecol = En(nc.sbuf_tensor("onecol", [128, 1], F32))
        epsP = En(nc.sbuf_tensor("epsP", [1, 1], F32))
        statrow = En(nc.sbuf_tensor("statrow", [1, 6 * QS], F32))
        thr = [En(nc.sbuf_tensor(f"thr{i}", [128, 2], F32)) for i in range(3)]
        pb = [En(nc.psum_tensor(f"pb{i}", [128, 512], F32)) for i in range(8)]

        P.o("dve", lambda e: e.memset(ones_bf[:], 1.0))
        P.o("dve", lambda e: e.memset(ones1f[:], 1.0))
        P.o("dve", lambda e: e.memset(onecol[:], 1.0))
        c_init = P.o("dve", lambda e: e.memset(epsP[:], LN_EPS), inc=s_dve)

        def ln_fixup_and_bcast(W):
            mu = statrow[0:1, 0 * QS:0 * QS + W]
            m2 = statrow[0:1, 1 * QS:1 * QS + W]
            msq = statrow[0:1, 2 * QS:2 * QS + W]
            var = statrow[0:1, 3 * QS:3 * QS + W]
            rstd = statrow[0:1, 4 * QS:4 * QS + W]
            nmr = statrow[0:1, 5 * QS:5 * QS + W]
            P.o("dve", lambda e: e.tensor_scalar_mul(mu, pb[4][0:1, :W], 1.0 / E), inc=s_dve)
            e1 = P.o("dve", lambda e: e.tensor_scalar_mul(m2, pb[5][0:1, :W], 1.0 / E), inc=s_dve)
            P.w("dve", s_dve, e1)
            e2 = P.o("dve", lambda e: e.tensor_mul(msq, mu, mu), inc=s_dve)
            P.w("dve", s_dve, e2)
            dc = P.o("dve", lambda e: e.tensor_sub(var, m2, msq), inc=s_dve)
            P.w("act", s_dve, dc)
            P.w("act", s_dve, c_init)
            ac = P.o("act", lambda e: e.activation(var, var, AF.Sqrt, bias=epsP[:], scale=1.0),
                     inc=s_act)
            P.w("dve", s_act, ac)
            e3 = P.o("dve", lambda e: e.reciprocal(rstd, var), inc=s_dve)
            P.w("dve", s_dve, e3)
            dc = P.o("dve", lambda e: e.scalar_tensor_tensor(nmr, mu, -1.0, rstd, ALU.mult, ALU.mult),
                     inc=s_dve)
            P.w("pe", s_dve, dc)
            P.o("pe", lambda e: e.matmul(pb[6][:, :W], ones1f[0:1, :], rstd, start=True, stop=True))
            pc = P.o("pe", lambda e: e.matmul(pb[7][:, :W], ones1f[0:1, :], nmr, start=True, stop=True),
                     inc=s_pe)
            return pc

        cast_done = {}
        qz_stack = ExitStack()
        ldring = LRing(P, nc, qz_stack, "qzld", 2, [128, 2048], F32)
        stg = SRing(P, nc, qz_stack, "qzst", 2, [128, 2048], BF16)
        acc = [qz_stack.enter_context(nc.sbuf_tensor(f"qzacc{i}", [128, 48], F32)) for i in range(3)]
        accs = [qz_stack.enter_context(nc.sbuf_tensor(f"qzaccs{i}", [128, 1], F32)) for i in range(3)]
        crow = qz_stack.enter_context(nc.sbuf_tensor("qzcrow", [1, 1], F32))

        it = 0
        for nm, wsrc in (("wq", wqT), ("wk", wkT), ("wv", wvT), ("wo", woT)):
            for ecc in range(EC):
                buf, lw_ = ldring.load("sp", it, wsrc[ecc * 128:(ecc + 1) * 128, :])
                P.wt("dve", lw_)
                sb = stg.acq("dve", it)
                dc = P.o("dve", lambda e, o=sb, i=buf: e.tensor_copy(o[:], i[:]), inc=s_dve)
                ldring.rel(it, s_dve, dc)
                P.w("act", s_dve, dc)
                if nm == "wv":
                    stg.store("act", it, wvP[ecc * 128:(ecc + 1) * 128, :])
                else:
                    stg.store("act", it, wB[nm][:, :, ecc, :].rearrange("b p f -> p b f"),
                              view=lambda t: t[:].rearrange("p (b f) -> p b f", f=128))
                it += 1
            cast_done[nm] = [(s, P.cnt[s.name]) for s in stg.sems]

        WCHUNKS = []
        for wi, (src_, dst_, rows, cols) in enumerate(((w1T, q1B, E, HD), (w2T, q2B, E, HD),
                                                       (w3T, q3B, HD, E))):
            for r0 in range(0, rows, 128):
                for c0 in range(0, cols, 2048):
                    WCHUNKS.append((wi, src_, dst_, r0, c0, min(2048, cols - c0)))
        for i in range(3):
            P.o("dve", lambda e, a=acc[i]: e.memset(a[:], 0.0))
        pieces_seen = [0, 0, 0]
        gamma_cnt = [None, None, None]
        for (wi, src_, dst_, r0, c0, wdt) in WCHUNKS:
            buf, lw_ = ldring.load("sp", it, src_[r0:r0 + 128, c0:c0 + wdt],
                                   view=lambda t, w=wdt: t[:, :w])
            P.wt("dve", lw_)
            kcol = pieces_seen[wi]
            pieces_seen[wi] += 1
            dc = P.o("dve", lambda e, b=buf, w=wdt, a=acc[wi], k=kcol: e.reduce_sum(
                a[:, k:k + 1], b[:, :w], axis=AX.X, apply_absolute_value=True), inc=s_dve)
            ldring.rel(it, s_dve, dc)
            gamma_cnt[wi] = dc
            it += 1
        thr_done = [None, None, None]
        for wi in range(3):
            P.w("dve", s_dve, gamma_cnt[wi])
            gs_ = P.o("dve", lambda e, a=acc[wi], s_=accs[wi], n_=pieces_seen[wi]: e.reduce_sum(
                s_[:], a[:, :n_], axis=AX.X), inc=s_dve)
            P.w("pe", s_dve, gs_)
            pc = P.o("pe", lambda e, a=accs[wi]: e.matmul(pb[0][0:1, 0:1], a[:], onecol[:],
                                                          start=True, stop=True), inc=s_pe)
            P.w("dve", s_pe, pc)
            numel = float(E * HD)
            dc = P.o("dve", lambda e, n_=numel: e.tensor_scalar(
                crow[0:1, 0:1], pb[0][0:1, 0:1], 0.5 / n_, 0.5 * QZ_EPS, ALU.mult, ALU.add),
                inc=s_dve)
            P.w("pe", s_dve, dc)
            pc = P.o("pe", lambda e: e.matmul(pb[1][:, 0:1], ones1f[0:1, :], crow[0:1, 0:1],
                                              start=True, stop=True), inc=s_pe)
            P.w("dve", s_pe, pc)
            ec0 = P.o("dve", lambda e, t=thr[wi]: e.tensor_copy(t[:, 0:1], pb[1][:, 0:1]), inc=s_dve)
            P.w("dve", s_dve, ec0)
            thr_done[wi] = P.o("dve", lambda e, t=thr[wi]: e.tensor_scalar_mul(t[:, 1:2], t[:, 0:1], -1.0),
                               inc=s_dve)
        qz_stack.close()
        barrier()
        if "thrtap" in taps:
            for wi in range(3):
                once("act", thrtap[wi], thr[wi][:])
            barrier()

        qz_state = {"i": 0}

        def emit_quant(nmax, quld, qsA, qsB, qst):
            for _ in range(nmax):
                i = qz_state["i"]
                if i >= len(WCHUNKS):
                    return
                (wi, src_, dst_, r0, c0, wdt) = WCHUNKS[i]
                buf, lw_ = quld.load("sp", i, src_[r0:r0 + 128, c0:c0 + wdt],
                                     view=lambda t, w=wdt: t[:, :w])
                P.wt("pool", lw_)
                P.w("pool", s_dve, thr_done[wi])
                P.o("pool", lambda e, b=buf, w=wdt, t=thr[wi]: e.tensor_scalar(
                    qsA[:, :w], b[:, :w], t[:, 0:1], 2.0, ALU.is_ge, ALU.mult), inc=s_pool)
                a2 = P.o("pool", lambda e, b=buf, w=wdt, t=thr[wi]: e.tensor_scalar(
                    qsB[:, :w], b[:, :w], t[:, 1:2], 2.0, ALU.is_le, ALU.mult), inc=s_pool)
                sb = qst.acq("pool", i)
                P.w("pool", s_pool, a2)
                pc = P.o("pool", lambda e, o=sb, w=wdt: e.tensor_sub(o[:, :w], qsA[:, :w], qsB[:, :w]),
                         inc=s_pool)
                quld.rel(i, s_pool, pc)
                P.w("act", s_pool, pc)
                nb = wdt // 128
                gb0 = c0 // 128
                rcc = r0 // 128
                qst.store("act", i, dst_[gb0:gb0 + nb, :, rcc, :].rearrange("b p f -> p b f"),
                          view=lambda t, w=wdt: t[:, :w].rearrange("p (b f) -> p b f", f=128))
                qz_state["i"] = i + 1

        qst_rings = []
        if upto == "qz":
            qq_stack = ExitStack()
            quld = LRing(P, nc, qq_stack, "quld", 2, [128, 2048], F32)
            qsA = qq_stack.enter_context(nc.sbuf_tensor("qsA", [128, 2048], F32))
            qsB = qq_stack.enter_context(nc.sbuf_tensor("qsB", [128, 2048], F32))
            qst = SRing(P, nc, qq_stack, "qust", 2, [128, 2048], BF16)
            emit_quant(len(WCHUNKS), quld, qsA, qsB, qst)
            qq_stack.close()

        qt_stack = ExitStack()
        if have("q"):
            QT = [qt_stack.enter_context(nc.sbuf_tensor(f"QT{h}", [128, SLOTS * QS], BF16)) for h in range(H)]

        if have("kv"):
            kv_stack = ExitStack()
            xbf = [kv_stack.enter_context(nc.sbuf_tensor(f"xbf{i}", [128, QS], BF16)) for i in range(EC)]
            kv_ld = LRing(P, nc, kv_stack, "kvld", 2, [128, QS], F32)
            kv_sqt = [kv_stack.enter_context(nc.sbuf_tensor(f"kvsq{i}", [128, QS], BF16)) for i in range(2)]
            kv_scr = kv_stack.enter_context(nc.sbuf_tensor("kvscr", [128, QS], F32))
            wvsp = [kv_stack.enter_context(nc.sbuf_tensor(f"kvwv{i}", [128, E], BF16)) for i in range(EC)]
            wk_r = LRing(P, nc, kv_stack, "kvwk", 2, [128, E], BF16)
            kst = SRing(P, nc, kv_stack, "kvks", 2, [128, QS], BF16)
            vst = SRing(P, nc, kv_stack, "kvvs", 2, [128, E], BF16)
            ctab = LRing(P, nc, kv_stack, "kvct", 2, [128, QS], F32)
            stab = LRing(P, nc, kv_stack, "kvst", 2, [128, QS], F32)
            kv_t1 = kv_stack.enter_context(nc.sbuf_tensor("kvt1", [128, QS], F32))
            kv_t2 = kv_stack.enter_context(nc.sbuf_tensor("kvt2", [128, QS], F32))
            kv_sqm = [None, None]
            kbmark = [None, None]

            def ln_block(tiles, sq_tiles, sqm, load_src, ldr, scr, it0):
                it_ = it0
                stat_stop = None
                for ecc in range(EC):
                    buf, lw_ = ldr.load("sp", it_, load_src(ecc))
                    P.wt("dve", lw_)
                    dc = P.o("dve", lambda e, o=tiles[ecc], i=buf: e.tensor_copy(o[:], i[:]), inc=s_dve)
                    ldr.rel(it_, s_dve, dc)
                    k2 = it_ % 2
                    if sqm[k2] is not None:
                        P.wt("act", sqm[k2])
                    sqb = sq_tiles[k2]
                    P.w("act", s_dve, dc)
                    ac = P.o("act", lambda e, o=sqb, i=tiles[ecc]: e.activation(o[:], i[:], AF.Square),
                             inc=s_act)
                    P.w("pe", s_act, ac)
                    P.o("pe", lambda e, i=tiles[ecc], st=(ecc == 0), sp_=(ecc == EC - 1): e.matmul(
                        pb[4][0:1, :], ones_bf[:], i[:], start=st, stop=sp_))
                    pc = P.o("pe", lambda e, i=sqb, st=(ecc == 0), sp_=(ecc == EC - 1): e.matmul(
                        pb[5][0:1, :], ones_bf[:], i[:], start=st, stop=sp_), inc=s_pe)
                    sqm[k2] = (s_pe, pc)
                    stat_stop = pc
                    it_ += 1
                P.w("dve", s_pe, stat_stop)
                bc = ln_fixup_and_bcast(QS)
                P.w("dve", s_pe, bc)
                ndone = None
                for ecc in range(EC):
                    em = P.o("dve", lambda e, x=tiles[ecc]: e.tensor_mul(scr[:], x[:], pb[6][:, :]),
                             inc=s_dve)
                    P.w("dve", s_dve, em)
                    ndone = P.o("dve", lambda e, x=tiles[ecc]: e.tensor_add(x[:], scr[:], pb[7][:, :]),
                                inc=s_dve)
                return ndone, it_

            for tup in cast_done["wv"]:
                P.wt("sp", tup)
            wv_lc = None
            for ecc in range(EC):
                wv_lc = once("sp", wvsp[ecc][:], wvP[ecc * 128:(ecc + 1) * 128, :])

            it = 0
            for qs_i in range(NQS):
                gt0 = qs_i * QS
                ndone, it = ln_block(xbf, kv_sqt, kv_sqm,
                                     lambda ecc, g=gt0: xt[ecc * 128:(ecc + 1) * 128, g:g + QS],
                                     kv_ld, kv_scr, it)
                P.w("pe", s_dve, ndone)
                P.wt("pe", wv_lc)
                for tt in range(4):
                    vmm = None
                    for ecc in range(EC):
                        for f4 in range(4):
                            vmm = P.o("pe", lambda e, b=pb[2 + f4], w=wvsp[ecc], c=ecc, t_=tt, f_=f4,
                                      st=(ecc == 0), sp_=(ecc == EC - 1): e.matmul(
                                b[:, :], xbf[c][:, t_ * 128:(t_ + 1) * 128], w[:, f_ * 512:(f_ + 1) * 512],
                                start=st, stop=sp_),
                                inc=(s_pe if (ecc == EC - 1 and f4 == 3) else None))
                    vs = vst.acq("act", qs_i * 4 + tt)
                    P.w("act", s_pe, vmm)
                    vdone = None
                    for f4 in range(4):
                        vdone = P.o("act", lambda e, o=vs, b=pb[2 + f4], f_=f4: e.copy(
                            o[:, f_ * 512:(f_ + 1) * 512], b[:, :]), inc=s_act)
                    P.w("act", s_act, vdone)
                    vst.store("act", qs_i * 4 + tt, Vb[gt0 + tt * 128:gt0 + (tt + 1) * 128, :])
                    P.w("pe", s_act, vdone)
                cb, cl = ctab.load("sp", qs_i, cosKF[:, gt0:gt0 + QS])
                sb_, sl = stab.load("sp", qs_i, sinKF[:, gt0:gt0 + QS])
                for tup in cast_done["wk"]:
                    P.wt("sp", tup)
                kdone = None
                for fb in range(FBn):
                    wkb, lk = wk_r.load("sp", qs_i * FBn + fb,
                                        wB["wk"][fb],
                                        view=lambda t: t[:].rearrange("p (c f) -> p c f", f=128))
                    bk = pb[0 + (fb % 2)]
                    P.wt("pe", kbmark[fb % 2])
                    P.wt("pe", lk)
                    kmm = None
                    for ecc in range(EC):
                        kmm = P.o("pe", lambda e, b=bk, w=wkb, c=ecc, st=(ecc == 0), sp_=(ecc == EC - 1): e.matmul(
                            b[:, :], w[:, c * 128:(c + 1) * 128], xbf[c][:, :], start=st, stop=sp_),
                            inc=(s_pe if ecc == EC - 1 else None))
                    wk_r.rel(qs_i * FBn + fb, s_pe, kmm)
                    ks = kst.acq("dve", qs_i * FBn + fb)
                    P.w("dve", s_pe, kmm)
                    P.wt("dve", cl)
                    P.wt("dve", sl)
                    P.o("dve", lambda e, b=bk, c=cb: e.tensor_mul(kv_t1[:], b[:, :], c[:]), inc=s_dve)
                    P.o("dve", lambda e, b=bk, s=sb_: e.tensor_mul(kv_t2[0:64, :], b[64:128, :], s[64:128, :]),
                        inc=s_dve)
                    em = P.o("dve", lambda e, b=bk, s=sb_: e.tensor_mul(kv_t2[64:128, :], b[0:64, :], s[0:64, :]),
                             inc=s_dve)
                    P.w("dve", s_dve, em)
                    kdone = P.o("dve", lambda e, o=ks: e.tensor_add(o[:], kv_t1[:], kv_t2[:]), inc=s_dve)
                    kbmark[fb % 2] = (s_dve, kdone)
                    P.w("act", s_dve, kdone)
                    kst.store("act", qs_i * FBn + fb, KbT[fb * 128:(fb + 1) * 128, gt0:gt0 + QS])
                ctab.rel(qs_i, s_dve, kdone)
                stab.rel(qs_i, s_dve, kdone)
            kv_stack.close()
            barrier()

        if have("q"):
            q_stack = ExitStack()
            xqb = [q_stack.enter_context(nc.sbuf_tensor(f"xqb{i}", [128, QS], BF16)) for i in range(EC)]
            q_ld = LRing(P, nc, q_stack, "qld", 2, [128, QS], F32)
            q_sqt = [q_stack.enter_context(nc.sbuf_tensor(f"qsq{i}", [128, QS], BF16)) for i in range(2)]
            q_scr = q_stack.enter_context(nc.sbuf_tensor("qscr", [128, QS], F32))
            wq_r = LRing(P, nc, q_stack, "qwq", 2, [128, E], BF16)
            qtabC = q_stack.enter_context(nc.sbuf_tensor("qtabC", [128, SLOTS * QS], F32))
            qtabS = q_stack.enter_context(nc.sbuf_tensor("qtabS", [128, SLOTS * QS], F32))
            q_t1 = q_stack.enter_context(nc.sbuf_tensor("qt1", [128, QS], F32))
            q_t2 = q_stack.enter_context(nc.sbuf_tensor("qt2", [128, QS], F32))
            q_sqm = [None, None]
            qbmark = [None, None]

            def ln_block_q(tiles, sq_tiles, sqm, load_src, ldr, scr, it0):
                it_ = it0
                stat_stop = None
                for ecc in range(EC):
                    buf, lw_ = ldr.load("sp", it_, load_src(ecc))
                    P.wt("dve", lw_)
                    dc = P.o("dve", lambda e, o=tiles[ecc], i=buf: e.tensor_copy(o[:], i[:]), inc=s_dve)
                    ldr.rel(it_, s_dve, dc)
                    k2 = it_ % 2
                    if sqm[k2] is not None:
                        P.wt("act", sqm[k2])
                    sqb = sq_tiles[k2]
                    P.w("act", s_dve, dc)
                    ac = P.o("act", lambda e, o=sqb, i=tiles[ecc]: e.activation(o[:], i[:], AF.Square),
                             inc=s_act)
                    P.w("pe", s_act, ac)
                    P.o("pe", lambda e, i=tiles[ecc], st=(ecc == 0), sp_=(ecc == EC - 1): e.matmul(
                        pb[4][0:1, :], ones_bf[:], i[:], start=st, stop=sp_))
                    pc = P.o("pe", lambda e, i=sqb, st=(ecc == 0), sp_=(ecc == EC - 1): e.matmul(
                        pb[5][0:1, :], ones_bf[:], i[:], start=st, stop=sp_), inc=s_pe)
                    sqm[k2] = (s_pe, pc)
                    stat_stop = pc
                    it_ += 1
                P.w("dve", s_pe, stat_stop)
                bc = ln_fixup_and_bcast(QS)
                P.w("dve", s_pe, bc)
                ndone = None
                for ecc in range(EC):
                    em = P.o("dve", lambda e, x=tiles[ecc]: e.tensor_mul(scr[:], x[:], pb[6][:, :]),
                             inc=s_dve)
                    P.w("dve", s_dve, em)
                    ndone = P.o("dve", lambda e, x=tiles[ecc]: e.tensor_add(x[:], scr[:], pb[7][:, :]),
                                inc=s_dve)
                return ndone, it_

            tc0 = once("sp", qtabC[:], cosQFd)
            tc1 = once("sp", qtabS[:], sinQFd)

            it = 0
            for s in range(SLOTS):
                q0 = s * QS
                ndone, it = ln_block_q(xqb, q_sqt, q_sqm,
                                       lambda ecc, q=q0: xq[ecc * 128:(ecc + 1) * 128, q:q + QS],
                                       q_ld, q_scr, it)
                P.w("pe", s_dve, ndone)
                for tup in cast_done["wq"]:
                    P.wt("sp", tup)
                P.wt("dve", tc0)
                P.wt("dve", tc1)
                for fb in range(FBn):
                    wqb_t, lq = wq_r.load("sp", s * FBn + fb,
                                          wB["wq"][fb],
                                          view=lambda t: t[:].rearrange("p (c f) -> p c f", f=128))
                    bq = pb[0 + (fb % 2)]
                    P.wt("pe", qbmark[fb % 2])
                    P.wt("pe", lq)
                    qmm = None
                    for ecc in range(EC):
                        qmm = P.o("pe", lambda e, b=bq, w=wqb_t, c=ecc, st=(ecc == 0), sp_=(ecc == EC - 1): e.matmul(
                            b[:, :], w[:, c * 128:(c + 1) * 128], xqb[c][:, :], start=st, stop=sp_),
                            inc=(s_pe if ecc == EC - 1 else None))
                    wq_r.rel(s * FBn + fb, s_pe, qmm)
                    P.w("dve", s_pe, qmm)
                    P.o("dve", lambda e, b=bq, q_=q0: e.tensor_mul(q_t1[:], b[:, :], qtabC[:, q_:q_ + QS]),
                        inc=s_dve)
                    P.o("dve", lambda e, b=bq, q_=q0: e.tensor_mul(q_t2[0:64, :], b[64:128, :], qtabS[64:128, q_:q_ + QS]),
                        inc=s_dve)
                    em = P.o("dve", lambda e, b=bq, q_=q0: e.tensor_mul(q_t2[64:128, :], b[0:64, :], qtabS[0:64, q_:q_ + QS]),
                             inc=s_dve)
                    P.w("dve", s_dve, em)
                    qdone = P.o("dve", lambda e, h=fb, q_=q0: e.tensor_add(QT[h][:, q_:q_ + QS], q_t1[:], q_t2[:]),
                                inc=s_dve)
                    qbmark[fb % 2] = (s_dve, qdone)
            q_stack.close()
            barrier()
            if "qtap" in taps:
                for h in range(H):
                    once("act", qtap[h * 128:(h + 1) * 128, :], QT[h][:])
                barrier()

        if have("att"):
            att_stack = ExitStack()
            kt_r = LRing(P, nc, att_stack, "akt", 2, [128, 1024], BF16)
            vt_r = LRing(P, nc, att_stack, "avt", 2, [128, 8 * 128], BF16)
            ptile = [att_stack.enter_context(nc.sbuf_tensor(f"ap{i}", [128, QS], BF16)) for i in range(6)]
            pmark = [None] * 6
            masks = [att_stack.enter_context(nc.sbuf_tensor(f"amask{i}", [128, 2432], BF16)) for i in range(SLOTS)]
            rrow = att_stack.enter_context(nc.sbuf_tensor("arr", [1, 2 * QS], F32))
            rbt = att_stack.enter_context(nc.sbuf_tensor("arb", [128, QS], F32))
            ost = SRing(P, nc, att_stack, "aost", 2, [128, QS], BF16)
            quld = LRing(P, nc, att_stack, "quld", 2, [128, 2048], F32)
            qsA = att_stack.enter_context(nc.sbuf_tensor("qsA", [128, 2048], F32))
            qsB = att_stack.enter_context(nc.sbuf_tensor("qsB", [128, 2048], F32))
            qst = SRing(P, nc, att_stack, "qust", 2, [128, 2048], BF16)
            qst_rings = [qst]

            mload = None
            for s in range(SLOTS):
                mload = once("sp", masks[s][:], maskm[s])

            sbw = [None, None]
            pvfree = [None, None]
            rfree = [None, None]
            rrmark = [None, None]
            pb6mark = [None]
            oi = [0]
            ci = 0

            def finish(h, slot, ridx, pvstop):
                rbank = pb[2 + ridx]
                vbank = pb[4 + ridx]
                P.w("dve", s_pe, pvstop)
                if rrmark[ridx] is not None:
                    P.w("dve", s_pe, rrmark[ridx])
                rr = rrow[0:1, ridx * QS:(ridx + 1) * QS]
                dc = P.o("dve", lambda e, b=rbank, r=rr: e.reciprocal(r, b[0:1, :]), inc=s_dve)
                rfree[ridx] = dc
                P.w("pe", s_dve, dc)
                if pb6mark[0] is not None:
                    P.w("pe", s_dve, pb6mark[0])
                bc = P.o("pe", lambda e, r=rr: e.matmul(pb[6][:, :], ones1f[0:1, :], r,
                                                        start=True, stop=True), inc=s_pe)
                rrmark[ridx] = bc
                P.w("dve", s_pe, bc)
                em = P.o("dve", lambda e: e.tensor_copy(rbt[:], pb[6][:, :]), inc=s_dve)
                P.w("dve", s_dve, em)
                osb = ost.acq("dve", oi[0])
                dc2 = P.o("dve", lambda e, o=osb, b=vbank: e.tensor_mul(o[:], b[:, :], rbt[:]), inc=s_dve)
                pb6mark[0] = dc2
                pvfree[ridx] = dc2
                P.w("act", s_dve, dc2)
                ost.store("act", oi[0], outTd[h * 128:(h + 1) * 128, slot * QS:(slot + 1) * QS])
                oi[0] += 1

            P.wt("pool", mload)
            for h in range(H):
                for (sA, sB) in PASSES:
                    emax = max(EXT[sA], EXT[sB])
                    nkblk = (emax + 7) // 8
                    pend = [None]
                    last_pv = {}
                    for kb in range(nkblk):
                        kt, lk = kt_r.load("sp", kb, KbT[h * 128:(h + 1) * 128, kb * 1024:(kb + 1) * 1024])
                        vt, lv = vt_r.load("sp", kb,
                                           Vb[kb * 1024:(kb + 1) * 1024, h * 128:(h + 1) * 128]
                                           .rearrange("(kc p) d -> p kc d", p=128),
                                           view=lambda t: t[:].rearrange("p (kc d) -> p kc d", d=128))
                        last_smm = None
                        for i8 in range(8):
                            kc = kb * 8 + i8
                            for slot, ridx in ((sA, 0), (sB, 1)):
                                if kc >= EXT[slot]:
                                    continue
                                sbank_i = ci % 2
                                sbank = pb[sbank_i]
                                pslot_i = ci % 6
                                pslot = ptile[pslot_i]
                                if pmark[pslot_i] is not None:
                                    P.wt("act", pmark[pslot_i])
                                if sbw[sbank_i] is not None:
                                    P.w("pe", s_act, sbw[sbank_i])
                                P.wt("pe", lk)
                                smm = P.o("pe", lambda e, b=sbank, k=kt, i=i8, s_=slot, h_=h: e.matmul(
                                    b[:, :], k[:, i * 128:(i + 1) * 128], QT[h_][:, s_ * QS:(s_ + 1) * QS],
                                    start=True, stop=True), inc=s_pe)
                                last_smm = smm
                                if pend[0] is not None:
                                    pend[0]()
                                    pend[0] = None
                                P.w("act", s_pe, smm)
                                ec_ = P.o("act", lambda e, o=pslot, b=sbank: e.activation(
                                    o[:], b[:, :], AF.Exp, bias=0.0, scale=ISQD), inc=s_act)
                                sbw[sbank_i] = ec_
                                gate = (s_act, ec_)
                                if 16 * slot <= kc < 16 * slot + 16:
                                    j0 = 128 * (16 * slot + 15 - kc)
                                    P.w("pool", s_act, ec_)
                                    mc = P.o("pool", lambda e, p_=pslot, m=masks[slot], j=j0: e.tensor_mul(
                                        p_[:], p_[:], m[:, j:j + QS]), inc=s_pool)
                                    gate = (s_pool, mc)

                                def mk_pend(slot=slot, ridx=ridx, kc=kc, pslot=pslot, pslot_i=pslot_i,
                                            gate=gate, vt=vt, i8=i8, lv=lv, h=h, kb=kb):
                                    def emit():
                                        P.w("pe", gate[0], gate[1])
                                        P.wt("pe", lv)
                                        rbank = pb[2 + ridx]
                                        vbank = pb[4 + ridx]
                                        if kc == 0:
                                            if rfree[ridx] is not None:
                                                P.w("pe", s_dve, rfree[ridx])
                                            if pvfree[ridx] is not None:
                                                P.w("pe", s_dve, pvfree[ridx])
                                        P.o("pe", lambda e, b=rbank, p_=pslot, st=(kc == 0),
                                            sp_=(kc == EXT[slot] - 1): e.matmul(
                                            b[0:1, :], ones_bf[:], p_[:], start=st, stop=sp_))
                                        pc = P.o("pe", lambda e, b=vbank, v=vt, i=i8, p_=pslot,
                                                 st=(kc == 0), sp_=(kc == EXT[slot] - 1): e.matmul(
                                            b[:, :], v[:, i * 128:(i + 1) * 128], p_[:], start=st, stop=sp_),
                                            inc=s_pe)
                                        pmark[pslot_i] = (s_pe, pc)
                                        last_pv[kb] = pc
                                        if kc == EXT[slot] - 1:
                                            finish(h, slot, ridx, pc)
                                    return emit

                                pend[0] = mk_pend()
                                ci += 1
                        kt_r.rel(kb, s_pe, last_smm)
                        if kb - 1 in last_pv:
                            vt_r.rel(kb - 1, s_pe, last_pv[kb - 1])
                    if pend[0] is not None:
                        pend[0]()
                        pend[0] = None
                    for kb in range(nkblk):
                        if kb in last_pv:
                            vt_r.rel(kb, s_pe, last_pv[kb])
                    emit_quant(3, quld, qsA, qsB, qst)
            emit_quant(len(WCHUNKS), quld, qsA, qsB, qst)
            qst_done = [(s, P.cnt[s.name]) for s in qst.sems]
            att_stack.close()
            qt_stack.close()
            barrier()
        else:
            qt_stack.close()
            qst_done = []

        if have("w"):
            w_stack = ExitStack()
            s_ot = w_stack.enter_context(nc.semaphore("s_ot"))
            otile = [w_stack.enter_context(nc.sbuf_tensor(f"wot{i}", [128, QS], BF16)) for i in range(EC)]
            x1T = [w_stack.enter_context(nc.sbuf_tensor(f"wx1{i}", [128, QS], F32)) for i in range(FBn)]
            ln2 = [w_stack.enter_context(nc.sbuf_tensor(f"wln{i}", [128, QS], BF16)) for i in range(FBn)]
            hid = [w_stack.enter_context(nc.sbuf_tensor(f"whid{i}", [128, QS], BF16)) for i in range(HBn)]
            wcol = LRing(P, nc, w_stack, "wwc", 2, [128, E], BF16)
            q12c = LRing(P, nc, w_stack, "wq12", 4, [128, E], BF16)
            q3c = LRing(P, nc, w_stack, "wq3", 2, [128, HBn * 128], BF16)
            xres = LRing(P, nc, w_stack, "wxr", 2, [128, QS], F32)
            sig2 = [w_stack.enter_context(nc.sbuf_tensor(f"wsig{i}", [128, QS], BF16)) for i in range(2)]
            w_sqt = [w_stack.enter_context(nc.sbuf_tensor(f"wsq{i}", [128, QS], BF16)) for i in range(2)]
            w_sqm = [None, None]
            wbmark = [None, None]
            b1mark = [None, None]
            b2mark = [None, None]
            b3mark = [None, None]
            sigmark = [None, None]
            yst = SRing(P, nc, w_stack, "wyst", 2, [128, QS], F32)
            w_scr = w_stack.enter_context(nc.sbuf_tensor("wscr", [128, QS], F32))

            for tup in qst_done:
                P.wt("sp", tup)
            wo_last = [None]
            for s in range(SLOTS):
                q0 = s * QS
                if wo_last[0] is not None:
                    P.w("sp", s_pe, wo_last[0])
                for ecc in range(EC):
                    P.dma("sp", s_ot, otile[ecc][:], outTd[ecc * 128:(ecc + 1) * 128, q0:q0 + QS])
                olc = (s_ot, P.cnt[s_ot.name])
                for tup in cast_done["wo"]:
                    P.wt("sp", tup)
                for fb in range(FBn):
                    wc, lw = wcol.load("sp", s * FBn + fb,
                                       wB["wo"][fb],
                                       view=lambda t: t[:].rearrange("p (c f) -> p c f", f=128))
                    xr, lx = xres.load("sp", s * FBn + fb, xq[fb * 128:(fb + 1) * 128, q0:q0 + QS])
                    bw = pb[fb % 2]
                    P.wt("pe", wbmark[fb % 2])
                    P.wt("pe", lw)
                    P.wt("pe", olc)
                    wmm = None
                    for ecc in range(EC):
                        wmm = P.o("pe", lambda e, b=bw, w=wc, c=ecc, st=(ecc == 0), sp_=(ecc == EC - 1): e.matmul(
                            b[:, :], w[:, c * 128:(c + 1) * 128], otile[c][:], start=st, stop=sp_),
                            inc=(s_pe if ecc == EC - 1 else None))
                    wo_last[0] = wmm
                    wcol.rel(s * FBn + fb, s_pe, wmm)
                    P.w("dve", s_pe, wmm)
                    P.wt("dve", lx)
                    dc = P.o("dve", lambda e, o=x1T[fb], b=bw, x=xr: e.tensor_add(o[:], b[:, :], x[:]),
                             inc=s_dve)
                    xres.rel(s * FBn + fb, s_dve, dc)
                    wbmark[fb % 2] = (s_dve, dc)
                stat_stop = None
                P.w("act", s_dve, P.cnt[s_dve.name])
                for fb in range(FBn):
                    k2 = fb % 2
                    if w_sqm[k2] is not None:
                        P.wt("act", w_sqm[k2])
                    sqb = w_sqt[k2]
                    ac = P.o("act", lambda e, o=sqb, i=x1T[fb]: e.activation(o[:], i[:], AF.Square),
                             inc=s_act)
                    P.w("pe", s_act, ac)
                    P.o("pe", lambda e, i=x1T[fb], st=(fb == 0), sp_=(fb == FBn - 1): e.matmul(
                        pb[4][0:1, :], onecol[:], i[:], start=st, stop=sp_))
                    pc = P.o("pe", lambda e, i=sqb, st=(fb == 0), sp_=(fb == FBn - 1): e.matmul(
                        pb[5][0:1, :], ones_bf[:], i[:], start=st, stop=sp_), inc=s_pe)
                    w_sqm[k2] = (s_pe, pc)
                    stat_stop = pc
                P.w("dve", s_pe, stat_stop)
                bc = ln_fixup_and_bcast(QS)
                P.w("dve", s_pe, bc)
                ndone = None
                for fb in range(FBn):
                    em = P.o("dve", lambda e, x=x1T[fb]: e.tensor_mul(w_scr[:], x[:], pb[6][:, :]),
                             inc=s_dve)
                    P.w("dve", s_dve, em)
                    ndone = P.o("dve", lambda e, o=ln2[fb]: e.tensor_add(o[:], w_scr[:], pb[7][:, :]),
                                inc=s_dve)
                P.w("pe", s_dve, ndone)
                for hb in range(HBn):
                    c1, l1 = q12c.load("sp", 2 * (s * HBn + hb),
                                       q1B[hb],
                                       view=lambda t: t[:].rearrange("p (c f) -> p c f", f=128))
                    c2, l2 = q12c.load("sp", 2 * (s * HBn + hb) + 1,
                                       q2B[hb],
                                       view=lambda t: t[:].rearrange("p (c f) -> p c f", f=128))
                    b1 = pb[hb % 2]
                    b2 = pb[2 + hb % 2]
                    P.wt("pe", b1mark[hb % 2])
                    P.wt("pe", l1)
                    g1 = None
                    for ecc in range(EC):
                        g1 = P.o("pe", lambda e, b=b1, w=c1, c=ecc, st=(ecc == 0), sp_=(ecc == EC - 1): e.matmul(
                            b[:, :], w[:, c * 128:(c + 1) * 128], ln2[c][:], start=st, stop=sp_),
                            inc=(s_pe if ecc == EC - 1 else None))
                    P.wt("pe", b2mark[hb % 2])
                    P.wt("pe", l2)
                    g2 = None
                    for ecc in range(EC):
                        g2 = P.o("pe", lambda e, b=b2, w=c2, c=ecc, st=(ecc == 0), sp_=(ecc == EC - 1): e.matmul(
                            b[:, :], w[:, c * 128:(c + 1) * 128], ln2[c][:], start=st, stop=sp_),
                            inc=(s_pe if ecc == EC - 1 else None))
                    q12c.rel(2 * (s * HBn + hb), s_pe, g1)
                    q12c.rel(2 * (s * HBn + hb) + 1, s_pe, g2)
                    P.w("act", s_pe, g1)
                    sgt = sig2[hb % 2]
                    P.wt("act", sigmark[hb % 2])
                    ac = P.o("act", lambda e, b=b1, o=sgt: e.activation(o[:], b[:, :], AF.Sigmoid,
                                                                        bias=0.0, scale=0.5), inc=s_act)
                    b1mark[hb % 2] = (s_act, ac)
                    P.w("dve", s_pe, g2)
                    P.w("dve", s_act, ac)
                    dc = P.o("dve", lambda e, o=hid[hb], b=b2, sg=sgt: e.scalar_tensor_tensor(
                        o[:], b[:, :], 0.5, sg[:], ALU.mult, ALU.mult), inc=s_dve)
                    b2mark[hb % 2] = (s_dve, dc)
                    sigmark[hb % 2] = (s_dve, dc)
                for fb in range(FBn):
                    c3, l3 = q3c.load("sp", s * FBn + fb,
                                      q3B[fb],
                                      view=lambda t: t[:].rearrange("p (c f) -> p c f", f=128))
                    b3 = pb[4 + fb % 2]
                    P.wt("pe", b3mark[fb % 2])
                    P.wt("pe", l3)
                    g3 = None
                    for hb in range(HBn):
                        g3 = P.o("pe", lambda e, b=b3, w=c3, c=hb, st=(hb == 0), sp_=(hb == HBn - 1): e.matmul(
                            b[:, :], w[:, c * 128:(c + 1) * 128], hid[c][:], start=st, stop=sp_),
                            inc=(s_pe if hb == HBn - 1 else None))
                    q3c.rel(s * FBn + fb, s_pe, g3)
                    ys = yst.acq("dve", s * FBn + fb)
                    P.w("dve", s_pe, g3)
                    dc = P.o("dve", lambda e, o=ys, b=b3, x=x1T[fb]: e.scalar_tensor_tensor(
                        o[:], b[:, :], 0.5, x[:], ALU.mult, ALU.add), inc=s_dve)
                    b3mark[fb % 2] = (s_dve, dc)
                    P.w("act", s_dve, dc)
                    yst.store("act", s * FBn + fb, yt[fb * 128:(fb + 1) * 128, q0:q0 + QS])
            w_stack.close()

        for (eng, sem) in sorted(P.dma_regs, key=lambda x: (x[0], x[1].name)):
            P.w("sp", sem, P.cnt[sem.name])

        with nc.Block() as block:
            @block.tensor
            def _(t):
                for f in P.th["pe"]:
                    f(t)

            @block.scalar
            def _(a):
                for f in P.th["act"]:
                    f(a)

            @block.vector
            def _(v):
                for f in P.th["dve"]:
                    f(v)

            @block.gpsimd
            def _(g):
                for f in P.th["pool"]:
                    f(g)

            @block.sync
            def _(sy):
                for f in P.th["sp"]:
                    f(sy)

    return nc



def _rope_tables():
    inv = (1.0 / (np.float32(1000000.0) ** (np.arange(0, D, 2, dtype=np.float32) / np.float32(D)))).astype(np.float32)
    t = np.arange(T, dtype=np.float32)
    f = t[:, None] * inv[None, :]
    return np.cos(f).astype(np.float32), np.sin(f).astype(np.float32)


_NC_CACHE = {}


def make_in_maps(x, wq, wk, wv, wo, w1, w2, w3):
    cosf, sinf = _rope_tables()
    cosKF = np.ascontiguousarray(np.concatenate([cosf.T, cosf.T], axis=0))
    sinKF = np.ascontiguousarray(np.concatenate([sinf.T, -sinf.T], axis=0))
    xT = [np.ascontiguousarray(x[b].T) for b in range(Bn)]
    wqT = np.ascontiguousarray(wq.T); wkT = np.ascontiguousarray(wk.T)
    wvT = np.ascontiguousarray(wv.T); woT = np.ascontiguousarray(wo.T)
    w1T = np.ascontiguousarray(w1.T); w2T = np.ascontiguousarray(w2.T)
    w3T = np.ascontiguousarray(w3.T)
    jcols = np.arange(2432)
    kps = np.arange(128)
    in_maps = []
    for c in range(8):
        b, j = c // 4, c % 4
        gs = QTILES[j]
        cols = np.concatenate([np.arange(g * QS, (g + 1) * QS) for g in gs])
        xqT = np.ascontiguousarray(xT[b][:, cols])
        cosQ = np.ascontiguousarray(cosKF[:, cols])
        sinQ = np.ascontiguousarray(np.concatenate([sinf.T[:, cols], -sinf.T[:, cols]], axis=0))
        mm = np.zeros((SLOTS, 128, 2432), dtype=np.float32)
        for s in range(SLOTS):
            C = 128 * (16 * s + 15) - QS * gs[s]
            mm[s] = (jcols[None, :] - kps[:, None] >= C).astype(np.float32)
        in_maps.append({
            "xt": xT[b], "xq": xqT,
            "wqT": wqT, "wkT": wkT, "wvT": wvT, "woT": woT,
            "w1T": w1T, "w2T": w2T, "w3T": w3T,
            "cosKF": cosKF, "sinKF": sinKF,
            "cosQF": cosQ, "sinQF": sinQ,
            "maskm": mm.astype(ml_dtypes.bfloat16),
        })
    return in_maps


def kernel(L0_attn_x, L0_attn_wq, L0_attn_wk, L0_attn_wv, L0_attn_wo,
           L0_ffn_w1, L0_ffn_w2, L0_ffn_w3):
    x = np.asarray(L0_attn_x)
    in_maps = make_in_maps(x, np.asarray(L0_attn_wq), np.asarray(L0_attn_wk),
                           np.asarray(L0_attn_wv), np.asarray(L0_attn_wo),
                           np.asarray(L0_ffn_w1), np.asarray(L0_ffn_w2), np.asarray(L0_ffn_w3))
    if "prog" not in _NC_CACHE:
        _NC_CACHE["prog"] = build_program()
    nc = _NC_CACHE["prog"]
    res = run_bass_kernel_spmd(nc, in_maps, core_ids=list(range(8)))
    out = np.empty((Bn, T, E), dtype=np.float32)
    for c in range(8):
        b, j = c // 4, c % 4
        ytc = res.results[c]["yt"]
        for s in range(SLOTS):
            g = QTILES[j][s]
            out[b, g * QS:(g + 1) * QS, :] = ytc[:, s * QS:(s + 1) * QS].T
    return out


# version 05
# speedup vs baseline: 1.1018x; 1.0485x over previous
import numpy as np
import ml_dtypes
from collections import defaultdict
from contextlib import ExitStack

import concourse.bass as bass
import concourse.mybir as mybir
from concourse.bass_utils import run_bass_kernel_spmd

F32 = mybir.dt.float32
BF16 = mybir.dt.bfloat16
AF = mybir.ActivationFunctionType
ALU = mybir.AluOpType
AX = mybir.AxisListType

E = 2048
T = 8192
H = 16
D = 128
HD = 5632
Bn = 2
EC = E // 128
FBn = 16
HBn = 44
QS = 512
NQS = T // QS
SLOTS = 4
EXT = (16, 32, 48, 64)
PASSES = ((0, 3), (1, 2))
ISQD = float(1.0 / np.sqrt(np.float32(128.0)))
LN_EPS = 1e-5
QZ_EPS = 1e-8
QTILES = [[j, 7 - j, 8 + j, 15 - j] for j in range(4)]

ENGS = ("pe", "act", "dve", "pool", "sp")
PHASE_ORDER = ("qz", "kv", "q", "att", "w")


class Prog:
    def __init__(self):
        self.th = {k: [] for k in ENGS}
        self.cnt = defaultdict(int)
        self._waited = defaultdict(int)
        self.dma_regs = set()

    def w(self, eng, sem, val):
        if val is None or val <= 0:
            return
        assert val <= self.cnt[sem.name], \
            f"wait ahead of emission: {sem.name} {val} > {self.cnt[sem.name]}"
        key = (eng, sem.name)
        if self._waited[key] >= val:
            return
        self._waited[key] = val
        self.th[eng].append(lambda e, s=sem, v=val: e.wait_ge(s, v))

    def wt(self, eng, tup):
        if tup is not None:
            self.w(eng, tup[0], tup[1])

    def o(self, eng, fn, inc=None, n=1):
        if inc is not None:
            self.cnt[inc.name] += n
            self.th[eng].append(lambda e, f=fn, s=inc, m=n: f(e).then_inc(s, m))
            return self.cnt[inc.name]
        self.th[eng].append(lambda e, f=fn: f(e))
        return None

    def dma(self, eng, sem, out, in_):
        self.dma_regs.add((eng, sem))
        return self.o(eng, lambda e, o=out, i=in_: e.dma_start(out=o, in_=i), inc=sem, n=16)


class LRing:

    def __init__(self, P, nc, stk, name, n, shape, dt):
        self.P = P
        self.n = n
        self.tiles = [stk.enter_context(nc.sbuf_tensor(f"{name}{i}", shape, dt)) for i in range(n)]
        self.sems = [stk.enter_context(nc.semaphore(f"{name}S{i}")) for i in range(n)]
        self.mark = [None] * n

    def load(self, eng, i, in_, view=None):
        k = i % self.n
        if self.mark[k] is not None:
            self.P.wt(eng, self.mark[k])
        t = self.tiles[k]
        out = view(t) if view else t[:]
        c = self.P.dma(eng, self.sems[k], out, in_)
        return t, (self.sems[k], c)

    def rel(self, i, sem, cnt):
        if cnt:
            self.mark[i % self.n] = (sem, cnt)


class SRing:

    def __init__(self, P, nc, stk, name, n, shape, dt):
        self.P = P
        self.n = n
        self.tiles = [stk.enter_context(nc.sbuf_tensor(f"{name}{i}", shape, dt)) for i in range(n)]
        self.sems = [stk.enter_context(nc.semaphore(f"{name}S{i}")) for i in range(n)]
        self.mark = [None] * n

    def acq(self, eng, i):
        k = i % self.n
        if self.mark[k] is not None:
            self.P.wt(eng, self.mark[k])
        return self.tiles[k]

    def store(self, eng, i, out, view=None):
        k = i % self.n
        t = self.tiles[k]
        in_ = view(t) if view else t[:]
        c = self.P.dma(eng, self.sems[k], out, in_)
        self.mark[k] = (self.sems[k], c)
        return (self.sems[k], c)


def build_program(taps=(), upto="w"):
    nc = bass.Bass("TRN2", target_bir_lowering=False)
    taps = set(taps)
    nphase = PHASE_ORDER.index(upto)

    def have(ph):
        return PHASE_ORDER.index(ph) <= nphase

    def dram(name, shape, dt):
        kind = "ExternalOutput" if name in taps else "Internal"
        return nc.dram_tensor(name, shape, dt, kind=kind).ap()

    xt = nc.dram_tensor("xt", [E, T], F32, kind="ExternalInput").ap()
    xq = nc.dram_tensor("xq", [E, SLOTS * QS], F32, kind="ExternalInput").ap()
    wqT = nc.dram_tensor("wqT", [E, E], F32, kind="ExternalInput").ap()
    wkT = nc.dram_tensor("wkT", [E, E], F32, kind="ExternalInput").ap()
    wvT = nc.dram_tensor("wvT", [E, E], F32, kind="ExternalInput").ap()
    woT = nc.dram_tensor("woT", [E, E], F32, kind="ExternalInput").ap()
    w1T = nc.dram_tensor("w1T", [E, HD], F32, kind="ExternalInput").ap()
    w2T = nc.dram_tensor("w2T", [E, HD], F32, kind="ExternalInput").ap()
    w3T = nc.dram_tensor("w3T", [HD, E], F32, kind="ExternalInput").ap()
    cosKF = nc.dram_tensor("cosKF", [128, T], F32, kind="ExternalInput").ap()
    sinKF = nc.dram_tensor("sinKF", [128, T], F32, kind="ExternalInput").ap()
    cosQFd = nc.dram_tensor("cosQF", [128, SLOTS * QS], F32, kind="ExternalInput").ap()
    sinQFd = nc.dram_tensor("sinQF", [128, SLOTS * QS], F32, kind="ExternalInput").ap()
    maskm = nc.dram_tensor("maskm", [SLOTS, 128, 2432], BF16, kind="ExternalInput").ap()

    wB = {nm: dram(nm + "B", [FBn, 128, EC, 128], BF16) for nm in ("wq", "wk", "wo")}
    wvP = dram("wvP", [E, E], BF16)
    q1B = dram("q1B", [HBn, 128, EC, 128], BF16)
    q2B = dram("q2B", [HBn, 128, EC, 128], BF16)
    q3B = dram("q3B", [FBn, 128, HBn, 128], BF16)
    KbT = dram("KbT", [E, T], BF16)
    Vb = dram("Vb", [T, E], BF16)
    outTd = dram("outTd", [E, SLOTS * QS], BF16)
    if "qtap" in taps:
        qtap = nc.dram_tensor("qtap", [E, SLOTS * QS], BF16, kind="ExternalOutput").ap()
    if "thrtap" in taps:
        thrtap = nc.dram_tensor("thrtap", [3, 128, 2], F32, kind="ExternalOutput").ap()
    yt = nc.dram_tensor("yt", [E, SLOTS * QS], F32, kind="ExternalOutput").ap()

    P = Prog()

    with ExitStack() as top:
        En = top.enter_context

        s_pe = En(nc.semaphore("s_pe"))
        s_act = En(nc.semaphore("s_act"))
        s_dve = En(nc.semaphore("s_dve"))
        s_pool = En(nc.semaphore("s_pool"))
        s_bar = En(nc.semaphore("s_bar"))
        s_once = En(nc.semaphore("s_once"))
        s_out = En(nc.semaphore("s_out"))

        def once(eng, out, in_):
            sem = s_once if eng == "sp" else s_out
            return sem, P.dma(eng, sem, out, in_)

        def barrier():
            for (eng, sem) in sorted(P.dma_regs, key=lambda x: (x[0], x[1].name)):
                P.w(eng, sem, P.cnt[sem.name])
            for eng in ENGS:
                P.cnt[s_bar.name] += 1
                P.th[eng].append(lambda e: e.sem_inc(s_bar, 1))
            tgt = P.cnt[s_bar.name]
            for eng in ENGS:
                P.w(eng, s_bar, tgt)

        ones_bf = En(nc.sbuf_tensor("ones_bf", [128, 1], BF16))
        ones1f = En(nc.sbuf_tensor("ones1f", [1, 128], F32))
        onecol = En(nc.sbuf_tensor("onecol", [128, 1], F32))
        epsP = En(nc.sbuf_tensor("epsP", [1, 1], F32))
        statrow = En(nc.sbuf_tensor("statrow", [1, 6 * QS], F32))
        thr = [En(nc.sbuf_tensor(f"thr{i}", [128, 2], F32)) for i in range(3)]
        pb = [En(nc.psum_tensor(f"pb{i}", [128, 512], F32)) for i in range(8)]

        P.o("dve", lambda e: e.memset(ones_bf[:], 1.0))
        P.o("dve", lambda e: e.memset(ones1f[:], 1.0))
        P.o("dve", lambda e: e.memset(onecol[:], 1.0))
        c_init = P.o("dve", lambda e: e.memset(epsP[:], LN_EPS), inc=s_dve)

        def ln_fixup_and_bcast(W):
            mu = statrow[0:1, 0 * QS:0 * QS + W]
            m2 = statrow[0:1, 1 * QS:1 * QS + W]
            msq = statrow[0:1, 2 * QS:2 * QS + W]
            var = statrow[0:1, 3 * QS:3 * QS + W]
            rstd = statrow[0:1, 4 * QS:4 * QS + W]
            nmr = statrow[0:1, 5 * QS:5 * QS + W]
            P.o("dve", lambda e: e.tensor_scalar_mul(mu, pb[4][0:1, :W], 1.0 / E), inc=s_dve)
            e1 = P.o("dve", lambda e: e.tensor_scalar_mul(m2, pb[5][0:1, :W], 1.0 / E), inc=s_dve)
            P.w("dve", s_dve, e1)
            e2 = P.o("dve", lambda e: e.tensor_mul(msq, mu, mu), inc=s_dve)
            P.w("dve", s_dve, e2)
            dc = P.o("dve", lambda e: e.tensor_sub(var, m2, msq), inc=s_dve)
            P.w("act", s_dve, dc)
            P.w("act", s_dve, c_init)
            ac = P.o("act", lambda e: e.activation(var, var, AF.Sqrt, bias=epsP[:], scale=1.0),
                     inc=s_act)
            P.w("dve", s_act, ac)
            e3 = P.o("dve", lambda e: e.reciprocal(rstd, var), inc=s_dve)
            P.w("dve", s_dve, e3)
            dc = P.o("dve", lambda e: e.scalar_tensor_tensor(nmr, mu, -1.0, rstd, ALU.mult, ALU.mult),
                     inc=s_dve)
            P.w("pe", s_dve, dc)
            P.o("pe", lambda e: e.matmul(pb[6][:, :W], ones1f[0:1, :], rstd, start=True, stop=True))
            pc = P.o("pe", lambda e: e.matmul(pb[7][:, :W], ones1f[0:1, :], nmr, start=True, stop=True),
                     inc=s_pe)
            return pc

        cast_done = {}
        qz_stack = ExitStack()
        ldring = LRing(P, nc, qz_stack, "qzld", 2, [128, 2048], F32)
        stg = SRing(P, nc, qz_stack, "qzst", 2, [128, 2048], BF16)
        acc = [qz_stack.enter_context(nc.sbuf_tensor(f"qzacc{i}", [128, 48], F32)) for i in range(3)]
        accs = [qz_stack.enter_context(nc.sbuf_tensor(f"qzaccs{i}", [128, 1], F32)) for i in range(3)]
        crow = qz_stack.enter_context(nc.sbuf_tensor("qzcrow", [1, 1], F32))

        it = 0
        for nm, wsrc in (("wq", wqT), ("wk", wkT), ("wv", wvT), ("wo", woT)):
            for ecc in range(EC):
                buf, lw_ = ldring.load("sp", it, wsrc[ecc * 128:(ecc + 1) * 128, :])
                P.wt("dve", lw_)
                sb = stg.acq("dve", it)
                dc = P.o("dve", lambda e, o=sb, i=buf: e.tensor_copy(o[:], i[:]), inc=s_dve)
                ldring.rel(it, s_dve, dc)
                P.w("act", s_dve, dc)
                if nm == "wv":
                    stg.store("act", it, wvP[ecc * 128:(ecc + 1) * 128, :])
                else:
                    stg.store("act", it, wB[nm][:, :, ecc, :].rearrange("b p f -> p b f"),
                              view=lambda t: t[:].rearrange("p (b f) -> p b f", f=128))
                it += 1
            cast_done[nm] = [(s, P.cnt[s.name]) for s in stg.sems]

        WCHUNKS = []
        for wi, (src_, dst_, rows, cols) in enumerate(((w1T, q1B, E, HD), (w2T, q2B, E, HD),
                                                       (w3T, q3B, HD, E))):
            for r0 in range(0, rows, 128):
                for c0 in range(0, cols, 2048):
                    WCHUNKS.append((wi, src_, dst_, r0, c0, min(2048, cols - c0)))
        for i in range(3):
            P.o("dve", lambda e, a=acc[i]: e.memset(a[:], 0.0))
        pieces_seen = [0, 0, 0]
        gamma_cnt = [None, None, None]
        for (wi, src_, dst_, r0, c0, wdt) in WCHUNKS:
            buf, lw_ = ldring.load("sp", it, src_[r0:r0 + 128, c0:c0 + wdt],
                                   view=lambda t, w=wdt: t[:, :w])
            P.wt("dve", lw_)
            kcol = pieces_seen[wi]
            pieces_seen[wi] += 1
            dc = P.o("dve", lambda e, b=buf, w=wdt, a=acc[wi], k=kcol: e.reduce_sum(
                a[:, k:k + 1], b[:, :w], axis=AX.X, apply_absolute_value=True), inc=s_dve)
            ldring.rel(it, s_dve, dc)
            gamma_cnt[wi] = dc
            it += 1
        thr_done = [None, None, None]
        for wi in range(3):
            P.w("dve", s_dve, gamma_cnt[wi])
            gs_ = P.o("dve", lambda e, a=acc[wi], s_=accs[wi], n_=pieces_seen[wi]: e.reduce_sum(
                s_[:], a[:, :n_], axis=AX.X), inc=s_dve)
            P.w("pe", s_dve, gs_)
            pc = P.o("pe", lambda e, a=accs[wi]: e.matmul(pb[0][0:1, 0:1], a[:], onecol[:],
                                                          start=True, stop=True), inc=s_pe)
            P.w("dve", s_pe, pc)
            numel = float(E * HD)
            dc = P.o("dve", lambda e, n_=numel: e.tensor_scalar(
                crow[0:1, 0:1], pb[0][0:1, 0:1], 0.5 / n_, 0.5 * QZ_EPS, ALU.mult, ALU.add),
                inc=s_dve)
            P.w("pe", s_dve, dc)
            pc = P.o("pe", lambda e: e.matmul(pb[1][:, 0:1], ones1f[0:1, :], crow[0:1, 0:1],
                                              start=True, stop=True), inc=s_pe)
            P.w("dve", s_pe, pc)
            ec0 = P.o("dve", lambda e, t=thr[wi]: e.tensor_copy(t[:, 0:1], pb[1][:, 0:1]), inc=s_dve)
            P.w("dve", s_dve, ec0)
            thr_done[wi] = P.o("dve", lambda e, t=thr[wi]: e.tensor_scalar_mul(t[:, 1:2], t[:, 0:1], -1.0),
                               inc=s_dve)
        qz_stack.close()
        barrier()
        if "thrtap" in taps:
            for wi in range(3):
                once("act", thrtap[wi], thr[wi][:])
            barrier()

        qz_state = {"i": 0}

        def emit_quant(nmax, quld, qsA, qsB, qst):
            for _ in range(nmax):
                i = qz_state["i"]
                if i >= len(WCHUNKS):
                    return
                (wi, src_, dst_, r0, c0, wdt) = WCHUNKS[i]
                buf, lw_ = quld.load("sp", i, src_[r0:r0 + 128, c0:c0 + wdt],
                                     view=lambda t, w=wdt: t[:, :w])
                P.wt("pool", lw_)
                P.w("pool", s_dve, thr_done[wi])
                if qz_state.get("prev_pc"):
                    P.w("pool", s_dve, qz_state["prev_pc"])
                P.o("pool", lambda e, b=buf, w=wdt, t=thr[wi]: e.tensor_scalar(
                    qsA[:, :w], b[:, :w], t[:, 0:1], -1.0, ALU.is_ge, ALU.mult), inc=s_pool)
                a2 = P.o("pool", lambda e, b=buf, w=wdt, t=thr[wi]: e.tensor_scalar(
                    qsB[:, :w], b[:, :w], t[:, 1:2], -1.0, ALU.is_le, ALU.mult), inc=s_pool)
                sb = qst.acq("dve", i)
                P.w("dve", s_pool, a2)
                pc = P.o("dve", lambda e, o=sb, w=wdt: e.tensor_sub(o[:, :w], qsA[:, :w], qsB[:, :w]),
                         inc=s_dve)
                quld.rel(i, s_pool, a2)
                qz_state["prev_pc"] = pc
                P.w("act", s_dve, pc)
                nb = wdt // 128
                gb0 = c0 // 128
                rcc = r0 // 128
                qst.store("act", i, dst_[gb0:gb0 + nb, :, rcc, :].rearrange("b p f -> p b f"),
                          view=lambda t, w=wdt: t[:, :w].rearrange("p (b f) -> p b f", f=128))
                qz_state["i"] = i + 1

        qst_rings = []
        if upto == "qz":
            qq_stack = ExitStack()
            quld = LRing(P, nc, qq_stack, "quld", 2, [128, 2048], F32)
            qsA = qq_stack.enter_context(nc.sbuf_tensor("qsA", [128, 2048], F32))
            qsB = qq_stack.enter_context(nc.sbuf_tensor("qsB", [128, 2048], F32))
            qst = SRing(P, nc, qq_stack, "qust", 2, [128, 2048], BF16)
            emit_quant(len(WCHUNKS), quld, qsA, qsB, qst)
            qq_stack.close()

        qt_stack = ExitStack()
        if have("q"):
            QT = [qt_stack.enter_context(nc.sbuf_tensor(f"QT{h}", [128, SLOTS * QS], BF16)) for h in range(H)]

        if have("kv"):
            kv_stack = ExitStack()
            xbf = [kv_stack.enter_context(nc.sbuf_tensor(f"xbf{i}", [128, QS], BF16)) for i in range(EC)]
            kv_ld = LRing(P, nc, kv_stack, "kvld", 2, [128, QS], F32)
            kv_sqt = [kv_stack.enter_context(nc.sbuf_tensor(f"kvsq{i}", [128, QS], BF16)) for i in range(2)]
            kv_scr = kv_stack.enter_context(nc.sbuf_tensor("kvscr", [128, QS], F32))
            wvsp = [kv_stack.enter_context(nc.sbuf_tensor(f"kvwv{i}", [128, E], BF16)) for i in range(EC)]
            wk_r = LRing(P, nc, kv_stack, "kvwk", 2, [128, E], BF16)
            kst = SRing(P, nc, kv_stack, "kvks", 2, [128, QS], BF16)
            vst = SRing(P, nc, kv_stack, "kvvs", 2, [128, E], BF16)
            ctab = LRing(P, nc, kv_stack, "kvct", 2, [128, QS], F32)
            stab = LRing(P, nc, kv_stack, "kvst", 2, [128, QS], F32)
            kv_t1 = kv_stack.enter_context(nc.sbuf_tensor("kvt1", [128, QS], F32))
            kv_t2 = kv_stack.enter_context(nc.sbuf_tensor("kvt2", [128, QS], F32))
            kv_sqm = [None, None]
            kbmark = [None, None]

            def ln_block(tiles, sq_tiles, sqm, load_src, ldr, scr, it0):
                it_ = it0
                stat_stop = None
                for ecc in range(EC):
                    buf, lw_ = ldr.load("sp", it_, load_src(ecc))
                    P.wt("dve", lw_)
                    dc = P.o("dve", lambda e, o=tiles[ecc], i=buf: e.tensor_copy(o[:], i[:]), inc=s_dve)
                    ldr.rel(it_, s_dve, dc)
                    k2 = it_ % 2
                    if sqm[k2] is not None:
                        P.wt("act", sqm[k2])
                    sqb = sq_tiles[k2]
                    P.w("act", s_dve, dc)
                    ac = P.o("act", lambda e, o=sqb, i=tiles[ecc]: e.activation(o[:], i[:], AF.Square),
                             inc=s_act)
                    P.w("pe", s_act, ac)
                    P.o("pe", lambda e, i=tiles[ecc], st=(ecc == 0), sp_=(ecc == EC - 1): e.matmul(
                        pb[4][0:1, :], ones_bf[:], i[:], start=st, stop=sp_))
                    pc = P.o("pe", lambda e, i=sqb, st=(ecc == 0), sp_=(ecc == EC - 1): e.matmul(
                        pb[5][0:1, :], ones_bf[:], i[:], start=st, stop=sp_), inc=s_pe)
                    sqm[k2] = (s_pe, pc)
                    stat_stop = pc
                    it_ += 1
                P.w("dve", s_pe, stat_stop)
                bc = ln_fixup_and_bcast(QS)
                P.w("dve", s_pe, bc)
                ndone = None
                for ecc in range(EC):
                    em = P.o("dve", lambda e, x=tiles[ecc]: e.tensor_mul(scr[:], x[:], pb[6][:, :]),
                             inc=s_dve)
                    P.w("dve", s_dve, em)
                    ndone = P.o("dve", lambda e, x=tiles[ecc]: e.tensor_add(x[:], scr[:], pb[7][:, :]),
                                inc=s_dve)
                return ndone, it_

            for tup in cast_done["wv"]:
                P.wt("sp", tup)
            wv_lc = None
            for ecc in range(EC):
                wv_lc = once("sp", wvsp[ecc][:], wvP[ecc * 128:(ecc + 1) * 128, :])

            it = 0
            for qs_i in range(NQS):
                gt0 = qs_i * QS
                ndone, it = ln_block(xbf, kv_sqt, kv_sqm,
                                     lambda ecc, g=gt0: xt[ecc * 128:(ecc + 1) * 128, g:g + QS],
                                     kv_ld, kv_scr, it)
                P.w("pe", s_dve, ndone)
                P.wt("pe", wv_lc)
                for tt in range(4):
                    vmm = None
                    for ecc in range(EC):
                        for f4 in range(4):
                            vmm = P.o("pe", lambda e, b=pb[2 + f4], w=wvsp[ecc], c=ecc, t_=tt, f_=f4,
                                      st=(ecc == 0), sp_=(ecc == EC - 1): e.matmul(
                                b[:, :], xbf[c][:, t_ * 128:(t_ + 1) * 128], w[:, f_ * 512:(f_ + 1) * 512],
                                start=st, stop=sp_),
                                inc=(s_pe if (ecc == EC - 1 and f4 == 3) else None))
                    vs = vst.acq("act", qs_i * 4 + tt)
                    P.w("act", s_pe, vmm)
                    vdone = None
                    for f4 in range(4):
                        vdone = P.o("act", lambda e, o=vs, b=pb[2 + f4], f_=f4: e.copy(
                            o[:, f_ * 512:(f_ + 1) * 512], b[:, :]), inc=s_act)
                    P.w("act", s_act, vdone)
                    vst.store("act", qs_i * 4 + tt, Vb[gt0 + tt * 128:gt0 + (tt + 1) * 128, :])
                    P.w("pe", s_act, vdone)
                cb, cl = ctab.load("sp", qs_i, cosKF[:, gt0:gt0 + QS])
                sb_, sl = stab.load("sp", qs_i, sinKF[:, gt0:gt0 + QS])
                for tup in cast_done["wk"]:
                    P.wt("sp", tup)
                kdone = None
                for fb in range(FBn):
                    wkb, lk = wk_r.load("sp", qs_i * FBn + fb,
                                        wB["wk"][fb],
                                        view=lambda t: t[:].rearrange("p (c f) -> p c f", f=128))
                    bk = pb[0 + (fb % 2)]
                    P.wt("pe", kbmark[fb % 2])
                    P.wt("pe", lk)
                    kmm = None
                    for ecc in range(EC):
                        kmm = P.o("pe", lambda e, b=bk, w=wkb, c=ecc, st=(ecc == 0), sp_=(ecc == EC - 1): e.matmul(
                            b[:, :], w[:, c * 128:(c + 1) * 128], xbf[c][:, :], start=st, stop=sp_),
                            inc=(s_pe if ecc == EC - 1 else None))
                    wk_r.rel(qs_i * FBn + fb, s_pe, kmm)
                    ks = kst.acq("dve", qs_i * FBn + fb)
                    P.w("dve", s_pe, kmm)
                    P.wt("dve", cl)
                    P.wt("dve", sl)
                    P.o("dve", lambda e, b=bk, c=cb: e.tensor_mul(kv_t1[:], b[:, :], c[:]), inc=s_dve)
                    P.o("dve", lambda e, b=bk, s=sb_: e.tensor_mul(kv_t2[0:64, :], b[64:128, :], s[64:128, :]),
                        inc=s_dve)
                    em = P.o("dve", lambda e, b=bk, s=sb_: e.tensor_mul(kv_t2[64:128, :], b[0:64, :], s[0:64, :]),
                             inc=s_dve)
                    P.w("dve", s_dve, em)
                    kdone = P.o("dve", lambda e, o=ks: e.tensor_add(o[:], kv_t1[:], kv_t2[:]), inc=s_dve)
                    kbmark[fb % 2] = (s_dve, kdone)
                    P.w("act", s_dve, kdone)
                    kst.store("act", qs_i * FBn + fb, KbT[fb * 128:(fb + 1) * 128, gt0:gt0 + QS])
                ctab.rel(qs_i, s_dve, kdone)
                stab.rel(qs_i, s_dve, kdone)
            kv_stack.close()
            barrier()

        if have("q"):
            q_stack = ExitStack()
            xqb = [q_stack.enter_context(nc.sbuf_tensor(f"xqb{i}", [128, QS], BF16)) for i in range(EC)]
            q_ld = LRing(P, nc, q_stack, "qld", 2, [128, QS], F32)
            q_sqt = [q_stack.enter_context(nc.sbuf_tensor(f"qsq{i}", [128, QS], BF16)) for i in range(2)]
            q_scr = q_stack.enter_context(nc.sbuf_tensor("qscr", [128, QS], F32))
            wq_r = LRing(P, nc, q_stack, "qwq", 2, [128, E], BF16)
            qtabC = q_stack.enter_context(nc.sbuf_tensor("qtabC", [128, SLOTS * QS], F32))
            qtabS = q_stack.enter_context(nc.sbuf_tensor("qtabS", [128, SLOTS * QS], F32))
            q_t1 = q_stack.enter_context(nc.sbuf_tensor("qt1", [128, QS], F32))
            q_t2 = q_stack.enter_context(nc.sbuf_tensor("qt2", [128, QS], F32))
            q_sqm = [None, None]
            qbmark = [None, None]

            def ln_block_q(tiles, sq_tiles, sqm, load_src, ldr, scr, it0):
                it_ = it0
                stat_stop = None
                for ecc in range(EC):
                    buf, lw_ = ldr.load("sp", it_, load_src(ecc))
                    P.wt("dve", lw_)
                    dc = P.o("dve", lambda e, o=tiles[ecc], i=buf: e.tensor_copy(o[:], i[:]), inc=s_dve)
                    ldr.rel(it_, s_dve, dc)
                    k2 = it_ % 2
                    if sqm[k2] is not None:
                        P.wt("act", sqm[k2])
                    sqb = sq_tiles[k2]
                    P.w("act", s_dve, dc)
                    ac = P.o("act", lambda e, o=sqb, i=tiles[ecc]: e.activation(o[:], i[:], AF.Square),
                             inc=s_act)
                    P.w("pe", s_act, ac)
                    P.o("pe", lambda e, i=tiles[ecc], st=(ecc == 0), sp_=(ecc == EC - 1): e.matmul(
                        pb[4][0:1, :], ones_bf[:], i[:], start=st, stop=sp_))
                    pc = P.o("pe", lambda e, i=sqb, st=(ecc == 0), sp_=(ecc == EC - 1): e.matmul(
                        pb[5][0:1, :], ones_bf[:], i[:], start=st, stop=sp_), inc=s_pe)
                    sqm[k2] = (s_pe, pc)
                    stat_stop = pc
                    it_ += 1
                P.w("dve", s_pe, stat_stop)
                bc = ln_fixup_and_bcast(QS)
                P.w("dve", s_pe, bc)
                ndone = None
                for ecc in range(EC):
                    em = P.o("dve", lambda e, x=tiles[ecc]: e.tensor_mul(scr[:], x[:], pb[6][:, :]),
                             inc=s_dve)
                    P.w("dve", s_dve, em)
                    ndone = P.o("dve", lambda e, x=tiles[ecc]: e.tensor_add(x[:], scr[:], pb[7][:, :]),
                                inc=s_dve)
                return ndone, it_

            tc0 = once("sp", qtabC[:], cosQFd)
            tc1 = once("sp", qtabS[:], sinQFd)

            it = 0
            for s in range(SLOTS):
                q0 = s * QS
                ndone, it = ln_block_q(xqb, q_sqt, q_sqm,
                                       lambda ecc, q=q0: xq[ecc * 128:(ecc + 1) * 128, q:q + QS],
                                       q_ld, q_scr, it)
                P.w("pe", s_dve, ndone)
                for tup in cast_done["wq"]:
                    P.wt("sp", tup)
                P.wt("dve", tc0)
                P.wt("dve", tc1)
                for fb in range(FBn):
                    wqb_t, lq = wq_r.load("sp", s * FBn + fb,
                                          wB["wq"][fb],
                                          view=lambda t: t[:].rearrange("p (c f) -> p c f", f=128))
                    bq = pb[0 + (fb % 2)]
                    P.wt("pe", qbmark[fb % 2])
                    P.wt("pe", lq)
                    qmm = None
                    for ecc in range(EC):
                        qmm = P.o("pe", lambda e, b=bq, w=wqb_t, c=ecc, st=(ecc == 0), sp_=(ecc == EC - 1): e.matmul(
                            b[:, :], w[:, c * 128:(c + 1) * 128], xqb[c][:, :], start=st, stop=sp_),
                            inc=(s_pe if ecc == EC - 1 else None))
                    wq_r.rel(s * FBn + fb, s_pe, qmm)
                    P.w("dve", s_pe, qmm)
                    P.o("dve", lambda e, b=bq, q_=q0: e.tensor_mul(q_t1[:], b[:, :], qtabC[:, q_:q_ + QS]),
                        inc=s_dve)
                    P.o("dve", lambda e, b=bq, q_=q0: e.tensor_mul(q_t2[0:64, :], b[64:128, :], qtabS[64:128, q_:q_ + QS]),
                        inc=s_dve)
                    em = P.o("dve", lambda e, b=bq, q_=q0: e.tensor_mul(q_t2[64:128, :], b[0:64, :], qtabS[0:64, q_:q_ + QS]),
                             inc=s_dve)
                    P.w("dve", s_dve, em)
                    qdone = P.o("dve", lambda e, h=fb, q_=q0: e.tensor_add(QT[h][:, q_:q_ + QS], q_t1[:], q_t2[:]),
                                inc=s_dve)
                    qbmark[fb % 2] = (s_dve, qdone)
            q_stack.close()
            barrier()
            if "qtap" in taps:
                for h in range(H):
                    once("act", qtap[h * 128:(h + 1) * 128, :], QT[h][:])
                barrier()

        if have("att"):
            att_stack = ExitStack()
            kt_r = LRing(P, nc, att_stack, "akt", 2, [128, 1024], BF16)
            vt_r = LRing(P, nc, att_stack, "avt", 2, [128, 8 * 128], BF16)
            ptile = [att_stack.enter_context(nc.sbuf_tensor(f"ap{i}", [128, QS], BF16)) for i in range(6)]
            pmark = [None] * 6
            masks = [att_stack.enter_context(nc.sbuf_tensor(f"amask{i}", [128, 2432], BF16)) for i in range(SLOTS)]
            rrow = att_stack.enter_context(nc.sbuf_tensor("arr", [1, 2 * QS], F32))
            rbt = att_stack.enter_context(nc.sbuf_tensor("arb", [128, QS], F32))
            ost = SRing(P, nc, att_stack, "aost", 2, [128, QS], BF16)
            quld = LRing(P, nc, att_stack, "quld", 2, [128, 2048], F32)
            qsA = att_stack.enter_context(nc.sbuf_tensor("qsA", [128, 2048], F32))
            qsB = att_stack.enter_context(nc.sbuf_tensor("qsB", [128, 2048], F32))
            qst = SRing(P, nc, att_stack, "qust", 2, [128, 2048], BF16)
            qst_rings = [qst]

            mload = None
            for s in range(SLOTS):
                mload = once("sp", masks[s][:], maskm[s])

            sbw = [None, None]
            pvfree = [None, None]
            rfree = [None, None]
            rrmark = [None, None]
            pb6mark = [None]
            oi = [0]
            ci = 0

            def finish(h, slot, ridx, pvstop):
                rbank = pb[2 + ridx]
                vbank = pb[4 + ridx]
                P.w("dve", s_pe, pvstop)
                if rrmark[ridx] is not None:
                    P.w("dve", s_pe, rrmark[ridx])
                rr = rrow[0:1, ridx * QS:(ridx + 1) * QS]
                dc = P.o("dve", lambda e, b=rbank, r=rr: e.reciprocal(r, b[0:1, :]), inc=s_dve)
                rfree[ridx] = dc
                P.w("pe", s_dve, dc)
                if pb6mark[0] is not None:
                    P.w("pe", s_dve, pb6mark[0])
                bc = P.o("pe", lambda e, r=rr: e.matmul(pb[6][:, :], ones1f[0:1, :], r,
                                                        start=True, stop=True), inc=s_pe)
                rrmark[ridx] = bc
                P.w("dve", s_pe, bc)
                em = P.o("dve", lambda e: e.tensor_copy(rbt[:], pb[6][:, :]), inc=s_dve)
                P.w("dve", s_dve, em)
                osb = ost.acq("dve", oi[0])
                dc2 = P.o("dve", lambda e, o=osb, b=vbank: e.tensor_mul(o[:], b[:, :], rbt[:]), inc=s_dve)
                pb6mark[0] = dc2
                pvfree[ridx] = dc2
                P.w("act", s_dve, dc2)
                ost.store("act", oi[0], outTd[h * 128:(h + 1) * 128, slot * QS:(slot + 1) * QS])
                oi[0] += 1

            P.wt("dve", mload)
            for h in range(H):
                for (sA, sB) in PASSES:
                    emax = max(EXT[sA], EXT[sB])
                    nkblk = (emax + 7) // 8
                    pend = [None]
                    last_pv = {}
                    for kb in range(nkblk):
                        kt, lk = kt_r.load("sp", kb, KbT[h * 128:(h + 1) * 128, kb * 1024:(kb + 1) * 1024])
                        vt, lv = vt_r.load("sp", kb,
                                           Vb[kb * 1024:(kb + 1) * 1024, h * 128:(h + 1) * 128]
                                           .rearrange("(kc p) d -> p kc d", p=128),
                                           view=lambda t: t[:].rearrange("p (kc d) -> p kc d", d=128))
                        last_smm = None
                        for i8 in range(8):
                            kc = kb * 8 + i8
                            for slot, ridx in ((sA, 0), (sB, 1)):
                                if kc >= EXT[slot]:
                                    continue
                                sbank_i = ci % 2
                                sbank = pb[sbank_i]
                                pslot_i = ci % 6
                                pslot = ptile[pslot_i]
                                if pmark[pslot_i] is not None:
                                    P.wt("act", pmark[pslot_i])
                                if sbw[sbank_i] is not None:
                                    P.w("pe", s_act, sbw[sbank_i])
                                P.wt("pe", lk)
                                smm = P.o("pe", lambda e, b=sbank, k=kt, i=i8, s_=slot, h_=h: e.matmul(
                                    b[:, :], k[:, i * 128:(i + 1) * 128], QT[h_][:, s_ * QS:(s_ + 1) * QS],
                                    start=True, stop=True), inc=s_pe)
                                last_smm = smm
                                if pend[0] is not None:
                                    pend[0]()
                                    pend[0] = None
                                P.w("act", s_pe, smm)
                                ec_ = P.o("act", lambda e, o=pslot, b=sbank: e.activation(
                                    o[:], b[:, :], AF.Exp, bias=0.0, scale=ISQD), inc=s_act)
                                sbw[sbank_i] = ec_
                                gate = (s_act, ec_)
                                if 16 * slot <= kc < 16 * slot + 16:
                                    j0 = 128 * (16 * slot + 15 - kc)
                                    P.w("dve", s_act, ec_)
                                    mc = P.o("dve", lambda e, p_=pslot, m=masks[slot], j=j0: e.tensor_mul(
                                        p_[:], p_[:], m[:, j:j + QS]), inc=s_dve)
                                    gate = (s_dve, mc)

                                def mk_pend(slot=slot, ridx=ridx, kc=kc, pslot=pslot, pslot_i=pslot_i,
                                            gate=gate, vt=vt, i8=i8, lv=lv, h=h, kb=kb):
                                    def emit():
                                        P.w("pe", gate[0], gate[1])
                                        P.wt("pe", lv)
                                        rbank = pb[2 + ridx]
                                        vbank = pb[4 + ridx]
                                        if kc == 0:
                                            if rfree[ridx] is not None:
                                                P.w("pe", s_dve, rfree[ridx])
                                            if pvfree[ridx] is not None:
                                                P.w("pe", s_dve, pvfree[ridx])
                                        P.o("pe", lambda e, b=rbank, p_=pslot, st=(kc == 0),
                                            sp_=(kc == EXT[slot] - 1): e.matmul(
                                            b[0:1, :], ones_bf[:], p_[:], start=st, stop=sp_))
                                        pc = P.o("pe", lambda e, b=vbank, v=vt, i=i8, p_=pslot,
                                                 st=(kc == 0), sp_=(kc == EXT[slot] - 1): e.matmul(
                                            b[:, :], v[:, i * 128:(i + 1) * 128], p_[:], start=st, stop=sp_),
                                            inc=s_pe)
                                        pmark[pslot_i] = (s_pe, pc)
                                        last_pv[kb] = pc
                                        if kc == EXT[slot] - 1:
                                            finish(h, slot, ridx, pc)
                                    return emit

                                pend[0] = mk_pend()
                                ci += 1
                        kt_r.rel(kb, s_pe, last_smm)
                        if kb - 1 in last_pv:
                            vt_r.rel(kb - 1, s_pe, last_pv[kb - 1])
                    if pend[0] is not None:
                        pend[0]()
                        pend[0] = None
                    for kb in range(nkblk):
                        if kb in last_pv:
                            vt_r.rel(kb, s_pe, last_pv[kb])
                    emit_quant(3, quld, qsA, qsB, qst)
            emit_quant(len(WCHUNKS), quld, qsA, qsB, qst)
            qst_done = [(s, P.cnt[s.name]) for s in qst.sems]
            att_stack.close()
            qt_stack.close()
            barrier()
        else:
            qt_stack.close()
            qst_done = []

        if have("w"):
            w_stack = ExitStack()
            s_ot = w_stack.enter_context(nc.semaphore("s_ot"))
            otile = [w_stack.enter_context(nc.sbuf_tensor(f"wot{i}", [128, QS], BF16)) for i in range(EC)]
            x1T = [w_stack.enter_context(nc.sbuf_tensor(f"wx1{i}", [128, QS], F32)) for i in range(FBn)]
            ln2 = [w_stack.enter_context(nc.sbuf_tensor(f"wln{i}", [128, QS], BF16)) for i in range(FBn)]
            hid = [w_stack.enter_context(nc.sbuf_tensor(f"whid{i}", [128, QS], BF16)) for i in range(HBn)]
            wcol = LRing(P, nc, w_stack, "wwc", 2, [128, E], BF16)
            q12c = LRing(P, nc, w_stack, "wq12", 4, [128, E], BF16)
            q3c = LRing(P, nc, w_stack, "wq3", 2, [128, HBn * 128], BF16)
            xres = LRing(P, nc, w_stack, "wxr", 2, [128, QS], F32)
            sig2 = [w_stack.enter_context(nc.sbuf_tensor(f"wsig{i}", [128, QS], BF16)) for i in range(2)]
            w_sqt = [w_stack.enter_context(nc.sbuf_tensor(f"wsq{i}", [128, QS], BF16)) for i in range(2)]
            w_sqm = [None, None]
            wbmark = [None, None]
            b1mark = [None, None]
            b2mark = [None, None]
            b3mark = [None, None]
            sigmark = [None, None]
            yst = SRing(P, nc, w_stack, "wyst", 2, [128, QS], F32)
            w_scr = w_stack.enter_context(nc.sbuf_tensor("wscr", [128, QS], F32))

            for tup in qst_done:
                P.wt("sp", tup)
            wo_last = [None]
            for s in range(SLOTS):
                q0 = s * QS
                if wo_last[0] is not None:
                    P.w("sp", s_pe, wo_last[0])
                for ecc in range(EC):
                    P.dma("sp", s_ot, otile[ecc][:], outTd[ecc * 128:(ecc + 1) * 128, q0:q0 + QS])
                olc = (s_ot, P.cnt[s_ot.name])
                for tup in cast_done["wo"]:
                    P.wt("sp", tup)
                for fb in range(FBn):
                    wc, lw = wcol.load("sp", s * FBn + fb,
                                       wB["wo"][fb],
                                       view=lambda t: t[:].rearrange("p (c f) -> p c f", f=128))
                    xr, lx = xres.load("sp", s * FBn + fb, xq[fb * 128:(fb + 1) * 128, q0:q0 + QS])
                    bw = pb[fb % 2]
                    P.wt("pe", wbmark[fb % 2])
                    P.wt("pe", lw)
                    P.wt("pe", olc)
                    wmm = None
                    for ecc in range(EC):
                        wmm = P.o("pe", lambda e, b=bw, w=wc, c=ecc, st=(ecc == 0), sp_=(ecc == EC - 1): e.matmul(
                            b[:, :], w[:, c * 128:(c + 1) * 128], otile[c][:], start=st, stop=sp_),
                            inc=(s_pe if ecc == EC - 1 else None))
                    wo_last[0] = wmm
                    wcol.rel(s * FBn + fb, s_pe, wmm)
                    P.w("dve", s_pe, wmm)
                    P.wt("dve", lx)
                    dc = P.o("dve", lambda e, o=x1T[fb], b=bw, x=xr: e.tensor_add(o[:], b[:, :], x[:]),
                             inc=s_dve)
                    xres.rel(s * FBn + fb, s_dve, dc)
                    wbmark[fb % 2] = (s_dve, dc)
                stat_stop = None
                P.w("act", s_dve, P.cnt[s_dve.name])
                for fb in range(FBn):
                    k2 = fb % 2
                    if w_sqm[k2] is not None:
                        P.wt("act", w_sqm[k2])
                    sqb = w_sqt[k2]
                    ac = P.o("act", lambda e, o=sqb, i=x1T[fb]: e.activation(o[:], i[:], AF.Square),
                             inc=s_act)
                    P.w("pe", s_act, ac)
                    P.o("pe", lambda e, i=x1T[fb], st=(fb == 0), sp_=(fb == FBn - 1): e.matmul(
                        pb[4][0:1, :], onecol[:], i[:], start=st, stop=sp_))
                    pc = P.o("pe", lambda e, i=sqb, st=(fb == 0), sp_=(fb == FBn - 1): e.matmul(
                        pb[5][0:1, :], ones_bf[:], i[:], start=st, stop=sp_), inc=s_pe)
                    w_sqm[k2] = (s_pe, pc)
                    stat_stop = pc
                P.w("dve", s_pe, stat_stop)
                bc = ln_fixup_and_bcast(QS)
                P.w("dve", s_pe, bc)
                ndone = None
                for fb in range(FBn):
                    em = P.o("dve", lambda e, x=x1T[fb]: e.tensor_mul(w_scr[:], x[:], pb[6][:, :]),
                             inc=s_dve)
                    P.w("dve", s_dve, em)
                    ndone = P.o("dve", lambda e, o=ln2[fb]: e.tensor_add(o[:], w_scr[:], pb[7][:, :]),
                                inc=s_dve)
                P.w("pe", s_dve, ndone)
                for hb in range(HBn):
                    c1, l1 = q12c.load("sp", 2 * (s * HBn + hb),
                                       q1B[hb],
                                       view=lambda t: t[:].rearrange("p (c f) -> p c f", f=128))
                    c2, l2 = q12c.load("sp", 2 * (s * HBn + hb) + 1,
                                       q2B[hb],
                                       view=lambda t: t[:].rearrange("p (c f) -> p c f", f=128))
                    b1 = pb[hb % 2]
                    b2 = pb[2 + hb % 2]
                    P.wt("pe", b1mark[hb % 2])
                    P.wt("pe", l1)
                    g1 = None
                    for ecc in range(EC):
                        g1 = P.o("pe", lambda e, b=b1, w=c1, c=ecc, st=(ecc == 0), sp_=(ecc == EC - 1): e.matmul(
                            b[:, :], w[:, c * 128:(c + 1) * 128], ln2[c][:], start=st, stop=sp_),
                            inc=(s_pe if ecc == EC - 1 else None))
                    P.wt("pe", b2mark[hb % 2])
                    P.wt("pe", l2)
                    g2 = None
                    for ecc in range(EC):
                        g2 = P.o("pe", lambda e, b=b2, w=c2, c=ecc, st=(ecc == 0), sp_=(ecc == EC - 1): e.matmul(
                            b[:, :], w[:, c * 128:(c + 1) * 128], ln2[c][:], start=st, stop=sp_),
                            inc=(s_pe if ecc == EC - 1 else None))
                    q12c.rel(2 * (s * HBn + hb), s_pe, g1)
                    q12c.rel(2 * (s * HBn + hb) + 1, s_pe, g2)
                    P.w("act", s_pe, g1)
                    sgt = sig2[hb % 2]
                    P.wt("act", sigmark[hb % 2])
                    ac = P.o("act", lambda e, b=b1, o=sgt: e.activation(o[:], b[:, :], AF.Sigmoid,
                                                                        bias=0.0, scale=-1.0), inc=s_act)
                    b1mark[hb % 2] = (s_act, ac)
                    P.w("dve", s_pe, g2)
                    P.w("dve", s_act, ac)
                    dc = P.o("dve", lambda e, o=hid[hb], b=b2, sg=sgt: e.scalar_tensor_tensor(
                        o[:], b[:, :], -1.0, sg[:], ALU.mult, ALU.mult), inc=s_dve)
                    b2mark[hb % 2] = (s_dve, dc)
                    sigmark[hb % 2] = (s_dve, dc)
                for fb in range(FBn):
                    c3, l3 = q3c.load("sp", s * FBn + fb,
                                      q3B[fb],
                                      view=lambda t: t[:].rearrange("p (c f) -> p c f", f=128))
                    b3 = pb[4 + fb % 2]
                    P.wt("pe", b3mark[fb % 2])
                    P.wt("pe", l3)
                    g3 = None
                    for hb in range(HBn):
                        g3 = P.o("pe", lambda e, b=b3, w=c3, c=hb, st=(hb == 0), sp_=(hb == HBn - 1): e.matmul(
                            b[:, :], w[:, c * 128:(c + 1) * 128], hid[c][:], start=st, stop=sp_),
                            inc=(s_pe if hb == HBn - 1 else None))
                    q3c.rel(s * FBn + fb, s_pe, g3)
                    ys = yst.acq("dve", s * FBn + fb)
                    P.w("dve", s_pe, g3)
                    dc = P.o("dve", lambda e, o=ys, b=b3, x=x1T[fb]: e.scalar_tensor_tensor(
                        o[:], b[:, :], -1.0, x[:], ALU.mult, ALU.add), inc=s_dve)
                    b3mark[fb % 2] = (s_dve, dc)
                    P.w("act", s_dve, dc)
                    yst.store("act", s * FBn + fb, yt[fb * 128:(fb + 1) * 128, q0:q0 + QS])
            w_stack.close()

        for (eng, sem) in sorted(P.dma_regs, key=lambda x: (x[0], x[1].name)):
            P.w("sp", sem, P.cnt[sem.name])

        with nc.Block() as block:
            @block.tensor
            def _(t):
                for f in P.th["pe"]:
                    f(t)

            @block.scalar
            def _(a):
                for f in P.th["act"]:
                    f(a)

            @block.vector
            def _(v):
                for f in P.th["dve"]:
                    f(v)

            @block.gpsimd
            def _(g):
                for f in P.th["pool"]:
                    f(g)

            @block.sync
            def _(sy):
                for f in P.th["sp"]:
                    f(sy)

    return nc



def _rope_tables():
    inv = (1.0 / (np.float32(1000000.0) ** (np.arange(0, D, 2, dtype=np.float32) / np.float32(D)))).astype(np.float32)
    t = np.arange(T, dtype=np.float32)
    f = t[:, None] * inv[None, :]
    return np.cos(f).astype(np.float32), np.sin(f).astype(np.float32)


_NC_CACHE = {}


def make_in_maps(x, wq, wk, wv, wo, w1, w2, w3):
    cosf, sinf = _rope_tables()
    cosKF = np.ascontiguousarray(np.concatenate([cosf.T, cosf.T], axis=0))
    sinKF = np.ascontiguousarray(np.concatenate([sinf.T, -sinf.T], axis=0))
    xT = [np.ascontiguousarray(x[b].T) for b in range(Bn)]
    wqT = np.ascontiguousarray(wq.T); wkT = np.ascontiguousarray(wk.T)
    wvT = np.ascontiguousarray(wv.T); woT = np.ascontiguousarray(wo.T)
    w1T = np.ascontiguousarray(w1.T); w2T = np.ascontiguousarray(w2.T)
    w3T = np.ascontiguousarray(w3.T)
    jcols = np.arange(2432)
    kps = np.arange(128)
    in_maps = []
    for c in range(8):
        b, j = c // 4, c % 4
        gs = QTILES[j]
        cols = np.concatenate([np.arange(g * QS, (g + 1) * QS) for g in gs])
        xqT = np.ascontiguousarray(xT[b][:, cols])
        cosQ = np.ascontiguousarray(cosKF[:, cols])
        sinQ = np.ascontiguousarray(np.concatenate([sinf.T[:, cols], -sinf.T[:, cols]], axis=0))
        mm = np.zeros((SLOTS, 128, 2432), dtype=np.float32)
        for s in range(SLOTS):
            C = 128 * (16 * s + 15) - QS * gs[s]
            mm[s] = (jcols[None, :] - kps[:, None] >= C).astype(np.float32)
        in_maps.append({
            "xt": xT[b], "xq": xqT,
            "wqT": wqT, "wkT": wkT, "wvT": wvT, "woT": woT,
            "w1T": w1T, "w2T": w2T, "w3T": w3T,
            "cosKF": cosKF, "sinKF": sinKF,
            "cosQF": cosQ, "sinQF": sinQ,
            "maskm": mm.astype(ml_dtypes.bfloat16),
        })
    return in_maps


def kernel(L0_attn_x, L0_attn_wq, L0_attn_wk, L0_attn_wv, L0_attn_wo,
           L0_ffn_w1, L0_ffn_w2, L0_ffn_w3):
    x = np.asarray(L0_attn_x)
    in_maps = make_in_maps(x, np.asarray(L0_attn_wq), np.asarray(L0_attn_wk),
                           np.asarray(L0_attn_wv), np.asarray(L0_attn_wo),
                           np.asarray(L0_ffn_w1), np.asarray(L0_ffn_w2), np.asarray(L0_ffn_w3))
    if "prog" not in _NC_CACHE:
        _NC_CACHE["prog"] = build_program()
    nc = _NC_CACHE["prog"]
    res = run_bass_kernel_spmd(nc, in_maps, core_ids=list(range(8)))
    out = np.empty((Bn, T, E), dtype=np.float32)
    for c in range(8):
        b, j = c // 4, c % 4
        ytc = res.results[c]["yt"]
        for s in range(SLOTS):
            g = QTILES[j][s]
            out[b, g * QS:(g + 1) * QS, :] = ytc[:, s * QS:(s + 1) * QS].T
    return out


# version 07 (final)
# speedup vs baseline: 1.9017x; 1.7260x over previous
import numpy as np
import ml_dtypes
from collections import defaultdict
from contextlib import ExitStack

import concourse.bass as bass
import concourse.mybir as mybir
from concourse.bass_utils import run_bass_kernel_spmd

F32 = mybir.dt.float32
BF16 = mybir.dt.bfloat16
AF = mybir.ActivationFunctionType
ALU = mybir.AluOpType
AX = mybir.AxisListType

E = 2048
T = 8192
H = 16
D = 128
HD = 5632
Bn = 2
EC = E // 128
FBn = 16
HBn = 44
QS = 512
NQS = T // QS
SLOTS = 4
EXT = (16, 32, 48, 64)
PASSES = ((0, 3), (1, 2))
ISQD = float(1.0 / np.sqrt(np.float32(128.0)))
LN_EPS = 1e-5
QZ_EPS = 1e-8
QTILES = [[j, 7 - j, 8 + j, 15 - j] for j in range(4)]

ENGS = ("pe", "act", "dve", "pool", "sp")
PHASE_ORDER = ("qz", "kv", "q", "att", "w")


class Prog:
    def __init__(self):
        self.th = {k: [] for k in ENGS}
        self.cnt = defaultdict(int)
        self._waited = defaultdict(int)
        self.dma_regs = set()

    def w(self, eng, sem, val):
        if val is None or val <= 0:
            return
        assert val <= self.cnt[sem.name], \
            f"wait ahead of emission: {sem.name} {val} > {self.cnt[sem.name]}"
        key = (eng, sem.name)
        if self._waited[key] >= val:
            return
        self._waited[key] = val
        self.th[eng].append(lambda e, s=sem, v=val: e.wait_ge(s, v))

    def wt(self, eng, tup):
        if tup is not None:
            self.w(eng, tup[0], tup[1])

    def o(self, eng, fn, inc=None, n=1):
        if inc is not None:
            self.cnt[inc.name] += n
            self.th[eng].append(lambda e, f=fn, s=inc, m=n: f(e).then_inc(s, m))
            return self.cnt[inc.name]
        self.th[eng].append(lambda e, f=fn: f(e))
        return None

    def dma(self, eng, sem, out, in_):
        self.dma_regs.add((eng, sem))
        return self.o(eng, lambda e, o=out, i=in_: e.dma_start(out=o, in_=i), inc=sem, n=16)


class LRing:

    def __init__(self, P, nc, stk, name, n, shape, dt):
        self.P = P
        self.n = n
        self.tiles = [stk.enter_context(nc.sbuf_tensor(f"{name}{i}", shape, dt)) for i in range(n)]
        self.sems = [stk.enter_context(nc.semaphore(f"{name}S{i}")) for i in range(n)]
        self.mark = [None] * n

    def load(self, eng, i, in_, view=None):
        k = i % self.n
        if self.mark[k] is not None:
            self.P.wt(eng, self.mark[k])
        t = self.tiles[k]
        out = view(t) if view else t[:]
        c = self.P.dma(eng, self.sems[k], out, in_)
        return t, (self.sems[k], c)

    def rel(self, i, sem, cnt):
        if cnt:
            self.mark[i % self.n] = (sem, cnt)


class SRing:

    def __init__(self, P, nc, stk, name, n, shape, dt):
        self.P = P
        self.n = n
        self.tiles = [stk.enter_context(nc.sbuf_tensor(f"{name}{i}", shape, dt)) for i in range(n)]
        self.sems = [stk.enter_context(nc.semaphore(f"{name}S{i}")) for i in range(n)]
        self.mark = [None] * n

    def acq(self, eng, i):
        k = i % self.n
        if self.mark[k] is not None:
            self.P.wt(eng, self.mark[k])
        return self.tiles[k]

    def store(self, eng, i, out, view=None):
        k = i % self.n
        t = self.tiles[k]
        in_ = view(t) if view else t[:]
        c = self.P.dma(eng, self.sems[k], out, in_)
        self.mark[k] = (self.sems[k], c)
        return (self.sems[k], c)


def build_program(taps=(), upto="w"):
    nc = bass.Bass("TRN2", target_bir_lowering=False)
    taps = set(taps)
    nphase = PHASE_ORDER.index(upto)

    def have(ph):
        return PHASE_ORDER.index(ph) <= nphase

    def dram(name, shape, dt):
        kind = "ExternalOutput" if name in taps else "Internal"
        return nc.dram_tensor(name, shape, dt, kind=kind).ap()

    xt = nc.dram_tensor("xt", [E, T], F32, kind="ExternalInput").ap()
    xq = nc.dram_tensor("xq", [E, SLOTS * QS], F32, kind="ExternalInput").ap()
    wqT = nc.dram_tensor("wqT", [E, E], F32, kind="ExternalInput").ap()
    wkT = nc.dram_tensor("wkT", [E, E], F32, kind="ExternalInput").ap()
    wvT = nc.dram_tensor("wvT", [E, E], F32, kind="ExternalInput").ap()
    woT = nc.dram_tensor("woT", [E, E], F32, kind="ExternalInput").ap()
    w1T = nc.dram_tensor("w1T", [E, HD], F32, kind="ExternalInput").ap()
    w2T = nc.dram_tensor("w2T", [E, HD], F32, kind="ExternalInput").ap()
    w3T = nc.dram_tensor("w3T", [HD, E], F32, kind="ExternalInput").ap()
    cosKF = nc.dram_tensor("cosKF", [128, T], F32, kind="ExternalInput").ap()
    sinKF = nc.dram_tensor("sinKF", [128, T], F32, kind="ExternalInput").ap()
    cosQFd = nc.dram_tensor("cosQF", [128, SLOTS * QS], F32, kind="ExternalInput").ap()
    sinQFd = nc.dram_tensor("sinQF", [128, SLOTS * QS], F32, kind="ExternalInput").ap()
    maskm = nc.dram_tensor("maskm", [SLOTS, 128, 2432], BF16, kind="ExternalInput").ap()

    wB = {nm: dram(nm + "B", [FBn, 128, EC, 128], BF16) for nm in ("wq", "wk", "wo")}
    wvP = dram("wvP", [E, E], BF16)
    q1B = dram("q1B", [HBn, 128, EC, 128], BF16)
    q2B = dram("q2B", [HBn, 128, EC, 128], BF16)
    q3B = dram("q3B", [FBn, 128, HBn, 128], BF16)
    KbT = dram("KbT", [E, T], BF16)
    Vb = dram("Vb", [T, E], BF16)
    outTd = dram("outTd", [E, SLOTS * QS], BF16)
    if "qtap" in taps:
        qtap = nc.dram_tensor("qtap", [E, SLOTS * QS], BF16, kind="ExternalOutput").ap()
    if "thrtap" in taps:
        thrtap = nc.dram_tensor("thrtap", [3, 128, 2], F32, kind="ExternalOutput").ap()
    yt = nc.dram_tensor("yt", [E, SLOTS * QS], F32, kind="ExternalOutput").ap()

    P = Prog()

    with ExitStack() as top:
        En = top.enter_context

        s_pe = En(nc.semaphore("s_pe"))
        s_act = En(nc.semaphore("s_act"))
        s_dve = En(nc.semaphore("s_dve"))
        s_pool = En(nc.semaphore("s_pool"))
        s_bar = En(nc.semaphore("s_bar"))
        s_once = En(nc.semaphore("s_once"))
        s_out = En(nc.semaphore("s_out"))

        def once(eng, out, in_):
            sem = s_once if eng == "sp" else s_out
            return sem, P.dma(eng, sem, out, in_)

        def barrier():
            for (eng, sem) in sorted(P.dma_regs, key=lambda x: (x[0], x[1].name)):
                P.w(eng, sem, P.cnt[sem.name])
            for eng in ENGS:
                P.cnt[s_bar.name] += 1
                P.th[eng].append(lambda e: e.sem_inc(s_bar, 1))
            tgt = P.cnt[s_bar.name]
            for eng in ENGS:
                P.w(eng, s_bar, tgt)

        ones_bf = En(nc.sbuf_tensor("ones_bf", [128, 1], BF16))
        ones1f = En(nc.sbuf_tensor("ones1f", [1, 128], F32))
        onecol = En(nc.sbuf_tensor("onecol", [128, 1], F32))
        epsP = En(nc.sbuf_tensor("epsP", [1, 1], F32))
        statrow = En(nc.sbuf_tensor("statrow", [1, 6 * QS], F32))
        thr = [En(nc.sbuf_tensor(f"thr{i}", [128, 2], F32)) for i in range(3)]
        pb = [En(nc.psum_tensor(f"pb{i}", [128, 512], F32)) for i in range(8)]

        P.o("dve", lambda e: e.memset(ones_bf[:], 1.0))
        P.o("dve", lambda e: e.memset(ones1f[:], 1.0))
        P.o("dve", lambda e: e.memset(onecol[:], 1.0))
        c_init = P.o("dve", lambda e: e.memset(epsP[:], LN_EPS), inc=s_dve)

        def ln_fixup_and_bcast(W):
            mu = statrow[0:1, 0 * QS:0 * QS + W]
            m2 = statrow[0:1, 1 * QS:1 * QS + W]
            msq = statrow[0:1, 2 * QS:2 * QS + W]
            var = statrow[0:1, 3 * QS:3 * QS + W]
            rstd = statrow[0:1, 4 * QS:4 * QS + W]
            nmr = statrow[0:1, 5 * QS:5 * QS + W]
            P.o("dve", lambda e: e.tensor_scalar_mul(mu, pb[4][0:1, :W], 1.0 / E), inc=s_dve)
            e1 = P.o("dve", lambda e: e.tensor_scalar_mul(m2, pb[5][0:1, :W], 1.0 / E), inc=s_dve)
            P.w("dve", s_dve, e1)
            e2 = P.o("dve", lambda e: e.tensor_mul(msq, mu, mu), inc=s_dve)
            P.w("dve", s_dve, e2)
            dc = P.o("dve", lambda e: e.tensor_sub(var, m2, msq), inc=s_dve)
            P.w("act", s_dve, dc)
            P.w("act", s_dve, c_init)
            ac = P.o("act", lambda e: e.activation(var, var, AF.Sqrt, bias=epsP[:], scale=1.0),
                     inc=s_act)
            P.w("dve", s_act, ac)
            e3 = P.o("dve", lambda e: e.reciprocal(rstd, var), inc=s_dve)
            P.w("dve", s_dve, e3)
            dc = P.o("dve", lambda e: e.scalar_tensor_tensor(nmr, mu, -1.0, rstd, ALU.mult, ALU.mult),
                     inc=s_dve)
            P.w("pe", s_dve, dc)
            P.o("pe", lambda e: e.matmul(pb[6][:, :W], ones1f[0:1, :], rstd, start=True, stop=True))
            pc = P.o("pe", lambda e: e.matmul(pb[7][:, :W], ones1f[0:1, :], nmr, start=True, stop=True),
                     inc=s_pe)
            return pc

        cast_done = {}
        qz_stack = ExitStack()
        ldring = LRing(P, nc, qz_stack, "qzld", 2, [128, 2048], F32)
        stg = SRing(P, nc, qz_stack, "qzst", 2, [128, 2048], BF16)
        acc = [qz_stack.enter_context(nc.sbuf_tensor(f"qzacc{i}", [128, 48], F32)) for i in range(3)]
        accs = [qz_stack.enter_context(nc.sbuf_tensor(f"qzaccs{i}", [128, 1], F32)) for i in range(3)]
        crow = qz_stack.enter_context(nc.sbuf_tensor("qzcrow", [1, 1], F32))

        it = 0
        for nm, wsrc in (("wq", wqT), ("wk", wkT), ("wv", wvT), ("wo", woT)):
            for ecc in range(EC):
                buf, lw_ = ldring.load("sp", it, wsrc[ecc * 128:(ecc + 1) * 128, :])
                P.wt("dve", lw_)
                sb = stg.acq("dve", it)
                dc = P.o("dve", lambda e, o=sb, i=buf: e.tensor_copy(o[:], i[:]), inc=s_dve)
                ldring.rel(it, s_dve, dc)
                P.w("act", s_dve, dc)
                if nm == "wv":
                    stg.store("act", it, wvP[ecc * 128:(ecc + 1) * 128, :])
                else:
                    stg.store("act", it, wB[nm][:, :, ecc, :].rearrange("b p f -> p b f"),
                              view=lambda t: t[:].rearrange("p (b f) -> p b f", f=128))
                it += 1
            cast_done[nm] = [(s, P.cnt[s.name]) for s in stg.sems]

        WCHUNKS = []
        for wi, (src_, dst_, rows, cols) in enumerate(((w1T, q1B, E, HD), (w2T, q2B, E, HD),
                                                       (w3T, q3B, HD, E))):
            for r0 in range(0, rows, 128):
                for c0 in range(0, cols, 2048):
                    WCHUNKS.append((wi, src_, dst_, r0, c0, min(2048, cols - c0)))
        for i in range(3):
            P.o("dve", lambda e, a=acc[i]: e.memset(a[:], 0.0))
        pieces_seen = [0, 0, 0]
        gamma_cnt = [None, None, None]
        for (wi, src_, dst_, r0, c0, wdt) in WCHUNKS:
            buf, lw_ = ldring.load("sp", it, src_[r0:r0 + 128, c0:c0 + wdt],
                                   view=lambda t, w=wdt: t[:, :w])
            P.wt("dve", lw_)
            kcol = pieces_seen[wi]
            pieces_seen[wi] += 1
            dc = P.o("dve", lambda e, b=buf, w=wdt, a=acc[wi], k=kcol: e.reduce_sum(
                a[:, k:k + 1], b[:, :w], axis=AX.X, apply_absolute_value=True), inc=s_dve)
            ldring.rel(it, s_dve, dc)
            gamma_cnt[wi] = dc
            it += 1
        thr_done = [None, None, None]
        for wi in range(3):
            P.w("dve", s_dve, gamma_cnt[wi])
            gs_ = P.o("dve", lambda e, a=acc[wi], s_=accs[wi], n_=pieces_seen[wi]: e.reduce_sum(
                s_[:], a[:, :n_], axis=AX.X), inc=s_dve)
            P.w("pe", s_dve, gs_)
            pc = P.o("pe", lambda e, a=accs[wi]: e.matmul(pb[0][0:1, 0:1], a[:], onecol[:],
                                                          start=True, stop=True), inc=s_pe)
            P.w("dve", s_pe, pc)
            numel = float(E * HD)
            dc = P.o("dve", lambda e, n_=numel: e.tensor_scalar(
                crow[0:1, 0:1], pb[0][0:1, 0:1], 0.5 / n_, 0.5 * QZ_EPS, ALU.mult, ALU.add),
                inc=s_dve)
            P.w("pe", s_dve, dc)
            pc = P.o("pe", lambda e: e.matmul(pb[1][:, 0:1], ones1f[0:1, :], crow[0:1, 0:1],
                                              start=True, stop=True), inc=s_pe)
            P.w("dve", s_pe, pc)
            ec0 = P.o("dve", lambda e, t=thr[wi]: e.tensor_copy(t[:, 0:1], pb[1][:, 0:1]), inc=s_dve)
            P.w("dve", s_dve, ec0)
            thr_done[wi] = P.o("dve", lambda e, t=thr[wi]: e.tensor_scalar_mul(t[:, 1:2], t[:, 0:1], -1.0),
                               inc=s_dve)
        qz_stack.close()
        barrier()
        if "thrtap" in taps:
            for wi in range(3):
                once("act", thrtap[wi], thr[wi][:])
            barrier()

        qz_state = {"i": 0}

        def emit_quant(nmax, quld, qsA, qsB, qst):
            for _ in range(nmax):
                i = qz_state["i"]
                if i >= len(WCHUNKS):
                    return
                (wi, src_, dst_, r0, c0, wdt) = WCHUNKS[i]
                buf, lw_ = quld.load("sp", i, src_[r0:r0 + 128, c0:c0 + wdt],
                                     view=lambda t, w=wdt: t[:, :w])
                P.wt("dve", lw_)
                P.w("dve", s_dve, thr_done[wi])
                if qz_state.get("prev_pc"):
                    P.w("dve", s_dve, qz_state["prev_pc"])
                P.o("dve", lambda e, b=buf, w=wdt, t=thr[wi]: e.tensor_scalar(
                    qsA[:, :w], b[:, :w], t[:, 0:1], -1.0, ALU.is_ge, ALU.mult), inc=s_dve)
                a2 = P.o("dve", lambda e, b=buf, w=wdt, t=thr[wi]: e.tensor_scalar(
                    qsB[:, :w], b[:, :w], t[:, 1:2], -1.0, ALU.is_le, ALU.mult), inc=s_dve)
                sb = qst.acq("dve", i)
                P.w("dve", s_dve, a2)
                pc = P.o("dve", lambda e, o=sb, w=wdt: e.tensor_sub(o[:, :w], qsA[:, :w], qsB[:, :w]),
                         inc=s_dve)
                quld.rel(i, s_dve, pc)
                qz_state["prev_pc"] = pc
                P.w("act", s_dve, pc)
                nb = wdt // 128
                gb0 = c0 // 128
                rcc = r0 // 128
                qst.store("act", i, dst_[gb0:gb0 + nb, :, rcc, :].rearrange("b p f -> p b f"),
                          view=lambda t, w=wdt: t[:, :w].rearrange("p (b f) -> p b f", f=128))
                qz_state["i"] = i + 1

        qst_rings = []
        if upto == "qz":
            qq_stack = ExitStack()
            quld = LRing(P, nc, qq_stack, "quld", 2, [128, 2048], F32)
            qsA = qq_stack.enter_context(nc.sbuf_tensor("qsA", [128, 2048], F32))
            qsB = qq_stack.enter_context(nc.sbuf_tensor("qsB", [128, 2048], F32))
            qst = SRing(P, nc, qq_stack, "qust", 2, [128, 2048], BF16)
            emit_quant(len(WCHUNKS), quld, qsA, qsB, qst)
            qq_stack.close()

        qt_stack = ExitStack()
        if have("q"):
            QT = [qt_stack.enter_context(nc.sbuf_tensor(f"QT{h}", [128, SLOTS * QS], BF16)) for h in range(H)]

        if have("kv"):
            kv_stack = ExitStack()
            xbf = [kv_stack.enter_context(nc.sbuf_tensor(f"xbf{i}", [128, QS], BF16)) for i in range(EC)]
            kv_ld = LRing(P, nc, kv_stack, "kvld", 2, [128, QS], F32)
            kv_sqt = [kv_stack.enter_context(nc.sbuf_tensor(f"kvsq{i}", [128, QS], BF16)) for i in range(2)]
            kv_scr = kv_stack.enter_context(nc.sbuf_tensor("kvscr", [128, QS], F32))
            wvsp = [kv_stack.enter_context(nc.sbuf_tensor(f"kvwv{i}", [128, E], BF16)) for i in range(EC)]
            wk_r = LRing(P, nc, kv_stack, "kvwk", 2, [128, E], BF16)
            kst = SRing(P, nc, kv_stack, "kvks", 2, [128, QS], BF16)
            vst = SRing(P, nc, kv_stack, "kvvs", 2, [128, E], BF16)
            ctab = LRing(P, nc, kv_stack, "kvct", 2, [128, QS], F32)
            stab = LRing(P, nc, kv_stack, "kvst", 2, [128, QS], F32)
            kv_t1 = kv_stack.enter_context(nc.sbuf_tensor("kvt1", [128, QS], F32))
            kv_t2 = kv_stack.enter_context(nc.sbuf_tensor("kvt2", [128, QS], F32))
            kv_sqm = [None, None]
            kbmark = [None, None]

            def ln_block(tiles, sq_tiles, sqm, load_src, ldr, scr, it0):
                it_ = it0
                stat_stop = None
                for ecc in range(EC):
                    buf, lw_ = ldr.load("sp", it_, load_src(ecc))
                    P.wt("dve", lw_)
                    dc = P.o("dve", lambda e, o=tiles[ecc], i=buf: e.tensor_copy(o[:], i[:]), inc=s_dve)
                    ldr.rel(it_, s_dve, dc)
                    k2 = it_ % 2
                    if sqm[k2] is not None:
                        P.wt("act", sqm[k2])
                    sqb = sq_tiles[k2]
                    P.w("act", s_dve, dc)
                    ac = P.o("act", lambda e, o=sqb, i=tiles[ecc]: e.activation(o[:], i[:], AF.Square),
                             inc=s_act)
                    P.w("pe", s_act, ac)
                    P.o("pe", lambda e, i=tiles[ecc], st=(ecc == 0), sp_=(ecc == EC - 1): e.matmul(
                        pb[4][0:1, :], ones_bf[:], i[:], start=st, stop=sp_))
                    pc = P.o("pe", lambda e, i=sqb, st=(ecc == 0), sp_=(ecc == EC - 1): e.matmul(
                        pb[5][0:1, :], ones_bf[:], i[:], start=st, stop=sp_), inc=s_pe)
                    sqm[k2] = (s_pe, pc)
                    stat_stop = pc
                    it_ += 1
                P.w("dve", s_pe, stat_stop)
                bc = ln_fixup_and_bcast(QS)
                P.w("dve", s_pe, bc)
                ndone = None
                for ecc in range(EC):
                    em = P.o("dve", lambda e, x=tiles[ecc]: e.tensor_mul(scr[:], x[:], pb[6][:, :]),
                             inc=s_dve)
                    P.w("dve", s_dve, em)
                    ndone = P.o("dve", lambda e, x=tiles[ecc]: e.tensor_add(x[:], scr[:], pb[7][:, :]),
                                inc=s_dve)
                return ndone, it_

            for tup in cast_done["wv"]:
                P.wt("sp", tup)
            wv_lc = None
            for ecc in range(EC):
                wv_lc = once("sp", wvsp[ecc][:], wvP[ecc * 128:(ecc + 1) * 128, :])

            it = 0
            for qs_i in range(NQS):
                gt0 = qs_i * QS
                ndone, it = ln_block(xbf, kv_sqt, kv_sqm,
                                     lambda ecc, g=gt0: xt[ecc * 128:(ecc + 1) * 128, g:g + QS],
                                     kv_ld, kv_scr, it)
                P.w("pe", s_dve, ndone)
                P.wt("pe", wv_lc)
                for tt in range(4):
                    vmm = None
                    for ecc in range(EC):
                        for f4 in range(4):
                            vmm = P.o("pe", lambda e, b=pb[2 + f4], w=wvsp[ecc], c=ecc, t_=tt, f_=f4,
                                      st=(ecc == 0), sp_=(ecc == EC - 1): e.matmul(
                                b[:, :], xbf[c][:, t_ * 128:(t_ + 1) * 128], w[:, f_ * 512:(f_ + 1) * 512],
                                start=st, stop=sp_),
                                inc=(s_pe if (ecc == EC - 1 and f4 == 3) else None))
                    vs = vst.acq("act", qs_i * 4 + tt)
                    P.w("act", s_pe, vmm)
                    vdone = None
                    for f4 in range(4):
                        vdone = P.o("act", lambda e, o=vs, b=pb[2 + f4], f_=f4: e.copy(
                            o[:, f_ * 512:(f_ + 1) * 512], b[:, :]), inc=s_act)
                    P.w("act", s_act, vdone)
                    vst.store("act", qs_i * 4 + tt, Vb[gt0 + tt * 128:gt0 + (tt + 1) * 128, :])
                    P.w("pe", s_act, vdone)
                cb, cl = ctab.load("sp", qs_i, cosKF[:, gt0:gt0 + QS])
                sb_, sl = stab.load("sp", qs_i, sinKF[:, gt0:gt0 + QS])
                for tup in cast_done["wk"]:
                    P.wt("sp", tup)
                kdone = None
                for fb in range(FBn):
                    wkb, lk = wk_r.load("sp", qs_i * FBn + fb,
                                        wB["wk"][fb],
                                        view=lambda t: t[:].rearrange("p (c f) -> p c f", f=128))
                    bk = pb[0 + (fb % 2)]
                    P.wt("pe", kbmark[fb % 2])
                    P.wt("pe", lk)
                    kmm = None
                    for ecc in range(EC):
                        kmm = P.o("pe", lambda e, b=bk, w=wkb, c=ecc, st=(ecc == 0), sp_=(ecc == EC - 1): e.matmul(
                            b[:, :], w[:, c * 128:(c + 1) * 128], xbf[c][:, :], start=st, stop=sp_),
                            inc=(s_pe if ecc == EC - 1 else None))
                    wk_r.rel(qs_i * FBn + fb, s_pe, kmm)
                    ks = kst.acq("dve", qs_i * FBn + fb)
                    P.w("dve", s_pe, kmm)
                    P.wt("dve", cl)
                    P.wt("dve", sl)
                    P.o("dve", lambda e, b=bk, c=cb: e.tensor_mul(kv_t1[:], b[:, :], c[:]), inc=s_dve)
                    P.o("dve", lambda e, b=bk, s=sb_: e.tensor_mul(kv_t2[0:64, :], b[64:128, :], s[64:128, :]),
                        inc=s_dve)
                    em = P.o("dve", lambda e, b=bk, s=sb_: e.tensor_mul(kv_t2[64:128, :], b[0:64, :], s[0:64, :]),
                             inc=s_dve)
                    P.w("dve", s_dve, em)
                    kdone = P.o("dve", lambda e, o=ks: e.tensor_add(o[:], kv_t1[:], kv_t2[:]), inc=s_dve)
                    kbmark[fb % 2] = (s_dve, kdone)
                    P.w("act", s_dve, kdone)
                    kst.store("act", qs_i * FBn + fb, KbT[fb * 128:(fb + 1) * 128, gt0:gt0 + QS])
                ctab.rel(qs_i, s_dve, kdone)
                stab.rel(qs_i, s_dve, kdone)
            kv_stack.close()
            barrier()

        if have("q"):
            q_stack = ExitStack()
            xqb = [q_stack.enter_context(nc.sbuf_tensor(f"xqb{i}", [128, QS], BF16)) for i in range(EC)]
            q_ld = LRing(P, nc, q_stack, "qld", 2, [128, QS], F32)
            q_sqt = [q_stack.enter_context(nc.sbuf_tensor(f"qsq{i}", [128, QS], BF16)) for i in range(2)]
            q_scr = q_stack.enter_context(nc.sbuf_tensor("qscr", [128, QS], F32))
            wq_r = LRing(P, nc, q_stack, "qwq", 2, [128, E], BF16)
            qtabC = q_stack.enter_context(nc.sbuf_tensor("qtabC", [128, SLOTS * QS], F32))
            qtabS = q_stack.enter_context(nc.sbuf_tensor("qtabS", [128, SLOTS * QS], F32))
            q_t1 = q_stack.enter_context(nc.sbuf_tensor("qt1", [128, QS], F32))
            q_t2 = q_stack.enter_context(nc.sbuf_tensor("qt2", [128, QS], F32))
            q_sqm = [None, None]
            qbmark = [None, None]

            def ln_block_q(tiles, sq_tiles, sqm, load_src, ldr, scr, it0):
                it_ = it0
                stat_stop = None
                for ecc in range(EC):
                    buf, lw_ = ldr.load("sp", it_, load_src(ecc))
                    P.wt("dve", lw_)
                    dc = P.o("dve", lambda e, o=tiles[ecc], i=buf: e.tensor_copy(o[:], i[:]), inc=s_dve)
                    ldr.rel(it_, s_dve, dc)
                    k2 = it_ % 2
                    if sqm[k2] is not None:
                        P.wt("act", sqm[k2])
                    sqb = sq_tiles[k2]
                    P.w("act", s_dve, dc)
                    ac = P.o("act", lambda e, o=sqb, i=tiles[ecc]: e.activation(o[:], i[:], AF.Square),
                             inc=s_act)
                    P.w("pe", s_act, ac)
                    P.o("pe", lambda e, i=tiles[ecc], st=(ecc == 0), sp_=(ecc == EC - 1): e.matmul(
                        pb[4][0:1, :], ones_bf[:], i[:], start=st, stop=sp_))
                    pc = P.o("pe", lambda e, i=sqb, st=(ecc == 0), sp_=(ecc == EC - 1): e.matmul(
                        pb[5][0:1, :], ones_bf[:], i[:], start=st, stop=sp_), inc=s_pe)
                    sqm[k2] = (s_pe, pc)
                    stat_stop = pc
                    it_ += 1
                P.w("dve", s_pe, stat_stop)
                bc = ln_fixup_and_bcast(QS)
                P.w("dve", s_pe, bc)
                ndone = None
                for ecc in range(EC):
                    em = P.o("dve", lambda e, x=tiles[ecc]: e.tensor_mul(scr[:], x[:], pb[6][:, :]),
                             inc=s_dve)
                    P.w("dve", s_dve, em)
                    ndone = P.o("dve", lambda e, x=tiles[ecc]: e.tensor_add(x[:], scr[:], pb[7][:, :]),
                                inc=s_dve)
                return ndone, it_

            tc0 = once("sp", qtabC[:], cosQFd)
            tc1 = once("sp", qtabS[:], sinQFd)

            it = 0
            for s in range(SLOTS):
                q0 = s * QS
                ndone, it = ln_block_q(xqb, q_sqt, q_sqm,
                                       lambda ecc, q=q0: xq[ecc * 128:(ecc + 1) * 128, q:q + QS],
                                       q_ld, q_scr, it)
                P.w("pe", s_dve, ndone)
                for tup in cast_done["wq"]:
                    P.wt("sp", tup)
                P.wt("dve", tc0)
                P.wt("dve", tc1)
                for fb in range(FBn):
                    wqb_t, lq = wq_r.load("sp", s * FBn + fb,
                                          wB["wq"][fb],
                                          view=lambda t: t[:].rearrange("p (c f) -> p c f", f=128))
                    bq = pb[0 + (fb % 2)]
                    P.wt("pe", qbmark[fb % 2])
                    P.wt("pe", lq)
                    qmm = None
                    for ecc in range(EC):
                        qmm = P.o("pe", lambda e, b=bq, w=wqb_t, c=ecc, st=(ecc == 0), sp_=(ecc == EC - 1): e.matmul(
                            b[:, :], w[:, c * 128:(c + 1) * 128], xqb[c][:, :], start=st, stop=sp_),
                            inc=(s_pe if ecc == EC - 1 else None))
                    wq_r.rel(s * FBn + fb, s_pe, qmm)
                    P.w("dve", s_pe, qmm)
                    P.o("dve", lambda e, b=bq, q_=q0: e.tensor_mul(q_t1[:], b[:, :], qtabC[:, q_:q_ + QS]),
                        inc=s_dve)
                    P.o("dve", lambda e, b=bq, q_=q0: e.tensor_mul(q_t2[0:64, :], b[64:128, :], qtabS[64:128, q_:q_ + QS]),
                        inc=s_dve)
                    em = P.o("dve", lambda e, b=bq, q_=q0: e.tensor_mul(q_t2[64:128, :], b[0:64, :], qtabS[0:64, q_:q_ + QS]),
                             inc=s_dve)
                    P.w("dve", s_dve, em)
                    qdone = P.o("dve", lambda e, h=fb, q_=q0: e.tensor_add(QT[h][:, q_:q_ + QS], q_t1[:], q_t2[:]),
                                inc=s_dve)
                    qbmark[fb % 2] = (s_dve, qdone)
            q_stack.close()
            barrier()
            if "qtap" in taps:
                for h in range(H):
                    once("act", qtap[h * 128:(h + 1) * 128, :], QT[h][:])
                barrier()

        if have("att"):
            att_stack = ExitStack()
            kt_r = LRing(P, nc, att_stack, "akt", 2, [128, 1024], BF16)
            vt_r = LRing(P, nc, att_stack, "avt", 2, [128, 8 * 128], BF16)
            ptile = [att_stack.enter_context(nc.sbuf_tensor(f"ap{i}", [128, QS], BF16)) for i in range(8)]
            pmark = [None] * 8
            masks = [att_stack.enter_context(nc.sbuf_tensor(f"amask{i}", [128, 2432], BF16)) for i in range(SLOTS)]
            rrow = att_stack.enter_context(nc.sbuf_tensor("arr", [1, 2 * QS], F32))
            rbt = att_stack.enter_context(nc.sbuf_tensor("arb", [128, QS], F32))
            ost = SRing(P, nc, att_stack, "aost", 2, [128, QS], BF16)
            quld = LRing(P, nc, att_stack, "quld", 2, [128, 2048], F32)
            qsA = att_stack.enter_context(nc.sbuf_tensor("qsA", [128, 2048], F32))
            qsB = att_stack.enter_context(nc.sbuf_tensor("qsB", [128, 2048], F32))
            qst = SRing(P, nc, att_stack, "qust", 2, [128, 2048], BF16)
            qst_rings = [qst]

            mload = None
            for s in range(SLOTS):
                mload = once("sp", masks[s][:], maskm[s])

            RB = {0: pb[2], 1: pb[6]}
            SB = [pb[0], pb[1], pb[7]]
            sbw = [None, None, None]
            pb3mark = [None]
            pvfree = [None, None]
            rfree = [None, None]
            rrmark = [None, None]
            oi = [0]

            def finish(h, slot, ridx, pvstop):
                rbank = RB[ridx]
                vbank = pb[4 + ridx]
                P.w("dve", s_pe, pvstop)
                if rrmark[ridx] is not None:
                    P.w("dve", s_pe, rrmark[ridx])
                rr = rrow[0:1, ridx * QS:(ridx + 1) * QS]
                dc = P.o("dve", lambda e, b=rbank, r=rr: e.reciprocal(r, b[0:1, :]), inc=s_dve)
                rfree[ridx] = dc
                P.w("pe", s_dve, dc)
                if pb3mark[0] is not None:
                    P.wt("pe", pb3mark[0])
                bc = P.o("pe", lambda e, r=rr: e.matmul(pb[3][:, :], ones1f[0:1, :], r,
                                                        start=True, stop=True), inc=s_pe)
                rrmark[ridx] = bc
                P.w("dve", s_pe, bc)
                em = P.o("dve", lambda e: e.tensor_copy(rbt[:], pb[3][:, :]), inc=s_dve)
                P.w("dve", s_dve, em)
                pb3mark[0] = (s_dve, em)
                osb = ost.acq("dve", oi[0])
                dc2 = P.o("dve", lambda e, o=osb, b=vbank: e.tensor_mul(o[:], b[:, :], rbt[:]), inc=s_dve)
                pvfree[ridx] = dc2
                P.w("act", s_dve, dc2)
                ost.store("act", oi[0], outTd[h * 128:(h + 1) * 128, slot * QS:(slot + 1) * QS])
                oi[0] += 1

            P.wt("dve", mload)
            GRP = 3
            for h in range(H):
                for (sA, sB) in PASSES:
                    emax = max(EXT[sA], EXT[sB])
                    nkblk = (emax + 7) // 8
                    sched = []
                    for kb in range(nkblk):
                        for i8 in range(8):
                            kc = kb * 8 + i8
                            for slot, ridx in ((sA, 0), (sB, 1)):
                                if kc < EXT[slot]:
                                    sched.append((kb, i8, kc, slot, ridx))
                    groups = [sched[i:i + GRP] for i in range(0, len(sched), GRP)]
                    ktinfo = {}
                    pend = [None]
                    pci = [0]
                    for g in groups:
                        for kb in sorted({c[0] for c in g}):
                            if kb not in ktinfo:
                                kt_t, lk = kt_r.load("sp", kb, KbT[h * 128:(h + 1) * 128,
                                                                  kb * 1024:(kb + 1) * 1024])
                                vt_t, lv = vt_r.load("sp", kb,
                                                     Vb[kb * 1024:(kb + 1) * 1024, h * 128:(h + 1) * 128]
                                                     .rearrange("(kc p) d -> p kc d", p=128),
                                                     view=lambda t: t[:].rearrange("p (kc d) -> p kc d", d=128))
                                ktinfo[kb] = [kt_t, lk, vt_t, lv, None, None]
                        smms = []
                        for gi, (kb, i8, kc, slot, ridx) in enumerate(g):
                            if sbw[gi] is not None:
                                P.wt("pe", sbw[gi])
                            P.wt("pe", ktinfo[kb][1])
                            smm = P.o("pe", lambda e, b=SB[gi], k=ktinfo[kb][0], i=i8, s_=slot, h_=h: e.matmul(
                                b[:, :], k[:, i * 128:(i + 1) * 128], QT[h_][:, s_ * QS:(s_ + 1) * QS],
                                start=True, stop=True), inc=s_pe)
                            ktinfo[kb][4] = smm
                            smms.append(smm)
                        if pend[0] is not None:
                            pend[0]()
                            pend[0] = None
                        ptiles = []
                        gates = []
                        for gi, (kb, i8, kc, slot, ridx) in enumerate(g):
                            pslot_i = (pci[0] + gi) % 8
                            pslot = ptile[pslot_i]
                            if pmark[pslot_i] is not None:
                                P.wt("act", pmark[pslot_i])
                            P.w("act", s_pe, smms[gi])
                            ec_ = P.o("act", lambda e, o=pslot, b=SB[gi]: e.activation(
                                o[:], b[:, :], AF.Exp, bias=0.0, scale=ISQD), inc=s_act)
                            sbw[gi] = (s_act, ec_)
                            gate = (s_act, ec_)
                            if 16 * slot <= kc < 16 * slot + 16:
                                j0 = 128 * (16 * slot + 15 - kc)
                                P.w("dve", s_act, ec_)
                                mc = P.o("dve", lambda e, p_=pslot, m=masks[slot], j=j0: e.tensor_mul(
                                    p_[:], p_[:], m[:, j:j + QS]), inc=s_dve)
                                gate = (s_dve, mc)
                            ptiles.append((pslot, pslot_i))
                            gates.append(gate)

                        def mk_pend(g=list(g), ptiles=list(ptiles), gates=list(gates),
                                    h=h, ktinfo=ktinfo):
                            def emit():
                                for gi, (kb, i8, kc, slot, ridx) in enumerate(g):
                                    P.wt("pe", gates[gi])
                                    P.wt("pe", ktinfo[kb][3])
                                    if kc == 0:
                                        if rfree[ridx] is not None:
                                            P.w("pe", s_dve, rfree[ridx])
                                        if pvfree[ridx] is not None:
                                            P.w("pe", s_dve, pvfree[ridx])
                                    pslot = ptiles[gi][0]
                                    P.o("pe", lambda e, b=RB[ridx], p_=pslot, st=(kc == 0),
                                        sp_=(kc == EXT[slot] - 1): e.matmul(
                                        b[0:1, :], ones_bf[:], p_[:], start=st, stop=sp_))
                                    pc = P.o("pe", lambda e, b=pb[4 + ridx], v=ktinfo[kb][2], i=i8,
                                             p_=pslot, st=(kc == 0), sp_=(kc == EXT[slot] - 1): e.matmul(
                                        b[:, :], v[:, i * 128:(i + 1) * 128], p_[:], start=st, stop=sp_),
                                        inc=s_pe)
                                    pmark[ptiles[gi][1]] = (s_pe, pc)
                                    ktinfo[kb][5] = pc
                                    if kc == EXT[slot] - 1:
                                        finish(h, slot, ridx, pc)
                            return emit

                        pend[0] = mk_pend()
                        pci[0] += len(g)
                        for kb_, info_ in ktinfo.items():
                            if info_[4] is not None:
                                kt_r.rel(kb_, s_pe, info_[4])
                            if info_[5] is not None:
                                vt_r.rel(kb_, s_pe, info_[5])
                    if pend[0] is not None:
                        pend[0]()
                        pend[0] = None
                    for kb, info in ktinfo.items():
                        if info[4] is not None:
                            kt_r.rel(kb, s_pe, info[4])
                        if info[5] is not None:
                            vt_r.rel(kb, s_pe, info[5])
                    emit_quant(3, quld, qsA, qsB, qst)
            emit_quant(len(WCHUNKS), quld, qsA, qsB, qst)
            qst_done = [(s, P.cnt[s.name]) for s in qst.sems]
            att_stack.close()
            qt_stack.close()
            barrier()
        else:
            qt_stack.close()
            qst_done = []

        if have("w"):
            w_stack = ExitStack()
            s_ot = w_stack.enter_context(nc.semaphore("s_ot"))
            otile = [w_stack.enter_context(nc.sbuf_tensor(f"wot{i}", [128, QS], BF16)) for i in range(EC)]
            x1T = [w_stack.enter_context(nc.sbuf_tensor(f"wx1{i}", [128, QS], F32)) for i in range(FBn)]
            ln2 = [w_stack.enter_context(nc.sbuf_tensor(f"wln{i}", [128, QS], BF16)) for i in range(FBn)]
            hid = [w_stack.enter_context(nc.sbuf_tensor(f"whid{i}", [128, QS], BF16)) for i in range(HBn)]
            wcol = LRing(P, nc, w_stack, "wwc", 2, [128, E], BF16)
            q12c = LRing(P, nc, w_stack, "wq12", 4, [128, E], BF16)
            q3c = LRing(P, nc, w_stack, "wq3", 2, [128, HBn * 128], BF16)
            xres = LRing(P, nc, w_stack, "wxr", 2, [128, QS], F32)
            sig2 = [w_stack.enter_context(nc.sbuf_tensor(f"wsig{i}", [128, QS], BF16)) for i in range(2)]
            w_sqt = [w_stack.enter_context(nc.sbuf_tensor(f"wsq{i}", [128, QS], BF16)) for i in range(2)]
            w_sqm = [None, None]
            wbmark = [None, None]
            b1mark = [None, None]
            b2mark = [None, None]
            b3mark = [None, None]
            sigmark = [None, None]
            yst = SRing(P, nc, w_stack, "wyst", 2, [128, QS], F32)
            w_scr = w_stack.enter_context(nc.sbuf_tensor("wscr", [128, QS], F32))

            for tup in qst_done:
                P.wt("sp", tup)
            wo_last = [None]
            for s in range(SLOTS):
                q0 = s * QS
                if wo_last[0] is not None:
                    P.w("sp", s_pe, wo_last[0])
                for ecc in range(EC):
                    P.dma("sp", s_ot, otile[ecc][:], outTd[ecc * 128:(ecc + 1) * 128, q0:q0 + QS])
                olc = (s_ot, P.cnt[s_ot.name])
                for tup in cast_done["wo"]:
                    P.wt("sp", tup)
                for fb in range(FBn):
                    wc, lw = wcol.load("sp", s * FBn + fb,
                                       wB["wo"][fb],
                                       view=lambda t: t[:].rearrange("p (c f) -> p c f", f=128))
                    xr, lx = xres.load("sp", s * FBn + fb, xq[fb * 128:(fb + 1) * 128, q0:q0 + QS])
                    bw = pb[fb % 2]
                    P.wt("pe", wbmark[fb % 2])
                    P.wt("pe", lw)
                    P.wt("pe", olc)
                    wmm = None
                    for ecc in range(EC):
                        wmm = P.o("pe", lambda e, b=bw, w=wc, c=ecc, st=(ecc == 0), sp_=(ecc == EC - 1): e.matmul(
                            b[:, :], w[:, c * 128:(c + 1) * 128], otile[c][:], start=st, stop=sp_),
                            inc=(s_pe if ecc == EC - 1 else None))
                    wo_last[0] = wmm
                    wcol.rel(s * FBn + fb, s_pe, wmm)
                    P.w("dve", s_pe, wmm)
                    P.wt("dve", lx)
                    dc = P.o("dve", lambda e, o=x1T[fb], b=bw, x=xr: e.tensor_add(o[:], b[:, :], x[:]),
                             inc=s_dve)
                    xres.rel(s * FBn + fb, s_dve, dc)
                    wbmark[fb % 2] = (s_dve, dc)
                stat_stop = None
                P.w("act", s_dve, P.cnt[s_dve.name])
                for fb in range(FBn):
                    k2 = fb % 2
                    if w_sqm[k2] is not None:
                        P.wt("act", w_sqm[k2])
                    sqb = w_sqt[k2]
                    ac = P.o("act", lambda e, o=sqb, i=x1T[fb]: e.activation(o[:], i[:], AF.Square),
                             inc=s_act)
                    P.w("pe", s_act, ac)
                    P.o("pe", lambda e, i=x1T[fb], st=(fb == 0), sp_=(fb == FBn - 1): e.matmul(
                        pb[4][0:1, :], onecol[:], i[:], start=st, stop=sp_))
                    pc = P.o("pe", lambda e, i=sqb, st=(fb == 0), sp_=(fb == FBn - 1): e.matmul(
                        pb[5][0:1, :], ones_bf[:], i[:], start=st, stop=sp_), inc=s_pe)
                    w_sqm[k2] = (s_pe, pc)
                    stat_stop = pc
                P.w("dve", s_pe, stat_stop)
                bc = ln_fixup_and_bcast(QS)
                P.w("dve", s_pe, bc)
                ndone = None
                for fb in range(FBn):
                    em = P.o("dve", lambda e, x=x1T[fb]: e.tensor_mul(w_scr[:], x[:], pb[6][:, :]),
                             inc=s_dve)
                    P.w("dve", s_dve, em)
                    ndone = P.o("dve", lambda e, o=ln2[fb]: e.tensor_add(o[:], w_scr[:], pb[7][:, :]),
                                inc=s_dve)
                P.w("pe", s_dve, ndone)
                for hb in range(HBn):
                    c1, l1 = q12c.load("sp", 2 * (s * HBn + hb),
                                       q1B[hb],
                                       view=lambda t: t[:].rearrange("p (c f) -> p c f", f=128))
                    c2, l2 = q12c.load("sp", 2 * (s * HBn + hb) + 1,
                                       q2B[hb],
                                       view=lambda t: t[:].rearrange("p (c f) -> p c f", f=128))
                    b1 = pb[hb % 2]
                    b2 = pb[2 + hb % 2]
                    P.wt("pe", b1mark[hb % 2])
                    P.wt("pe", l1)
                    g1 = None
                    for ecc in range(EC):
                        g1 = P.o("pe", lambda e, b=b1, w=c1, c=ecc, st=(ecc == 0), sp_=(ecc == EC - 1): e.matmul(
                            b[:, :], w[:, c * 128:(c + 1) * 128], ln2[c][:], start=st, stop=sp_),
                            inc=(s_pe if ecc == EC - 1 else None))
                    P.wt("pe", b2mark[hb % 2])
                    P.wt("pe", l2)
                    g2 = None
                    for ecc in range(EC):
                        g2 = P.o("pe", lambda e, b=b2, w=c2, c=ecc, st=(ecc == 0), sp_=(ecc == EC - 1): e.matmul(
                            b[:, :], w[:, c * 128:(c + 1) * 128], ln2[c][:], start=st, stop=sp_),
                            inc=(s_pe if ecc == EC - 1 else None))
                    q12c.rel(2 * (s * HBn + hb), s_pe, g1)
                    q12c.rel(2 * (s * HBn + hb) + 1, s_pe, g2)
                    P.w("act", s_pe, g1)
                    sgt = sig2[hb % 2]
                    P.wt("act", sigmark[hb % 2])
                    ac = P.o("act", lambda e, b=b1, o=sgt: e.activation(o[:], b[:, :], AF.Sigmoid,
                                                                        bias=0.0, scale=-1.0), inc=s_act)
                    b1mark[hb % 2] = (s_act, ac)
                    P.w("dve", s_pe, g2)
                    P.w("dve", s_act, ac)
                    dc = P.o("dve", lambda e, o=hid[hb], b=b2, sg=sgt: e.scalar_tensor_tensor(
                        o[:], b[:, :], -1.0, sg[:], ALU.mult, ALU.mult), inc=s_dve)
                    b2mark[hb % 2] = (s_dve, dc)
                    sigmark[hb % 2] = (s_dve, dc)
                for fb in range(FBn):
                    c3, l3 = q3c.load("sp", s * FBn + fb,
                                      q3B[fb],
                                      view=lambda t: t[:].rearrange("p (c f) -> p c f", f=128))
                    b3 = pb[4 + fb % 2]
                    P.wt("pe", b3mark[fb % 2])
                    P.wt("pe", l3)
                    g3 = None
                    for hb in range(HBn):
                        g3 = P.o("pe", lambda e, b=b3, w=c3, c=hb, st=(hb == 0), sp_=(hb == HBn - 1): e.matmul(
                            b[:, :], w[:, c * 128:(c + 1) * 128], hid[c][:], start=st, stop=sp_),
                            inc=(s_pe if hb == HBn - 1 else None))
                    q3c.rel(s * FBn + fb, s_pe, g3)
                    ys = yst.acq("dve", s * FBn + fb)
                    P.w("dve", s_pe, g3)
                    dc = P.o("dve", lambda e, o=ys, b=b3, x=x1T[fb]: e.scalar_tensor_tensor(
                        o[:], b[:, :], -1.0, x[:], ALU.mult, ALU.add), inc=s_dve)
                    b3mark[fb % 2] = (s_dve, dc)
                    P.w("act", s_dve, dc)
                    yst.store("act", s * FBn + fb, yt[fb * 128:(fb + 1) * 128, q0:q0 + QS])
            w_stack.close()

        for (eng, sem) in sorted(P.dma_regs, key=lambda x: (x[0], x[1].name)):
            P.w("sp", sem, P.cnt[sem.name])

        with nc.Block() as block:
            @block.tensor
            def _(t):
                for f in P.th["pe"]:
                    f(t)

            @block.scalar
            def _(a):
                for f in P.th["act"]:
                    f(a)

            @block.vector
            def _(v):
                for f in P.th["dve"]:
                    f(v)

            @block.gpsimd
            def _(g):
                for f in P.th["pool"]:
                    f(g)

            @block.sync
            def _(sy):
                for f in P.th["sp"]:
                    f(sy)

    return nc



def _rope_tables():
    inv = (1.0 / (np.float32(1000000.0) ** (np.arange(0, D, 2, dtype=np.float32) / np.float32(D)))).astype(np.float32)
    t = np.arange(T, dtype=np.float32)
    f = t[:, None] * inv[None, :]
    return np.cos(f).astype(np.float32), np.sin(f).astype(np.float32)


_NC_CACHE = {}


def make_in_maps(x, wq, wk, wv, wo, w1, w2, w3):
    cosf, sinf = _rope_tables()
    cosKF = np.ascontiguousarray(np.concatenate([cosf.T, cosf.T], axis=0))
    sinKF = np.ascontiguousarray(np.concatenate([sinf.T, -sinf.T], axis=0))
    xT = [np.ascontiguousarray(x[b].T) for b in range(Bn)]
    wqT = np.ascontiguousarray(wq.T); wkT = np.ascontiguousarray(wk.T)
    wvT = np.ascontiguousarray(wv.T); woT = np.ascontiguousarray(wo.T)
    w1T = np.ascontiguousarray(w1.T); w2T = np.ascontiguousarray(w2.T)
    w3T = np.ascontiguousarray(w3.T)
    jcols = np.arange(2432)
    kps = np.arange(128)
    in_maps = []
    for c in range(8):
        b, j = c // 4, c % 4
        gs = QTILES[j]
        cols = np.concatenate([np.arange(g * QS, (g + 1) * QS) for g in gs])
        xqT = np.ascontiguousarray(xT[b][:, cols])
        cosQ = np.ascontiguousarray(cosKF[:, cols])
        sinQ = np.ascontiguousarray(np.concatenate([sinf.T[:, cols], -sinf.T[:, cols]], axis=0))
        mm = np.zeros((SLOTS, 128, 2432), dtype=np.float32)
        for s in range(SLOTS):
            C = 128 * (16 * s + 15) - QS * gs[s]
            mm[s] = (jcols[None, :] - kps[:, None] >= C).astype(np.float32)
        in_maps.append({
            "xt": xT[b], "xq": xqT,
            "wqT": wqT, "wkT": wkT, "wvT": wvT, "woT": woT,
            "w1T": w1T, "w2T": w2T, "w3T": w3T,
            "cosKF": cosKF, "sinKF": sinKF,
            "cosQF": cosQ, "sinQF": sinQ,
            "maskm": mm.astype(ml_dtypes.bfloat16),
        })
    return in_maps


def kernel(L0_attn_x, L0_attn_wq, L0_attn_wk, L0_attn_wv, L0_attn_wo,
           L0_ffn_w1, L0_ffn_w2, L0_ffn_w3):
    x = np.asarray(L0_attn_x)
    in_maps = make_in_maps(x, np.asarray(L0_attn_wq), np.asarray(L0_attn_wk),
                           np.asarray(L0_attn_wv), np.asarray(L0_attn_wo),
                           np.asarray(L0_ffn_w1), np.asarray(L0_ffn_w2), np.asarray(L0_ffn_w3))
    if "prog" not in _NC_CACHE:
        _NC_CACHE["prog"] = build_program()
    nc = _NC_CACHE["prog"]
    res = run_bass_kernel_spmd(nc, in_maps, core_ids=list(range(8)))
    out = np.empty((Bn, T, E), dtype=np.float32)
    for c in range(8):
        b, j = c // 4, c % 4
        ytc = res.results[c]["yt"]
        for s in range(SLOTS):
            g = QTILES[j][s]
            out[b, g * QS:(g + 1) * QS, :] = ytc[:, s * QS:(s + 1) * QS].T
    return out
